# version 04
# baseline (speedup 1.0000x reference)
import contextlib
import numpy as np
import ml_dtypes
import concourse.bass as bass
import concourse.mybir as mybir
from concourse.bass_utils import run_bass_kernel_spmd

F32 = mybir.dt.float32
BF16 = mybir.dt.bfloat16
I32 = mybir.dt.int32
U8 = mybir.dt.uint8
AF = mybir.ActivationFunctionType
ALU = mybir.AluOpType
AX = mybir.AxisListType

B, L, V = 4, 2048, 4096
TOK = 1024
NCORES = 8
DEPTH_A, DIM_A, DFF_A = 4, 512, 2048
EPS_A = 1e-5
EPS_OUT = 1e-5


class Sem:
    def __init__(self, ctx, nc, name):
        self.h = ctx.enter_context(nc.semaphore(name))
        self.n = 0

    def inc(self, k=1):
        self.n += k
        return self.n


def _f32(x):
    return np.ascontiguousarray(np.asarray(x), dtype=np.float32)


def colpack(vec, nchunks):
    v = _f32(vec).reshape(nchunks, 128)
    return np.ascontiguousarray(v.T)


def arm_iter(nc, engines):
    for arm in nc.switch_core_id(engines=engines, n=128):
        try:
            lg = arm.logical
            if lg >= NCORES:
                continue
        except Exception:
            continue
        yield lg, lg % 2



def build_stageA(env=None):
    if env is None:
        ctx = contextlib.ExitStack()
        nc = bass.Bass("TRN2", target_bir_lowering=False)
        P_ = ""
    else:
        ctx = env["ctx"]
        nc = env["nc"]
        P_ = "A_"
    dt = lambda name, *a, **k: nc.dram_tensor(P_ + name, *a, **k)

    if env is None:
        xin = dt("xin", [V, L], F32, kind="ExternalInput").ap()
    else:
        xin = env["xin_full"]
    WinT = dt("WinT", [V, DIM_A], F32, kind="ExternalInput").ap()
    bin_c = dt("bin_c", [128, 4], F32, kind="ExternalInput").ap()
    ln1_rows = dt("ln1_rows", [DEPTH_A, 2, DIM_A], F32, kind="ExternalInput").ap()
    ln2_rows = dt("ln2_rows", [DEPTH_A, 2, DIM_A], F32, kind="ExternalInput").ap()
    lnl_rows = dt("lnl_rows", [2, DIM_A], F32, kind="ExternalInput").ap()
    phzre_c = dt("phzre_c", [DEPTH_A, 128, 4], F32, kind="ExternalInput").ap()
    phzim_c = dt("phzim_c", [DEPTH_A, 128, 4], F32, kind="ExternalInput").ap()
    lcre_c = dt("lcre_c", [DEPTH_A, 128, 4], F32, kind="ExternalInput").ap()
    lcim_c = dt("lcim_c", [DEPTH_A, 128, 4], F32, kind="ExternalInput").ap()
    scl_c = dt("scl_c", [DEPTH_A, 128, 4], F32, kind="ExternalInput").ap()
    W1T = dt("W1T", [DEPTH_A, DIM_A, DFF_A], F32, kind="ExternalInput").ap()
    b1_c = dt("b1_c", [DEPTH_A, 128, 16], F32, kind="ExternalInput").ap()
    W2T = dt("W2T", [DEPTH_A, DFF_A, DIM_A], F32, kind="ExternalInput").ap()
    b2_c = dt("b2_c", [DEPTH_A, 128, 4], F32, kind="ExternalInput").ap()
    WoutT = dt("WoutT", [DIM_A, V], F32, kind="ExternalInput").ap()
    bout_c = dt("bout_c", [128, 32], F32, kind="ExternalInput").ap()
    onesrow_in = dt("onesrow_in", [1, TOK], F32, kind="ExternalInput").ap()
    if env is None:
        xout = dt("xout", [V, L], F32, kind="ExternalOutput").ap()
    else:
        xout = env["x1d"]
    xpre = dt("xpre", [V, L], F32).ap()

    sub_sb = contextlib.ExitStack()
    sb = lambda name, shape, dtp=F32: sub_sb.enter_context(nc.sbuf_tensor(P_ + name, shape, dtp))
    if env is None:
        psz = lambda name, shape: ctx.enter_context(nc.psum_tensor(name, shape, F32))
    else:
        psz = None

    ht = sb("ht", [128, 8 * TOK])
    tre = sb("tre", [128, TOK])
    tim = sb("tim", [128, TOK])
    mre = sb("mre", [128, TOK])
    mim = sb("mim", [128, TOK])
    rhob = sb("rhob", [128, TOK])
    ure = sb("ure", [128, TOK])
    uim = sb("uim", [128, TOK])
    yt = sb("yt", [128, TOK])
    scr = sb("scr", [128, TOK])
    scr2 = sb("scr2", [128, TOK])
    y2f = sb("y2f", [128, 16 * TOK], BF16)
    ybf = sb("ybf", [128, 4 * TOK], BF16)
    sqb = sb("sqb", [128, 512], BF16)
    cpb = sb("cpb", [128, 512], BF16)
    xst = sb("xst", [128, 2 * 512])
    xbt = sb("xbt", [128, 2 * 512], BF16)
    wst = sb("wst", [128, 4096])
    wsb = sb("wsb", [128, 4096], BF16)
    iota1 = sb("iota1", [128, TOK])
    onescol = sb("onescol", [128, 1], BF16)
    onescolf = sb("onescolf", [128, 1])
    onesrow = sb("onesrow", [1, 128])
    pcol = {nm: sb("p_" + nm, [128, 4]) for nm in
            ["phzre", "phzim", "lcre", "lcim", "scl", "b2"]}
    b1col = sb("p_b1", [128, 16])
    bincol = sb("p_bin", [128, 4])
    boutcol = sb("p_bout", [128, 32])
    rr = sb("rr", [128, 4])
    rho = sb("rho", [128, 4])
    cwre = sb("cwre", [128, 4])
    cwim = sb("cwim", [128, 4])
    zsq = sb("zsq", [128, 4, 11])
    zsqi = sb("zsqi", [128, 4, 11])
    c0re = sb("c0re", [128, 4])
    c0im = sb("c0im", [128, 4])
    cendre = sb("cendre", [128, 4])
    cendim = sb("cendim", [128, 4])
    t4a = sb("t4a", [128, 4])
    t4b = sb("t4b", [128, 4])
    t4c = sb("t4c", [128, 4])
    lnw = sb("lnw", [2, DIM_A])
    rowsA = sb("rowsA", [1, TOK])
    rowsB = sb("rowsB", [2, TOK])
    srow = sb("srow", [1, 512])
    srow2 = sb("srow2", [1, 512])

    if env is None:
        pb = [psz(f"pb{i}", [128, 512]) for i in range(4)]
        pbc1 = psz("pbc1", [128, TOK])
        pst1 = psz("pst1", [1, 512])
        pst2 = psz("pst2", [1, 512])
    else:
        pb, pbc1, pst1, pst2 = env["pb"], env["pbc1"], env["pst1"], env["pst2"]

    s_ld = Sem(ctx, nc, P_ + "s_ld")
    s_pe = Sem(ctx, nc, P_ + "s_pe")
    s_v = Sem(ctx, nc, P_ + "s_v")
    s_s = Sem(ctx, nc, P_ + "s_s")
    s_st = Sem(ctx, nc, P_ + "s_st")
    s_bar = Sem(ctx, nc, P_ + "s_bar")

    PE, VE, SE, GE, SY = nc.tensor, nc.vector, nc.scalar, nc.gpsimd, nc.sync

    def v_done():
        VE.drain().then_inc(s_v.h, 1)
        return s_v.inc()

    def s_done():
        SE.drain().then_inc(s_s.h, 1)
        return s_s.inc()

    def pe_mark(inst):
        inst.then_inc(s_pe.h, 1)
        return s_pe.inc()

    def ld(dst, src):
        SY.dma_start(out=dst, in_=src).then_inc(s_ld.h, 16)
        return s_ld.inc(16)

    def gate():
        SY.wait_ge(s_v.h, s_v.n)
        SY.wait_ge(s_s.h, s_s.n)
        SY.wait_ge(s_pe.h, s_pe.n)

    def ln_stats_half(src_ap_fn, nchunks, th):
        SE.wait_ge(s_v.h, s_v.n)
        for d in range(nchunks):
            sl = src_ap_fn(d, th)
            if d > 0:
                SE.wait_ge(s_pe.h, s_pe.n)
                VE.wait_ge(s_pe.h, s_pe.n)
            SE.activation(out=sqb[:], in_=sl, func=AF.Square, scale=1.0)
            ns = s_done()
            VE.tensor_copy(out=cpb[:], in_=sl)
            nv = v_done()
            PE.wait_ge(s_s.h, ns)
            PE.wait_ge(s_v.h, nv)
            PE.matmul(out=pst1[0:1, :], lhsT=onescol[:], rhs=cpb[:],
                      start=(d == 0), stop=(d == nchunks - 1))
            mm = PE.matmul(out=pst2[0:1, :], lhsT=onescol[:], rhs=sqb[:],
                           start=(d == 0), stop=(d == nchunks - 1))
            pe_mark(mm)
        return s_pe.n

    def ln_rows_half(nfeat, eps, npe, th, src1=None, src2=None):
        if src1 is None:
            src1 = pst1[0:1, :]
        if src2 is None:
            src2 = pst2[0:1, :]
        sA = rowsA[0:1, th * 512:(th + 1) * 512]
        sB = rowsB[0:1, th * 512:(th + 1) * 512]
        sm = srow[0:1, :]
        sm2 = srow2[0:1, :]
        VE.wait_ge(s_pe.h, npe)
        VE.tensor_scalar(out=sm, in0=src1, scalar1=1.0 / nfeat, scalar2=None,
                         op0=ALU.mult)
        VE.tensor_scalar(out=sA, in0=src2, scalar1=1.0 / nfeat, scalar2=None,
                         op0=ALU.mult)
        VE.drain()
        VE.tensor_tensor(out=sm2, in0=sm, in1=sm, op=ALU.mult)
        VE.drain()
        VE.tensor_tensor(out=sA, in0=sA, in1=sm2, op=ALU.subtract)
        VE.drain()
        VE.tensor_scalar(out=sA, in0=sA, scalar1=float(eps), scalar2=None,
                         op0=ALU.add)
        nv = v_done()
        SE.wait_ge(s_v.h, nv)
        SE.activation(out=sA, in_=sA, func=AF.Sqrt, scale=1.0)
        ns = s_done()
        VE.wait_ge(s_s.h, ns)
        VE.reciprocal(out=sA, in_=sA)
        VE.drain()
        VE.tensor_tensor(out=sm2, in0=sm, in1=sA, op=ALU.mult)
        VE.drain()
        VE.tensor_scalar(out=sB, in0=sm2, scalar1=-1.0, scalar2=None, op0=ALU.mult)
        return v_done()

    def ln_full(src_ap_fn, nchunks, nfeat, eps):
        for th in range(2):
            npe = ln_stats_half(src_ap_fn, nchunks, th)
            ln_rows_half(nfeat, eps, npe, th)
        return v_done()

    def ln_bcast(d, affine):
        if affine:
            lh1 = lnw[0:1, d * 128:(d + 1) * 128]
            lh2 = lnw[0:2, d * 128:(d + 1) * 128]
        else:
            lh1 = onesrow[:]
            lh2 = None
        mm = None
        for th in range(2):
            PE.matmul(out=pbc1[:, th * 512:(th + 1) * 512], lhsT=lh1,
                      rhs=rowsA[0:1, th * 512:(th + 1) * 512], start=True, stop=True)
            tgt = pb[2] if th == 0 else pb[3]
            if affine:
                mm = PE.matmul(out=tgt[:, 0:512], lhsT=lh2,
                               rhs=rowsB[0:2, th * 512:(th + 1) * 512],
                               start=True, stop=True)
            else:
                mm = PE.matmul(out=tgt[:, 0:512], lhsT=lh1,
                               rhs=rowsB[0:1, th * 512:(th + 1) * 512],
                               start=True, stop=True)
        return pe_mark(mm)

    def ln_apply(src_ap, npe):
        VE.wait_ge(s_pe.h, npe)
        VE.wait_ge(s_st.h, s_st.n)
        VE.tensor_tensor(out=scr[:], in0=src_ap, in1=pbc1[:], op=ALU.mult)
        VE.drain()
        VE.tensor_tensor(out=scr[:, 0:512], in0=scr[:, 0:512], in1=pb[2][:, 0:512],
                         op=ALU.add)
        VE.tensor_tensor(out=scr[:, 512:1024], in0=scr[:, 512:1024],
                         in1=pb[3][:, 0:512], op=ALU.add)
        return v_done()

    def emit(g):
        n0 = 0
        for dsti, srci in [(bincol[:], bin_c), (boutcol[:], bout_c),
                           (onesrow[:], onesrow_in[0:1, 0:128]),
                           (rowsB[1:2, :], onesrow_in)]:
            n0 = ld(dsti, srci)
        GE.iota(iota1[:], pattern=[[1, TOK]], base=1, channel_multiplier=0,
                allow_small_or_imprecise_dtypes=True)
        GE.drain().then_inc(s_bar.h, 1)
        ng0 = s_bar.inc()
        VE.wait_ge(s_bar.h, ng0)
        VE.memset(onescolf[:], 1.0)
        VE.drain()
        VE.tensor_copy(out=onescol[:], in_=onescolf[:])
        VE.wait_ge(s_ld.h, n0)
        v_done()

        def hslice(hh, d, th=None):
            base = (hh * 4 + d) * TOK
            if th is None:
                return ht[:, base:base + TOK]
            return ht[:, base + th * 512:base + th * 512 + 512]

        for hh in range(2):
            for th in range(2):
                for i in range(32):
                    buf = i % 2
                    gate()
                    nx = ld(xst[:, buf * 512:(buf + 1) * 512],
                            xin.rearrange("(c p) t -> c p t", p=128)
                            [i, :, hh * TOK + th * 512:hh * TOK + th * 512 + 512])
                    nw = ld(wst[:, buf * 2048:buf * 2048 + 512],
                            WinT.rearrange("(c p) d -> c p d", p=128)[i])
                    VE.wait_ge(s_ld.h, max(nx, nw))
                    if i >= 2:
                        VE.wait_ge(s_pe.h, s_pe.n)
                    VE.tensor_copy(out=xbt[:, buf * 512:(buf + 1) * 512],
                                   in_=xst[:, buf * 512:(buf + 1) * 512])
                    VE.tensor_copy(out=wsb[:, buf * 2048:buf * 2048 + 512],
                                   in_=wst[:, buf * 2048:buf * 2048 + 512])
                    nv = v_done()
                    PE.wait_ge(s_v.h, nv)
                    for d in range(4):
                        mm = PE.matmul(
                            out=pb[d][:, 0:512],
                            lhsT=wsb[:, buf * 2048 + d * 128:buf * 2048 + (d + 1) * 128],
                            rhs=xbt[:, buf * 512:(buf + 1) * 512],
                            start=(i == 0), stop=(i == 31))
                    npe = pe_mark(mm)
                SE.wait_ge(s_pe.h, npe)
                for d in range(4):
                    SE.activation(out=hslice(hh, d, th), in_=pb[d][:, 0:512],
                                  func=AF.Identity, bias=bincol[:, d:d + 1], scale=1.0)
                ns = s_done()
                PE.wait_ge(s_s.h, ns)

        for t in range(DEPTH_A):
            gate()
            for nm, srci in [("phzre", phzre_c[t]), ("phzim", phzim_c[t]),
                             ("lcre", lcre_c[t]), ("lcim", lcim_c[t]),
                             ("scl", scl_c[t]), ("b2", b2_c[t])]:
                ld(pcol[nm][:], srci)
            ld(b1col[:], b1_c[t])
            nln1 = ld(lnw[:], ln1_rows[t])
            VE.wait_ge(s_ld.h, s_ld.n)
            VE.tensor_tensor(out=rr[:], in0=pcol["phzre"][:], in1=pcol["phzre"][:], op=ALU.mult)
            VE.tensor_tensor(out=t4a[:], in0=pcol["phzim"][:], in1=pcol["phzim"][:], op=ALU.mult)
            VE.drain()
            VE.tensor_tensor(out=rr[:], in0=rr[:], in1=t4a[:], op=ALU.add)
            nv = v_done()
            SE.wait_ge(s_v.h, nv)
            SE.activation(out=rr[:], in_=rr[:], func=AF.Sqrt, scale=1.0)
            SE.drain()
            SE.activation(out=rho[:], in_=rr[:], func=AF.Tanh, scale=1.0)
            ns = s_done()
            VE.wait_ge(s_s.h, ns)
            VE.reciprocal(out=t4a[:], in_=rr[:])
            VE.drain()
            VE.tensor_tensor(out=cwre[:], in0=pcol["phzre"][:], in1=t4a[:], op=ALU.mult)
            VE.tensor_tensor(out=cwim[:], in0=pcol["phzim"][:], in1=t4a[:], op=ALU.mult)
            VE.drain()
            VE.tensor_copy(out=zsq[:, :, 0], in_=cwre[:])
            VE.tensor_scalar(out=zsqi[:, :, 0], in0=cwim[:], scalar1=-1.0, scalar2=None,
                             op0=ALU.mult)
            VE.drain()
            for k in range(1, 11):
                a, bq = zsq[:, :, k - 1], zsqi[:, :, k - 1]
                VE.tensor_tensor(out=t4a[:], in0=a, in1=a, op=ALU.mult)
                VE.tensor_tensor(out=t4b[:], in0=bq, in1=bq, op=ALU.mult)
                VE.tensor_tensor(out=t4c[:], in0=a, in1=bq, op=ALU.mult)
                VE.drain()
                VE.tensor_tensor(out=zsq[:, :, k], in0=t4a[:], in1=t4b[:], op=ALU.subtract)
                VE.tensor_scalar(out=zsqi[:, :, k], in0=t4c[:], scalar1=2.0, scalar2=None,
                                 op0=ALU.mult)
                VE.drain()
            VE.tensor_tensor(out=t4a[:], in0=cwre[:], in1=pcol["lcre"][:], op=ALU.mult)
            VE.tensor_tensor(out=t4b[:], in0=cwim[:], in1=pcol["lcim"][:], op=ALU.mult)
            VE.tensor_tensor(out=t4c[:], in0=cwre[:], in1=pcol["lcim"][:], op=ALU.mult)
            VE.tensor_tensor(out=c0im[:], in0=cwim[:], in1=pcol["lcre"][:], op=ALU.mult)
            VE.drain()
            VE.tensor_tensor(out=c0re[:], in0=t4a[:], in1=t4b[:], op=ALU.subtract)
            VE.tensor_tensor(out=c0im[:], in0=c0im[:], in1=t4c[:], op=ALU.add)
            v_done()

            for hh in range(2):
                ln_full(lambda d, th: hslice(hh, d, th), 4, DIM_A, EPS_A)
                VE.wait_ge(s_ld.h, nln1)
                nv = v_done()
                PE.wait_ge(s_v.h, nv)
                for d in range(4):
                    npe = ln_bcast(d, affine=True)
                    nv = ln_apply(hslice(hh, d), npe)
                    SE.wait_ge(s_v.h, nv)
                    SE.activation(out=yt[:], in_=scr[:], func=AF.Silu, scale=1.0)
                    ns = s_done()
                    PE.wait_ge(s_s.h, ns)
                    VE.wait_ge(s_s.h, ns)
                    if hh == 0:
                        VE.memset(mre[:, 0:1], 1.0)
                        VE.memset(mim[:, 0:1], 0.0)
                    else:
                        VE.tensor_copy(out=mre[:, 0:1], in_=zsq[:, d:d + 1, 10])
                        VE.tensor_copy(out=mim[:, 0:1], in_=zsqi[:, d:d + 1, 10])
                    VE.drain()
                    for k in range(10):
                        w = 1 << k
                        re0 = mre[:, 0:w]
                        im0 = mim[:, 0:w]
                        VE.tensor_scalar(out=scr[:, 0:w], in0=re0, scalar1=zsq[:, d:d + 1, k],
                                         scalar2=None, op0=ALU.mult)
                        VE.tensor_scalar(out=scr2[:, 0:w], in0=im0, scalar1=zsqi[:, d:d + 1, k],
                                         scalar2=None, op0=ALU.mult)
                        VE.tensor_scalar(out=ure[:, 0:w], in0=re0, scalar1=zsqi[:, d:d + 1, k],
                                         scalar2=None, op0=ALU.mult)
                        VE.tensor_scalar(out=uim[:, 0:w], in0=im0, scalar1=zsq[:, d:d + 1, k],
                                         scalar2=None, op0=ALU.mult)
                        VE.drain()
                        VE.tensor_tensor(out=mre[:, w:2 * w], in0=scr[:, 0:w],
                                         in1=scr2[:, 0:w], op=ALU.subtract)
                        VE.tensor_tensor(out=mim[:, w:2 * w], in0=ure[:, 0:w],
                                         in1=uim[:, 0:w], op=ALU.add)
                        VE.drain()
                    VE.tensor_scalar(out=rhob[:], in0=iota1[:], scalar1=0.0,
                                     scalar2=rho[:, d:d + 1], op0=ALU.mult, op1=ALU.add)
                    VE.drain()
                    VE.tensor_tensor(out=ure[:], in0=yt[:], in1=mre[:],
                                     op=ALU.mult)
                    VE.tensor_tensor(out=uim[:], in0=yt[:], in1=mim[:],
                                     op=ALU.mult)
                    VE.drain()
                    init_re = c0re[:, d:d + 1] if hh == 0 else cendre[:, d:d + 1]
                    init_im = c0im[:, d:d + 1] if hh == 0 else cendim[:, d:d + 1]
                    VE.tensor_tensor_scan(out=tre[:], data0=rhob[:],
                                          data1=ure[:], initial=init_re,
                                          op0=ALU.mult, op1=ALU.add)
                    VE.drain()
                    VE.tensor_tensor_scan(out=tim[:], data0=rhob[:],
                                          data1=uim[:], initial=init_im,
                                          op0=ALU.mult, op1=ALU.add)
                    VE.drain()
                    if hh == 0:
                        VE.tensor_copy(out=cendre[:, d:d + 1],
                                       in_=tre[:, TOK - 1:TOK])
                        VE.tensor_copy(out=cendim[:, d:d + 1],
                                       in_=tim[:, TOK - 1:TOK])
                        VE.drain()
                    VE.tensor_tensor(out=scr[:], in0=mre[:],
                                     in1=tre[:], op=ALU.mult)
                    VE.tensor_tensor(out=scr2[:], in0=mim[:],
                                     in1=tim[:], op=ALU.mult)
                    VE.drain()
                    VE.tensor_tensor(out=scr[:], in0=scr[:], in1=scr2[:], op=ALU.add)
                    VE.drain()
                    VE.scalar_tensor_tensor(out=hslice(hh, d), in0=scr[:],
                                            scalar=pcol["scl"][:, d:d + 1],
                                            in1=hslice(hh, d), op0=ALU.mult, op1=ALU.add)
                    VE.drain()
                v_done()

            gate()
            nln2 = ld(lnw[:], ln2_rows[t])
            for hh in range(2):
                ln_full(lambda d, th: hslice(hh, d, th), 4, DIM_A, EPS_A)
                VE.wait_ge(s_ld.h, nln2)
                nv = v_done()
                PE.wait_ge(s_v.h, nv)
                for d in range(4):
                    npe = ln_bcast(d, affine=True)
                    nv = ln_apply(hslice(hh, d), npe)
                    SE.wait_ge(s_v.h, nv)
                    SE.activation(out=ybf[:, d * TOK:(d + 1) * TOK], in_=scr[:],
                                  func=AF.Copy, scale=1.0)
                    ns = s_done()
                    PE.wait_ge(s_s.h, ns)
                ny = s_s.n
                for o in range(16):
                    gate()
                    nw = ld(wst[:, 0:512].rearrange("p (c f) -> p c f", c=4),
                            W1T.rearrange("t (c p) f -> t p c f", p=128)
                            [t, :, :, o * 128:(o + 1) * 128])
                    VE.wait_ge(s_ld.h, nw)
                    VE.wait_ge(s_pe.h, s_pe.n)
                    VE.tensor_copy(out=wsb[:, 0:512], in_=wst[:, 0:512])
                    nv = v_done()
                    PE.wait_ge(s_v.h, nv)
                    PE.wait_ge(s_s.h, ny if o == 0 else s_s.n)
                    for c in range(4):
                        for th in range(2):
                            mm = PE.matmul(
                                out=pb[th][:, 0:512],
                                lhsT=wsb[:, c * 128:(c + 1) * 128],
                                rhs=ybf[:, c * TOK + th * 512:c * TOK + th * 512 + 512],
                                start=(c == 0), stop=(c == 3))
                    npe = pe_mark(mm)
                    SE.wait_ge(s_pe.h, npe)
                    for th in range(2):
                        SE.activation(out=y2f[:, o * TOK + th * 512:o * TOK + th * 512 + 512],
                                      in_=pb[th][:, 0:512], func=AF.Silu,
                                      bias=b1col[:, o:o + 1], scale=1.0)
                    s_done()
                for od in range(4):
                    gate()
                    nw = ld(wst[:, 0:2048].rearrange("p (i f) -> p i f", i=16),
                            W2T.rearrange("t (i p) f -> t p i f", p=128)
                            [t, :, :, od * 128:(od + 1) * 128])
                    VE.wait_ge(s_ld.h, nw)
                    VE.wait_ge(s_pe.h, s_pe.n)
                    VE.tensor_copy(out=wsb[:, 0:2048], in_=wst[:, 0:2048])
                    nv = v_done()
                    PE.wait_ge(s_v.h, nv)
                    PE.wait_ge(s_s.h, s_s.n)
                    for i in range(16):
                        for th in range(2):
                            mm = PE.matmul(
                                out=pb[th][:, 0:512],
                                lhsT=wsb[:, i * 128:(i + 1) * 128],
                                rhs=y2f[:, i * TOK + th * 512:i * TOK + th * 512 + 512],
                                start=(i == 0), stop=(i == 15))
                    npe = pe_mark(mm)
                    SE.wait_ge(s_pe.h, npe)
                    for th in range(2):
                        SE.activation(out=(scr if th == 0 else scr2)[:, 0:512],
                                      in_=pb[th][:, 0:512], func=AF.Identity,
                                      bias=pcol["b2"][:, od:od + 1], scale=1.0)
                    ns = s_done()
                    VE.wait_ge(s_s.h, ns)
                    VE.tensor_tensor(out=hslice(hh, od, 0), in0=hslice(hh, od, 0),
                                     in1=scr[:, 0:512], op=ALU.add)
                    VE.tensor_tensor(out=hslice(hh, od, 1), in0=hslice(hh, od, 1),
                                     in1=scr2[:, 0:512], op=ALU.add)
                    nv = v_done()
                    PE.wait_ge(s_v.h, nv)

        gate()
        nlnl = ld(lnw[:], lnl_rows)
        for hh in range(2):
            ln_full(lambda d, th: hslice(hh, d, th), 4, DIM_A, EPS_A)
            VE.wait_ge(s_ld.h, nlnl)
            nv = v_done()
            PE.wait_ge(s_v.h, nv)
            for d in range(4):
                npe = ln_bcast(d, affine=True)
                nv = ln_apply(hslice(hh, d), npe)
                SE.wait_ge(s_v.h, nv)
                SE.activation(out=ybf[:, d * TOK:(d + 1) * TOK], in_=scr[:],
                              func=AF.Copy, scale=1.0)
                ns = s_done()
                PE.wait_ge(s_s.h, ns)
            for o in range(32):
                gate()
                nw = ld(wst[:, 0:512].rearrange("p (c v) -> p c v", c=4),
                        WoutT.rearrange("(c p) v -> p c v", p=128)[:, :, o * 128:(o + 1) * 128])
                nx = ld(xst[:, 0:512],
                        xin.rearrange("(c p) t -> c p t", p=128)
                        [o, :, hh * TOK:hh * TOK + 512])
                nx2 = ld(xst[:, 512:1024],
                         xin.rearrange("(c p) t -> c p t", p=128)
                         [o, :, hh * TOK + 512:hh * TOK + 1024])
                VE.wait_ge(s_ld.h, max(nw, nx, nx2))
                if o > 0:
                    VE.wait_ge(s_pe.h, s_pe.n)
                VE.tensor_copy(out=wsb[:, 0:512], in_=wst[:, 0:512])
                nv = v_done()
                PE.wait_ge(s_v.h, nv)
                for c in range(4):
                    for th in range(2):
                        mm = PE.matmul(
                            out=pb[th][:, 0:512],
                            lhsT=wsb[:, c * 128:(c + 1) * 128],
                            rhs=ybf[:, c * TOK + th * 512:c * TOK + th * 512 + 512],
                            start=(c == 0), stop=(c == 3))
                npe = pe_mark(mm)
                SE.wait_ge(s_pe.h, npe)
                SE.wait_ge(s_st.h, s_st.n)
                for th in range(2):
                    SE.activation(out=(scr if th == 0 else scr2)[:, 0:512],
                                  in_=pb[th][:, 0:512], func=AF.Identity,
                                  bias=boutcol[:, o:o + 1], scale=1.0)
                ns = s_done()
                VE.wait_ge(s_s.h, ns)
                VE.tensor_tensor(out=scr[:, 0:512], in0=scr[:, 0:512], in1=xst[:, 0:512],
                                 op=ALU.add)
                VE.tensor_tensor(out=scr2[:, 0:512], in0=scr2[:, 0:512],
                                 in1=xst[:, 512:1024], op=ALU.add)
                nvp = v_done()
                SY.wait_ge(s_v.h, nvp)
                SY.dma_start(out=xpre.rearrange("(c p) t -> c p t", p=128)
                             [o, :, hh * TOK:hh * TOK + 512],
                             in_=scr[:, 0:512]).then_inc(s_st.h, 16)
                SY.dma_start(out=xpre.rearrange("(c p) t -> c p t", p=128)
                             [o, :, hh * TOK + 512:hh * TOK + 1024],
                             in_=scr2[:, 0:512]).then_inc(s_st.h, 16)
                s_st.inc(32)
                VE.tensor_copy(out=cpb[:], in_=scr[:, 0:512])
                nv = v_done()
                SE.wait_ge(s_v.h, nvp)
                SE.activation(out=sqb[:], in_=scr[:, 0:512], func=AF.Square, scale=1.0)
                ns = s_done()
                PE.wait_ge(s_v.h, nv)
                PE.wait_ge(s_s.h, ns)
                PE.matmul(out=pst1[0:1, :], lhsT=onescol[:], rhs=cpb[:],
                          start=(o == 0), stop=(o == 31))
                mm = PE.matmul(out=pst2[0:1, :], lhsT=onescol[:], rhs=sqb[:],
                               start=(o == 0), stop=(o == 31))
                pe_mark(mm)
                VE.wait_ge(s_pe.h, s_pe.n)
                VE.tensor_copy(out=cpb[:], in_=scr2[:, 0:512])
                nv = v_done()
                SE.wait_ge(s_pe.h, s_pe.n)
                SE.activation(out=sqb[:], in_=scr2[:, 0:512], func=AF.Square, scale=1.0)
                ns = s_done()
                PE.wait_ge(s_v.h, nv)
                PE.wait_ge(s_s.h, ns)
                PE.matmul(out=pb[2][0:1, :], lhsT=onescol[:], rhs=cpb[:],
                          start=(o == 0), stop=(o == 31))
                mm = PE.matmul(out=pb[3][0:1, :], lhsT=onescol[:], rhs=sqb[:],
                               start=(o == 0), stop=(o == 31))
                pe_mark(mm)
            npe = s_pe.n
            ln_rows_half(V, EPS_OUT, npe, 0)
            ln_rows_half(V, EPS_OUT, npe, 1, src1=pb[2][0:1, :], src2=pb[3][0:1, :])
            nv = v_done()
            PE.wait_ge(s_v.h, nv)
            SY.wait_ge(s_st.h, s_st.n)
            for o in range(32):
                npe = ln_bcast(0, affine=False)
                gate()
                nx = ld(xst[:, 0:512],
                        xpre.rearrange("(c p) t -> c p t", p=128)
                        [o, :, hh * TOK:hh * TOK + 512])
                nx2 = ld(xst[:, 512:1024],
                         xpre.rearrange("(c p) t -> c p t", p=128)
                         [o, :, hh * TOK + 512:hh * TOK + 1024])
                VE.wait_ge(s_ld.h, max(nx, nx2))
                VE.wait_ge(s_pe.h, npe)
                if o > 0:
                    VE.wait_ge(s_st.h, s_st.n)
                VE.tensor_tensor(out=scr[:], in0=xst[:, 0:1024], in1=pbc1[:], op=ALU.mult)
                VE.drain()
                VE.tensor_tensor(out=scr[:, 0:512], in0=scr[:, 0:512],
                                 in1=pb[2][:, 0:512], op=ALU.add)
                VE.tensor_tensor(out=scr[:, 512:1024], in0=scr[:, 512:1024],
                                 in1=pb[3][:, 0:512], op=ALU.add)
                nv = v_done()
                SY.wait_ge(s_v.h, nv)
                SY.dma_start(out=xout.rearrange("(c p) t -> c p t", p=128)
                             [o, :, hh * TOK:(hh + 1) * TOK],
                             in_=scr[:]).then_inc(s_st.h, 16)
                s_st.inc(16)
                PE.wait_ge(s_v.h, nv)
        SY.wait_ge(s_st.h, s_st.n)
        g.wait_ge(s_st.h, s_st.n)

    def emit_wrapped(g):
        if env is not None:
            for eng in (PE, VE, SE, SY, GE):
                for (sh, sv) in env.get("entry_deps", []):
                    eng.wait_ge(sh, sv)
        emit(g)
        sub_sb.close()
        if env is not None:
            env["entry_deps"] = [(s_v.h, s_v.n), (s_s.h, s_s.n), (s_pe.h, s_pe.n),
                                 (s_st.h, s_st.n), (s_ld.h, s_ld.n)]

    if env is None:
        with nc.Block() as block:
            @block.gpsimd
            def _(g):
                emit_wrapped(g)
        return nc
    return emit_wrapped


def hostprep_A(inputs, core):
    b = core // 2
    x = np.asarray(inputs["L0_attn_x"])
    im = {
        "xin": np.ascontiguousarray(_f32(x[b]).T),
        "WinT": np.ascontiguousarray(_f32(inputs["L0_attn_Win"]).T),
        "bin_c": colpack(inputs["L0_attn_bin"], 4),
        "ln1_rows": np.ascontiguousarray(
            np.stack([_f32(inputs["L0_attn_ln1_w"]), _f32(inputs["L0_attn_ln1_b"])], axis=1)),
        "ln2_rows": np.ascontiguousarray(
            np.stack([_f32(inputs["L0_attn_ln2_w"]), _f32(inputs["L0_attn_ln2_b"])], axis=1)),
        "lnl_rows": np.ascontiguousarray(
            np.stack([_f32(inputs["L0_attn_lnl_w"]), _f32(inputs["L0_attn_lnl_b"])], axis=0)),
        "phzre_c": np.stack([colpack(_f32(inputs["L0_attn_phz_re"])[t], 4) for t in range(4)]),
        "phzim_c": np.stack([colpack(_f32(inputs["L0_attn_phz_im"])[t], 4) for t in range(4)]),
        "lcre_c": np.stack([colpack(_f32(inputs["L0_attn_lc_re"])[t], 4) for t in range(4)]),
        "lcim_c": np.stack([colpack(_f32(inputs["L0_attn_lc_im"])[t], 4) for t in range(4)]),
        "scl_c": np.stack([colpack(_f32(inputs["L0_attn_sc_lin"])[t], 4) for t in range(4)]),
        "W1T": np.ascontiguousarray(_f32(inputs["L0_attn_W1"]).transpose(0, 2, 1)),
        "b1_c": np.stack([colpack(_f32(inputs["L0_attn_b1"])[t], 16) for t in range(4)]),
        "W2T": np.ascontiguousarray(_f32(inputs["L0_attn_W2"]).transpose(0, 2, 1)),
        "b2_c": np.stack([colpack(_f32(inputs["L0_attn_b2"])[t], 4) for t in range(4)]),
        "WoutT": np.ascontiguousarray(_f32(inputs["L0_attn_Wout"]).T),
        "bout_c": colpack(inputs["L0_attn_bout"], 32),
        "onesrow_in": np.ones((1, TOK), np.float32),
    }
    return im


def run_stageA(inputs):
    nc = build_stageA()
    ims = [hostprep_A(inputs, c) for c in range(NCORES)]
    res = run_bass_kernel_spmd(nc, ims, core_ids=list(range(NCORES)))
    out = np.empty((B, L, V), np.float32)
    for c in range(0, NCORES, 2):
        out[c // 2] = res.results[c]["xout"].T
    return out





def build_stageB(env=None):
    if env is None:
        ctx = contextlib.ExitStack()
        nc = bass.Bass("TRN2", target_bir_lowering=False)
        P_ = ""
    else:
        ctx = env["ctx"]
        nc = env["nc"]
        P_ = "B_"
    dt = lambda name, *a, **k: nc.dram_tensor(P_ + name, *a, **k)

    if env is None:
        xin = dt("xin", [V, TOK], F32, kind="ExternalInput").ap()
        xsrc = lambda i: xin.rearrange("(c p) t -> c p t", p=128)[i]
    else:
        xin = None
        x1f = env["x1d"].rearrange("(c p) t -> c p t", p=128)
        xsrc = lambda i: x1f[i]
    hsel = dt("hsel", [128, 2], F32, kind="ExternalInput").ap()
    WbT = dt("WbT", [V, V], F32, kind="ExternalInput").ap()
    bb_c = dt("bb_c", [128, 32], F32, kind="ExternalInput").ap()
    encT = dt("encT", [V, 40], F32, kind="ExternalInput").ap()
    FT = dt("FT", [V, V], F32, kind="ExternalInput").ap()
    coefs0 = dt("coefs0", [3, V, 32], F32, kind="ExternalInput").ap()
    coefs1 = dt("coefs1", [3, V, 8], F32, kind="ExternalInput").ap()
    mask0 = dt("mask0", [3, V, 32], U8, kind="ExternalInput").ap()
    mask1 = dt("mask1", [3, V, 8], U8, kind="ExternalInput").ap()
    onesrow_in = dt("onesrow_in", [1, TOK], F32, kind="ExternalInput").ap()
    if env is None:
        xout = dt("xout", [V, TOK], F32, kind="ExternalOutput").ap()
    else:
        xout = env["x2d"]
    xpre = dt("xpre", [V, TOK], F32).ap()

    sub_sb = contextlib.ExitStack()
    sb = lambda name, shape, dtp=F32: sub_sb.enter_context(nc.sbuf_tensor(P_ + name, shape, dtp))

    x1b = sb("x1b", [128, 32 * TOK], BF16)
    xw = sb("xw", [128, L]) if env is not None else None
    hsels = sb("hsels", [128, 2])
    w01 = sb("w01", [128, 3, 40])
    w01m = sb("w01m", [128, 3, 40])
    w01all = sb("w01all", [128, 32, 40], BF16)
    fwT = sb("fwT", [40, V], BF16)
    ebf = sb("ebf", [40, TOK], BF16)
    wst = sb("wst", [128, 4096])
    wsb = sb("wsb", [128, 4096], BF16)
    xst = sb("xst", [128, TOK])
    scr = sb("scr", [128, TOK])
    scr2 = sb("scr2", [128, TOK])
    sqb = sb("sqb", [128, 512], BF16)
    cpb = sb("cpb", [128, 512], BF16)
    onescol = sb("onescol", [128, 1], BF16)
    onescolf = sb("onescolf", [128, 1])
    onesrow = sb("onesrow", [1, 128])
    bbcol = sb("bbcol", [128, 32])
    rowsA = sb("rowsA", [1, TOK])
    rowsB = sb("rowsB", [2, TOK])
    srow = sb("srow", [1, 512])
    srow2 = sb("srow2", [1, 512])

    if env is None:
        psz = lambda name, shape: ctx.enter_context(nc.psum_tensor(name, shape, F32))
        pb = [psz(f"pb{i}", [128, 512]) for i in range(4)]
        pbc1 = psz("pbc1", [128, TOK])
        pst1 = psz("pst1", [1, 512])
        pst2 = psz("pst2", [1, 512])
    else:
        pb, pbc1, pst1, pst2 = env["pb"], env["pbc1"], env["pst1"], env["pst2"]

    s_ld = Sem(ctx, nc, P_ + "s_ld")
    s_pe = Sem(ctx, nc, P_ + "s_pe")
    s_v = Sem(ctx, nc, P_ + "s_v")
    s_s = Sem(ctx, nc, P_ + "s_s")
    s_st = Sem(ctx, nc, P_ + "s_st")
    s_lg = Sem(ctx, nc, P_ + "s_lg")

    PE, VE, SE, GE, SY = nc.tensor, nc.vector, nc.scalar, nc.gpsimd, nc.sync

    def v_done():
        VE.drain().then_inc(s_v.h, 1)
        return s_v.inc()

    def s_done():
        SE.drain().then_inc(s_s.h, 1)
        return s_s.inc()

    def pe_mark(inst):
        inst.then_inc(s_pe.h, 1)
        return s_pe.inc()

    def ld(dst, src):
        SY.dma_start(out=dst, in_=src).then_inc(s_ld.h, 16)
        return s_ld.inc(16)

    def ldx(g, dst, src):
        if env is None:
            return ("ld", ld(dst, src))
        g.dma_start(out=dst, in_=src).then_inc(s_lg.h, 16)
        return ("lg", s_lg.inc(16))

    def waitx(eng, tok):
        kindv, n = tok
        eng.wait_ge(s_ld.h if kindv == "ld" else s_lg.h, n)

    def gate():
        SY.wait_ge(s_v.h, s_v.n)
        SY.wait_ge(s_s.h, s_s.n)
        SY.wait_ge(s_pe.h, s_pe.n)

    def gateg(g):
        g.wait_ge(s_v.h, s_v.n)
        g.wait_ge(s_s.h, s_s.n)
        g.wait_ge(s_pe.h, s_pe.n)

    def ln_rows_half(nfeat, eps, npe, th, src1, src2):
        sA = rowsA[0:1, th * 512:(th + 1) * 512]
        sB = rowsB[0:1, th * 512:(th + 1) * 512]
        sm = srow[0:1, :]
        sm2 = srow2[0:1, :]
        VE.wait_ge(s_pe.h, npe)
        VE.tensor_scalar(out=sm, in0=src1, scalar1=1.0 / nfeat, scalar2=None,
                         op0=ALU.mult)
        VE.tensor_scalar(out=sA, in0=src2, scalar1=1.0 / nfeat, scalar2=None,
                         op0=ALU.mult)
        VE.drain()
        VE.tensor_tensor(out=sm2, in0=sm, in1=sm, op=ALU.mult)
        VE.drain()
        VE.tensor_tensor(out=sA, in0=sA, in1=sm2, op=ALU.subtract)
        VE.drain()
        VE.tensor_scalar(out=sA, in0=sA, scalar1=float(eps), scalar2=None,
                         op0=ALU.add)
        nv = v_done()
        SE.wait_ge(s_v.h, nv)
        SE.activation(out=sA, in_=sA, func=AF.Sqrt, scale=1.0)
        ns = s_done()
        VE.wait_ge(s_s.h, ns)
        VE.reciprocal(out=sA, in_=sA)
        VE.drain()
        VE.tensor_tensor(out=sm2, in0=sm, in1=sA, op=ALU.mult)
        VE.drain()
        VE.tensor_scalar(out=sB, in0=sm2, scalar1=-1.0, scalar2=None, op0=ALU.mult)
        return v_done()

    def ld_xsel(i, tok_v):
        if env is None:
            nx = ld(xst[:], xsrc(i))
            VE.wait_ge(s_ld.h, nx)
            return v_done()
        nx = ld(xw[:], xsrc(i))
        VE.wait_ge(s_ld.h, nx)
        VE.tensor_scalar(out=scr2[:], in0=xw[:, 0:TOK], scalar1=hsels[:, 0:1],
                         scalar2=None, op0=ALU.mult)
        VE.drain()
        VE.scalar_tensor_tensor(out=xst[:], in0=xw[:, TOK:L], scalar=hsels[:, 1:2],
                                in1=scr2[:], op0=ALU.mult, op1=ALU.add)
        return v_done()

    def emit(g):
        n0 = ld(hsels[:], hsel)
        n0 = ld(bbcol[:], bb_c)
        n0 = ld(onesrow[:], onesrow_in[0:1, 0:128])
        n0 = ld(rowsB[1:2, :], onesrow_in)
        VE.wait_ge(s_ld.h, n0)
        VE.memset(onescolf[:], 1.0)
        VE.drain()
        VE.tensor_copy(out=onescol[:], in_=onescolf[:])
        v_done()

        for i in range(32):
            gate()
            ld_xsel(i, None)
            VE.tensor_copy(out=x1b[:, i * TOK:(i + 1) * TOK], in_=xst[:])
            v_done()

        for c in range(32):
            gate()
            n1 = ld(w01[:, :, 0:32], coefs0.rearrange("e (c p) r -> c p e r", p=128)[c])
            n1 = ld(w01[:, :, 32:40], coefs1.rearrange("e (c p) r -> c p e r", p=128)[c])
            n1 = ld(w01m[:, :, 0:32].bitcast(U8) if False else w01m[:, :, 0:32],
                    mask0.rearrange("e (c p) r -> c p e r", p=128)[c]) if False else None
            n1 = ld(mu8[:, :, 0:32], mask0.rearrange("e (c p) r -> c p e r", p=128)[c])
            n1 = ld(mu8[:, :, 32:40], mask1.rearrange("e (c p) r -> c p e r", p=128)[c])
            VE.wait_ge(s_ld.h, n1)
            VE.tensor_copy(out=w01m[:], in_=mu8[:])
            VE.drain()
            VE.tensor_tensor(out=w01m[:], in0=w01m[:], in1=w01[:], op=ALU.mult)
            VE.drain()
            VE.tensor_tensor(out=w01[:, 0, :], in0=w01m[:, 0, :], in1=w01m[:, 1, :],
                             op=ALU.add)
            VE.drain()
            VE.tensor_tensor(out=w01all[:, c, :], in0=w01[:, 0, :], in1=w01m[:, 2, :],
                             op=ALU.add)
            v_done()

        for n in range(8):
            for j in range(32):
                gate()
                nw = ld(wst[:, 0:512], FT.rearrange("(j p) n -> j p n", p=128)
                        [j, :, n * 512:(n + 1) * 512])
                VE.wait_ge(s_ld.h, nw)
                VE.wait_ge(s_pe.h, s_pe.n)
                VE.tensor_copy(out=wsb[:, 0:512], in_=wst[:, 0:512])
                nv = v_done()
                PE.wait_ge(s_v.h, nv)
                mm = PE.matmul(out=pb[0][0:40, :], lhsT=w01all[:, j, :],
                               rhs=wsb[:, 0:512], start=(j == 0), stop=(j == 31))
                npe = pe_mark(mm)
            SE.wait_ge(s_pe.h, npe)
            SE.activation(out=fwT[:, n * 512:(n + 1) * 512], in_=pb[0][0:40, :],
                          func=AF.Copy, scale=1.0)
            ns = s_done()
            PE.wait_ge(s_s.h, ns)

        for i in range(32):
            gate()
            nw = ld(wst[:, 0:40], encT.rearrange("(c p) r -> c p r", p=128)[i])
            VE.wait_ge(s_ld.h, nw)
            if i >= 1:
                VE.wait_ge(s_pe.h, s_pe.n)
            VE.tensor_copy(out=wsb[:, 0:40], in_=wst[:, 0:40])
            nv = v_done()
            PE.wait_ge(s_v.h, nv)
            for th in range(2):
                mm = PE.matmul(out=(pb[0] if th == 0 else pb[1])[0:40, :],
                               lhsT=wsb[:, 0:40],
                               rhs=x1b[:, i * TOK + th * 512:i * TOK + th * 512 + 512],
                               start=(i == 0), stop=(i == 31))
            npe = pe_mark(mm)
        SE.wait_ge(s_pe.h, npe)
        for th in range(2):
            SE.activation(out=ebf[:, th * 512:(th + 1) * 512],
                          in_=(pb[0] if th == 0 else pb[1])[0:40, :],
                          func=AF.Copy, scale=1.0)
        ns = s_done()
        PE.wait_ge(s_s.h, ns)

        for o in range(32):
            gate()
            nw = ld(wst[:].rearrange("p (c f) -> p c f", c=32),
                    WbT.rearrange("(c p) f -> p c f", p=128)[:, :, o * 128:(o + 1) * 128])
            ld_xsel(o, None)
            VE.wait_ge(s_ld.h, nw)
            if o > 0:
                VE.wait_ge(s_pe.h, s_pe.n)
            VE.tensor_copy(out=wsb[:], in_=wst[:])
            nv = v_done()
            PE.wait_ge(s_v.h, nv)
            for c in range(32):
                for th in range(2):
                    mm = PE.matmul(
                        out=pb[th][:, 0:512],
                        lhsT=wsb[:, c * 128:(c + 1) * 128],
                        rhs=x1b[:, c * TOK + th * 512:c * TOK + th * 512 + 512],
                        start=(c == 0), stop=(c == 31))
            for th in range(2):
                mm = PE.matmul(out=pbc1[:, th * 512:th * 512 + 512],
                               lhsT=fwT[:, o * 128:(o + 1) * 128],
                               rhs=ebf[:, th * 512:(th + 1) * 512],
                               start=True, stop=True)
            npe = pe_mark(mm)
            SE.wait_ge(s_pe.h, npe)
            SE.wait_ge(s_st.h, s_st.n)
            for th in range(2):
                SE.activation(out=(scr if th == 0 else scr2)[:, 0:512],
                              in_=pb[th][:, 0:512], func=AF.Identity,
                              bias=bbcol[:, o:o + 1], scale=1.0)
            ns = s_done()
            VE.wait_ge(s_s.h, ns)
            VE.scalar_tensor_tensor(out=scr[:, 0:512], in0=pbc1[:, 0:512], scalar=0.5,
                                    in1=scr[:, 0:512], op0=ALU.mult, op1=ALU.add)
            VE.scalar_tensor_tensor(out=scr2[:, 0:512], in0=pbc1[:, 512:1024], scalar=0.5,
                                    in1=scr2[:, 0:512], op0=ALU.mult, op1=ALU.add)
            VE.drain()
            VE.tensor_tensor(out=scr[:, 0:512], in0=scr[:, 0:512], in1=xst[:, 0:512],
                             op=ALU.add)
            VE.tensor_tensor(out=scr2[:, 0:512], in0=scr2[:, 0:512], in1=xst[:, 512:1024],
                             op=ALU.add)
            nvp = v_done()
            SY.wait_ge(s_v.h, nvp)
            SY.dma_start(out=xpre.rearrange("(c p) t -> c p t", p=128)[o, :, 0:512],
                         in_=scr[:, 0:512]).then_inc(s_st.h, 16)
            SY.dma_start(out=xpre.rearrange("(c p) t -> c p t", p=128)[o, :, 512:1024],
                         in_=scr2[:, 0:512]).then_inc(s_st.h, 16)
            s_st.inc(32)
            VE.tensor_copy(out=cpb[:], in_=scr[:, 0:512])
            nv = v_done()
            SE.wait_ge(s_v.h, nvp)
            SE.activation(out=sqb[:], in_=scr[:, 0:512], func=AF.Square, scale=1.0)
            ns = s_done()
            PE.wait_ge(s_v.h, nv)
            PE.wait_ge(s_s.h, ns)
            PE.matmul(out=pst1[0:1, :], lhsT=onescol[:], rhs=cpb[:],
                      start=(o == 0), stop=(o == 31))
            mm = PE.matmul(out=pst2[0:1, :], lhsT=onescol[:], rhs=sqb[:],
                           start=(o == 0), stop=(o == 31))
            pe_mark(mm)
            VE.wait_ge(s_pe.h, s_pe.n)
            VE.tensor_copy(out=cpb[:], in_=scr2[:, 0:512])
            nv = v_done()
            SE.wait_ge(s_pe.h, s_pe.n)
            SE.activation(out=sqb[:], in_=scr2[:, 0:512], func=AF.Square, scale=1.0)
            ns = s_done()
            PE.wait_ge(s_v.h, nv)
            PE.wait_ge(s_s.h, ns)
            PE.matmul(out=pb[2][0:1, :], lhsT=onescol[:], rhs=cpb[:],
                      start=(o == 0), stop=(o == 31))
            mm = PE.matmul(out=pb[3][0:1, :], lhsT=onescol[:], rhs=sqb[:],
                           start=(o == 0), stop=(o == 31))
            pe_mark(mm)
        npe = s_pe.n
        ln_rows_half(V, EPS_OUT, npe, 0, pst1[0:1, :], pst2[0:1, :])
        ln_rows_half(V, EPS_OUT, npe, 1, pb[2][0:1, :], pb[3][0:1, :])
        nv = v_done()
        PE.wait_ge(s_v.h, nv)
        SY.wait_ge(s_st.h, s_st.n)
        for o in range(32):
            mm = None
            for th in range(2):
                PE.matmul(out=pbc1[:, th * 512:(th + 1) * 512], lhsT=onesrow[:],
                          rhs=rowsA[0:1, th * 512:(th + 1) * 512], start=True, stop=True)
                mm = PE.matmul(out=(pb[2] if th == 0 else pb[3])[:, 0:512],
                               lhsT=onesrow[:],
                               rhs=rowsB[0:1, th * 512:(th + 1) * 512],
                               start=True, stop=True)
            npe = pe_mark(mm)
            gate()
            nx = ld(xst[:], xpre.rearrange("(c p) t -> c p t", p=128)[o])
            VE.wait_ge(s_ld.h, nx)
            VE.wait_ge(s_pe.h, npe)
            VE.wait_ge(s_st.h, s_st.n)
            VE.tensor_tensor(out=scr[:], in0=xst[:], in1=pbc1[:], op=ALU.mult)
            VE.drain()
            VE.tensor_tensor(out=scr[:, 0:512], in0=scr[:, 0:512],
                             in1=pb[2][:, 0:512], op=ALU.add)
            VE.tensor_tensor(out=scr[:, 512:1024], in0=scr[:, 512:1024],
                             in1=pb[3][:, 0:512], op=ALU.add)
            nv = v_done()
            SY.wait_ge(s_v.h, nv)
            SY.dma_start(out=xout.rearrange("(c p) t -> c p t", p=128)[o],
                         in_=scr[:]).then_inc(s_st.h, 16)
            s_st.inc(16)
            PE.wait_ge(s_v.h, nv)
        SY.wait_ge(s_st.h, s_st.n)
        g.wait_ge(s_st.h, s_st.n)

    mu8 = sub_sb.enter_context(nc.sbuf_tensor(P_ + "mu8", [128, 3, 40], U8))

    def emit_wrapped(g):
        if env is not None:
            for eng in (PE, VE, SE, SY, GE):
                for (sh, sv) in env.get("entry_deps", []):
                    eng.wait_ge(sh, sv)
        emit(g)
        sub_sb.close()
        if env is not None:
            env["entry_deps"] = [(s_v.h, s_v.n), (s_s.h, s_s.n), (s_pe.h, s_pe.n),
                                 (s_st.h, s_st.n), (s_ld.h, s_ld.n)]

    if env is None:
        with nc.Block() as block:
            @block.gpsimd
            def _(g):
                emit_wrapped(g)
        return nc
    return emit_wrapped


def make_FT():
    n = V
    t = np.arange(n, dtype=np.float64)
    FT = np.empty((n, n), dtype=np.float32)
    FT[0, :] = 1.0 / np.sqrt(n)
    maxf = (n - 1) // 2
    ps = 1.0 / np.sqrt(n / 2.0)
    f = np.arange(1, maxf + 1, dtype=np.float64)
    ang = 2.0 * np.pi * np.outer(f, t) / n
    FT[1:1 + 2 * maxf:2, :] = (np.cos(ang) * ps).astype(np.float32)
    FT[2:2 + 2 * maxf:2, :] = (np.sin(ang) * ps).astype(np.float32)
    FT[n - 1, :] = ((-1.0) ** t / np.sqrt(n)).astype(np.float32)
    return FT


_FT_CACHE = [None]


def hostprep_B(inputs, core, x1):
    b, half = core // 2, core % 2
    if _FT_CACHE[0] is None:
        _FT_CACHE[0] = make_FT()
    im = {
        "xin": (np.zeros((V, TOK), np.float32) if x1 is None else
                np.ascontiguousarray(_f32(x1[b, half * TOK:(half + 1) * TOK, :]).T)),
        "WbT": np.ascontiguousarray(_f32(inputs["L0_ffn_W_base"]).T),
        "bb_c": colpack(inputs["L0_ffn_b_base"], 32),
        "encT": np.ascontiguousarray(np.concatenate(
            [_f32(inputs["L0_ffn_enc0"]).T, _f32(inputs["L0_ffn_enc1"]).T], axis=1)),
        "FT": _FT_CACHE[0],
        "coefs0": _f32(inputs["L0_ffn_coefs0"]),
        "coefs1": _f32(inputs["L0_ffn_coefs1"]),
        "mask0": np.ascontiguousarray(np.asarray(inputs["L0_ffn_mask0"]).view(np.uint8)),
        "mask1": np.ascontiguousarray(np.asarray(inputs["L0_ffn_mask1"]).view(np.uint8)),
        "onesrow_in": np.ones((1, TOK), np.float32),
        "hsel": np.broadcast_to(
            np.array([1.0 - half, float(half)], np.float32), (128, 2)).copy(),
    }
    return im


def run_stageB(inputs, x1):
    nc = build_stageB()
    ims = [hostprep_B(inputs, c, x1) for c in range(NCORES)]
    res = run_bass_kernel_spmd(nc, ims, core_ids=list(range(NCORES)))
    out = np.empty((B, L, V), np.float32)
    for c in range(NCORES):
        b, half = c // 2, c % 2
        out[b, half * TOK:(half + 1) * TOK, :] = res.results[c]["xout"].T
    return out





def build_stageD(env=None):
    if env is None:
        ctx = contextlib.ExitStack()
        nc = bass.Bass("TRN2", target_bir_lowering=False)
        P_ = ""
    else:
        ctx = env["ctx"]
        nc = env["nc"]
        P_ = "D_"
    dt = lambda name, *a, **k: nc.dram_tensor(P_ + name, *a, **k)

    if env is None:
        xin = dt("xin", [V, TOK], F32, kind="ExternalInput").ap()
    else:
        xin = env["x3d"]
    AwT = dt("AwT", [V, 512], F32, kind="ExternalInput").ap()
    BwT = dt("BwT", [512, V], F32, kind="ExternalInput").ap()
    WqT = dt("WqT", [V, 16], F32, kind="ExternalInput").ap()
    keysT = dt("keysT", [16, 64], F32, kind="ExternalInput").ap()
    ident = dt("ident", [128, 128], F32, kind="ExternalInput").ap()
    expT = dt("expT", [64, 512], F32, kind="ExternalInput").ap()
    onesrow_in = dt("onesrow_in", [1, TOK], F32, kind="ExternalInput").ap()
    xout = dt("xout", [V, TOK], F32, kind="ExternalOutput").ap()
    xpre = dt("xpre", [V, TOK], F32).ap()

    sub_sb = contextlib.ExitStack()
    sb = lambda name, shape, dtp=F32: sub_sb.enter_context(nc.sbuf_tensor(P_ + name, shape, dtp))

    x2b = sb("x2b", [128, 32 * TOK], BF16)
    x2f = sb("x2f", [128, 2 * TOK])
    wqs = sb("wqs", [128, 2 * 16])
    keyss = sb("keyss", [16, 64])
    idents = sb("idents", [128, 128])
    expTs = sb("expTs", [64, 512])
    qrs = sb("qrs", [16, TOK])
    scs = sb("scs", [64, TOK])
    stm = sb("stm", [128, 8, 64])
    work = sb("work", [128, 64])
    work2 = sb("work2", [128, 64])
    mx1 = sb("mx1", [128, 8])
    mx2 = sb("mx2", [128, 8])
    mneg = sb("mneg", [128, 1])
    expo = sb("expo", [128, 64])
    sel = sb("sel", [128, 64])
    den = sb("den", [128, 1])
    gtm = sb("gtm", [128, 8, 64])
    gfm = sb("gfm", [64, TOK])
    zgb = sb("zgb", [128, 4 * TOK], BF16)
    wst = sb("wst", [128, 4096])
    wsb = sb("wsb", [128, 4096], BF16)
    xst = sb("xst", [128, TOK])
    scr = sb("scr", [128, TOK])
    scr2 = sb("scr2", [128, TOK])
    sqb = sb("sqb", [128, 512], BF16)
    cpb = sb("cpb", [128, 512], BF16)
    onescol = sb("onescol", [128, 1], BF16)
    onescolf = sb("onescolf", [128, 1])
    onesrow = sb("onesrow", [1, 128])
    rowsA = sb("rowsA", [1, TOK])
    rowsB = sb("rowsB", [2, TOK])
    srow = sb("srow", [1, 512])
    srow2 = sb("srow2", [1, 512])

    if env is None:
        psz = lambda name, shape: ctx.enter_context(nc.psum_tensor(name, shape, F32))
        pb = [psz(f"pb{i}", [128, 512]) for i in range(4)]
        pbc1 = psz("pbc1", [128, TOK])
        pst1 = psz("pst1", [1, 512])
        pst2 = psz("pst2", [1, 512])
    else:
        pb, pbc1, pst1, pst2 = env["pb"], env["pbc1"], env["pst1"], env["pst2"]

    s_ld = Sem(ctx, nc, P_ + "s_ld")
    s_pe = Sem(ctx, nc, P_ + "s_pe")
    s_v = Sem(ctx, nc, P_ + "s_v")
    s_s = Sem(ctx, nc, P_ + "s_s")
    s_st = Sem(ctx, nc, P_ + "s_st")

    PE, VE, SE, GE, SY = nc.tensor, nc.vector, nc.scalar, nc.gpsimd, nc.sync

    def v_done():
        VE.drain().then_inc(s_v.h, 1)
        return s_v.inc()

    def s_done():
        SE.drain().then_inc(s_s.h, 1)
        return s_s.inc()

    def pe_mark(inst):
        inst.then_inc(s_pe.h, 1)
        return s_pe.inc()

    def ld(dst, src):
        SY.dma_start(out=dst, in_=src).then_inc(s_ld.h, 16)
        return s_ld.inc(16)

    def gate():
        SY.wait_ge(s_v.h, s_v.n)
        SY.wait_ge(s_s.h, s_s.n)
        SY.wait_ge(s_pe.h, s_pe.n)

    def ln_rows_half(nfeat, eps, npe, th, src1, src2):
        sA = rowsA[0:1, th * 512:(th + 1) * 512]
        sB = rowsB[0:1, th * 512:(th + 1) * 512]
        sm = srow[0:1, :]
        sm2 = srow2[0:1, :]
        VE.wait_ge(s_pe.h, npe)
        VE.tensor_scalar(out=sm, in0=src1, scalar1=1.0 / nfeat, scalar2=None,
                         op0=ALU.mult)
        VE.tensor_scalar(out=sA, in0=src2, scalar1=1.0 / nfeat, scalar2=None,
                         op0=ALU.mult)
        VE.drain()
        VE.tensor_tensor(out=sm2, in0=sm, in1=sm, op=ALU.mult)
        VE.drain()
        VE.tensor_tensor(out=sA, in0=sA, in1=sm2, op=ALU.subtract)
        VE.drain()
        VE.tensor_scalar(out=sA, in0=sA, scalar1=float(eps), scalar2=None,
                         op0=ALU.add)
        nv = v_done()
        SE.wait_ge(s_v.h, nv)
        SE.activation(out=sA, in_=sA, func=AF.Sqrt, scale=1.0)
        ns = s_done()
        VE.wait_ge(s_s.h, ns)
        VE.reciprocal(out=sA, in_=sA)
        VE.drain()
        VE.tensor_tensor(out=sm2, in0=sm, in1=sA, op=ALU.mult)
        VE.drain()
        VE.tensor_scalar(out=sB, in0=sm2, scalar1=-1.0, scalar2=None, op0=ALU.mult)
        return v_done()

    def emit(g):
        n0 = ld(onesrow[:], onesrow_in[0:1, 0:128])
        n0 = ld(rowsB[1:2, :], onesrow_in)
        n0 = ld(keyss[:], keysT)
        n0 = ld(idents[:], ident)
        n0 = ld(expTs[:], expT)
        VE.wait_ge(s_ld.h, n0)
        VE.memset(onescolf[:], 1.0)
        VE.drain()
        VE.tensor_copy(out=onescol[:], in_=onescolf[:])
        v_done()

        for i in range(32):
            gate()
            nx = ld(x2f[:, 0:TOK], xin.rearrange("(c p) t -> c p t", p=128)[i])
            nw = ld(wqs[:, 0:16], WqT.rearrange("(c p) r -> c p r", p=128)[i])
            VE.wait_ge(s_ld.h, max(nx, nw))
            if i >= 1:
                VE.wait_ge(s_pe.h, s_pe.n)
            VE.tensor_copy(out=x2b[:, i * TOK:(i + 1) * TOK], in_=x2f[:, 0:TOK])
            nv = v_done()
            PE.wait_ge(s_v.h, nv)
            for th in range(2):
                mm = PE.matmul(out=(pb[0] if th == 0 else pb[1])[0:16, :],
                               lhsT=wqs[:, 0:16],
                               rhs=x2f[:, th * 512:(th + 1) * 512],
                               start=(i == 0), stop=(i == 31))
            npe = pe_mark(mm)
        SE.wait_ge(s_pe.h, npe)
        for th in range(2):
            SE.activation(out=qrs[:, th * 512:(th + 1) * 512],
                          in_=(pb[0] if th == 0 else pb[1])[0:16, :],
                          func=AF.Copy, scale=1.0)
        ns = s_done()
        PE.wait_ge(s_s.h, ns)
        for th in range(2):
            mm = PE.matmul(out=(pb[0] if th == 0 else pb[1])[0:64, :], lhsT=keyss[:],
                           rhs=qrs[:, th * 512:(th + 1) * 512], start=True, stop=True)
        npe = pe_mark(mm)
        SE.wait_ge(s_pe.h, npe)
        for th in range(2):
            SE.activation(out=scs[:, th * 512:(th + 1) * 512],
                          in_=(pb[0] if th == 0 else pb[1])[0:64, :],
                          func=AF.Copy, scale=1.0)
        ns = s_done()
        PE.wait_ge(s_s.h, ns)
        for tt8 in range(8):
            mm = PE.transpose(out=pb[2][:, 0:64], in_=scs[0:64, tt8 * 128:(tt8 + 1) * 128],
                              identity=idents[0:64, 0:64])
            npe = pe_mark(mm)
            VE.wait_ge(s_pe.h, npe)
            VE.tensor_copy(out=stm[:, tt8, :], in_=pb[2][:, 0:64])
            nv = v_done()
            PE.wait_ge(s_v.h, nv)
        for tt8 in range(8):
            VE.tensor_copy(out=work[:], in_=stm[:, tt8, :])
            VE.drain()
            VE.max(out=mx1[:], in_=work[:])
            VE.drain()
            VE.match_replace(out=work2[:], in_to_replace=mx1[:], in_values=work[:],
                             imm_value=-1e30)
            VE.drain()
            VE.max(out=mx2[:], in_=work2[:])
            VE.drain()
            VE.tensor_scalar(out=mneg[:], in0=mx1[:, 0:1], scalar1=-1.0, scalar2=None,
                             op0=ALU.mult)
            nv = v_done()
            SE.wait_ge(s_v.h, nv)
            SE.activation(out=expo[:], in_=stm[:, tt8, :], func=AF.Exp, scale=1.0,
                          bias=mneg[:])
            ns = s_done()
            VE.wait_ge(s_s.h, ns)
            VE.tensor_scalar(out=sel[:], in0=stm[:, tt8, :], scalar1=mx2[:, 7:8],
                             scalar2=None, op0=ALU.is_ge)
            VE.drain()
            VE.tensor_tensor(out=expo[:], in0=expo[:], in1=sel[:], op=ALU.mult)
            VE.drain()
            VE.tensor_reduce(out=den[:], in_=expo[:], axis=AX.X, op=ALU.add)
            VE.drain()
            VE.reciprocal(out=den[:], in_=den[:])
            VE.drain()
            VE.tensor_scalar(out=gtm[:, tt8, :], in0=expo[:], scalar1=den[:],
                             scalar2=None, op0=ALU.mult)
            v_done()
        for tt8 in range(8):
            nv = v_done()
            PE.wait_ge(s_v.h, nv)
            mm = PE.transpose(out=pb[2][0:64, 0:128], in_=gtm[:, tt8, :], identity=idents[:])
            npe = pe_mark(mm)
            VE.wait_ge(s_pe.h, npe)
            VE.tensor_copy(out=gfm[:, tt8 * 128:(tt8 + 1) * 128], in_=pb[2][0:64, 0:128])
            v_done()
        for c in range(4):
            nv = v_done()
            PE.wait_ge(s_v.h, nv)
            for th in range(2):
                mm = PE.matmul(out=pbc1[:, th * 512:th * 512 + 512],
                               lhsT=expTs[:, c * 128:(c + 1) * 128],
                               rhs=gfm[:, th * 512:(th + 1) * 512], start=True, stop=True)
            pe_mark(mm)
            for i in range(32):
                gate()
                nw = ld(wst[:, 0:128], AwT.rearrange("(c p) f -> c p f", p=128)
                        [i, :, c * 128:(c + 1) * 128])
                VE.wait_ge(s_ld.h, nw)
                VE.wait_ge(s_pe.h, s_pe.n)
                VE.tensor_copy(out=wsb[:, 0:128], in_=wst[:, 0:128])
                nv = v_done()
                PE.wait_ge(s_v.h, nv)
                for th in range(2):
                    mm = PE.matmul(
                        out=pb[th][:, 0:512], lhsT=wsb[:, 0:128],
                        rhs=x2b[:, i * TOK + th * 512:i * TOK + th * 512 + 512],
                        start=(i == 0), stop=(i == 31))
                npe = pe_mark(mm)
            SE.wait_ge(s_pe.h, npe)
            for th in range(2):
                SE.activation(out=(scr if th == 0 else scr2)[:, 0:512],
                              in_=pb[th][:, 0:512], func=AF.Copy, scale=1.0)
            ns = s_done()
            VE.wait_ge(s_s.h, ns)
            VE.tensor_tensor(out=zgb[:, c * TOK:c * TOK + 512], in0=scr[:, 0:512],
                             in1=pbc1[:, 0:512], op=ALU.mult)
            VE.tensor_tensor(out=zgb[:, c * TOK + 512:(c + 1) * TOK], in0=scr2[:, 0:512],
                             in1=pbc1[:, 512:1024], op=ALU.mult)
            nv = v_done()
            PE.wait_ge(s_v.h, nv)

        for o in range(32):
            gate()
            nw = ld(wst[:, 0:512].rearrange("p (c f) -> p c f", c=4),
                    BwT.rearrange("(c p) f -> p c f", p=128)[:, :, o * 128:(o + 1) * 128])
            nx = ld(xst[:], xin.rearrange("(c p) t -> c p t", p=128)[o])
            VE.wait_ge(s_ld.h, max(nw, nx))
            if o > 0:
                VE.wait_ge(s_pe.h, s_pe.n)
            VE.tensor_copy(out=wsb[:, 0:512], in_=wst[:, 0:512])
            nv = v_done()
            PE.wait_ge(s_v.h, nv)
            for c in range(4):
                for th in range(2):
                    mm = PE.matmul(
                        out=pb[th][:, 0:512],
                        lhsT=wsb[:, c * 128:(c + 1) * 128],
                        rhs=zgb[:, c * TOK + th * 512:c * TOK + th * 512 + 512],
                        start=(c == 0), stop=(c == 3))
            npe = pe_mark(mm)
            SE.wait_ge(s_pe.h, npe)
            SE.wait_ge(s_st.h, s_st.n)
            for th in range(2):
                SE.activation(out=(scr if th == 0 else scr2)[:, 0:512],
                              in_=pb[th][:, 0:512], func=AF.Copy, scale=2.0)
            ns = s_done()
            VE.wait_ge(s_s.h, ns)
            VE.tensor_tensor(out=scr[:, 0:512], in0=scr[:, 0:512], in1=xst[:, 0:512],
                             op=ALU.add)
            VE.tensor_tensor(out=scr2[:, 0:512], in0=scr2[:, 0:512], in1=xst[:, 512:1024],
                             op=ALU.add)
            nvp = v_done()
            SY.wait_ge(s_v.h, nvp)
            SY.dma_start(out=xpre.rearrange("(c p) t -> c p t", p=128)[o, :, 0:512],
                         in_=scr[:, 0:512]).then_inc(s_st.h, 16)
            SY.dma_start(out=xpre.rearrange("(c p) t -> c p t", p=128)[o, :, 512:1024],
                         in_=scr2[:, 0:512]).then_inc(s_st.h, 16)
            s_st.inc(32)
            VE.tensor_copy(out=cpb[:], in_=scr[:, 0:512])
            nv = v_done()
            SE.wait_ge(s_v.h, nvp)
            SE.activation(out=sqb[:], in_=scr[:, 0:512], func=AF.Square, scale=1.0)
            ns = s_done()
            PE.wait_ge(s_v.h, nv)
            PE.wait_ge(s_s.h, ns)
            PE.matmul(out=pst1[0:1, :], lhsT=onescol[:], rhs=cpb[:],
                      start=(o == 0), stop=(o == 31))
            mm = PE.matmul(out=pst2[0:1, :], lhsT=onescol[:], rhs=sqb[:],
                           start=(o == 0), stop=(o == 31))
            pe_mark(mm)
            VE.wait_ge(s_pe.h, s_pe.n)
            VE.tensor_copy(out=cpb[:], in_=scr2[:, 0:512])
            nv = v_done()
            SE.wait_ge(s_pe.h, s_pe.n)
            SE.activation(out=sqb[:], in_=scr2[:, 0:512], func=AF.Square, scale=1.0)
            ns = s_done()
            PE.wait_ge(s_v.h, nv)
            PE.wait_ge(s_s.h, ns)
            PE.matmul(out=pb[2][0:1, :], lhsT=onescol[:], rhs=cpb[:],
                      start=(o == 0), stop=(o == 31))
            mm = PE.matmul(out=pb[3][0:1, :], lhsT=onescol[:], rhs=sqb[:],
                           start=(o == 0), stop=(o == 31))
            pe_mark(mm)
        npe = s_pe.n
        ln_rows_half(V, EPS_OUT, npe, 0, pst1[0:1, :], pst2[0:1, :])
        ln_rows_half(V, EPS_OUT, npe, 1, pb[2][0:1, :], pb[3][0:1, :])
        nv = v_done()
        PE.wait_ge(s_v.h, nv)
        SY.wait_ge(s_st.h, s_st.n)
        for o in range(32):
            mm = None
            for th in range(2):
                PE.matmul(out=pbc1[:, th * 512:(th + 1) * 512], lhsT=onesrow[:],
                          rhs=rowsA[0:1, th * 512:(th + 1) * 512], start=True, stop=True)
                mm = PE.matmul(out=(pb[2] if th == 0 else pb[3])[:, 0:512],
                               lhsT=onesrow[:],
                               rhs=rowsB[0:1, th * 512:(th + 1) * 512],
                               start=True, stop=True)
            npe = pe_mark(mm)
            gate()
            nx = ld(xst[:], xpre.rearrange("(c p) t -> c p t", p=128)[o])
            VE.wait_ge(s_ld.h, nx)
            VE.wait_ge(s_pe.h, npe)
            VE.wait_ge(s_st.h, s_st.n)
            VE.tensor_tensor(out=scr[:], in0=xst[:], in1=pbc1[:], op=ALU.mult)
            VE.drain()
            VE.tensor_tensor(out=scr[:, 0:512], in0=scr[:, 0:512],
                             in1=pb[2][:, 0:512], op=ALU.add)
            VE.tensor_tensor(out=scr[:, 512:1024], in0=scr[:, 512:1024],
                             in1=pb[3][:, 0:512], op=ALU.add)
            nv = v_done()
            SY.wait_ge(s_v.h, nv)
            SY.dma_start(out=xout.rearrange("(c p) t -> c p t", p=128)[o],
                         in_=scr[:]).then_inc(s_st.h, 16)
            s_st.inc(16)
            PE.wait_ge(s_v.h, nv)
        SY.wait_ge(s_st.h, s_st.n)
        g.wait_ge(s_st.h, s_st.n)

    def emit_wrapped(g):
        if env is not None:
            for eng in (PE, VE, SE, SY, GE):
                for (sh, sv) in env.get("entry_deps", []):
                    eng.wait_ge(sh, sv)
        emit(g)
        sub_sb.close()

    if env is None:
        with nc.Block() as block:
            @block.gpsimd
            def _(g):
                emit_wrapped(g)
        return nc
    return emit_wrapped


def hostprep_D(inputs, core, x2):
    b, half = core // 2, core % 2
    expm = np.zeros((64, 512), np.float32)
    for e in range(64):
        expm[e, e * 8:(e + 1) * 8] = 1.0
    im = {
        "xin": (np.zeros((V, TOK), np.float32) if x2 is None else
                np.ascontiguousarray(_f32(x2[b, half * TOK:(half + 1) * TOK, :]).T)),
        "AwT": np.ascontiguousarray(_f32(inputs["L1_ffn_A_w"]).T),
        "BwT": np.ascontiguousarray(_f32(inputs["L1_ffn_B_w"]).T),
        "WqT": np.ascontiguousarray(_f32(inputs["L1_ffn_Wq_w"]).T),
        "keysT": np.ascontiguousarray(_f32(inputs["L1_ffn_keys"]).T),
        "ident": np.eye(128, dtype=np.float32),
        "expT": expm,
        "onesrow_in": np.ones((1, TOK), np.float32),
    }
    return im


def run_stageD(inputs, x2):
    nc = build_stageD()
    ims = [hostprep_D(inputs, c, x2) for c in range(NCORES)]
    res = run_bass_kernel_spmd(nc, ims, core_ids=list(range(NCORES)))
    out = np.empty((B, L, V), np.float32)
    for c in range(NCORES):
        b, half = c // 2, c % 2
        out[b, half * TOK:(half + 1) * TOK, :] = res.results[c]["xout"].T
    return out





def build_stageC(stop=99, env=None):
    if env is None:
        ctx = contextlib.ExitStack()
        nc = bass.Bass("TRN2", target_bir_lowering=False)
        P_ = ""
    else:
        ctx = env["ctx"]
        nc = env["nc"]
        P_ = "C_"
    dt = lambda name, *a, **k: nc.dram_tensor(P_ + name, *a, **k)

    if env is None:
        xin = dt("xin", [V, TOK], F32, kind="ExternalInput").ap()
    else:
        xin = env["x2d"]
    WqT = dt("WqT", [V, V], F32, kind="ExternalInput").ap()
    WkT = dt("WkT", [V, V], F32, kind="ExternalInput").ap()
    WvT = dt("WvT", [V, V], F32, kind="ExternalInput").ap()
    WoT = dt("WoT", [V, V], F32, kind="ExternalInput").ap()
    bq_c = dt("bq_c", [128, 32], F32, kind="ExternalInput").ap()
    bk_c = dt("bk_c", [128, 32], F32, kind="ExternalInput").ap()
    bvrow = dt("bvrow", [1, V], F32, kind="ExternalInput").ap()
    bo_c = dt("bo_c", [128, 32], F32, kind="ExternalInput").ap()
    gqrow_in = dt("gqrow_in", [1, 128], F32, kind="ExternalInput").ap()
    gkrow_in = dt("gkrow_in", [1, 128], F32, kind="ExternalInput").ap()
    ident = dt("ident", [128, 128], F32, kind="ExternalInput").ap()
    onesrow_in = dt("onesrow_in", [1, TOK], F32, kind="ExternalInput").ap()
    ep2 = dt("ep2", [128, 512], I32, kind="ExternalInput").ap()
    if env is None:
        xout = dt("xout", [V, TOK], F32, kind="ExternalOutput").ap()
    else:
        xout = env["x3d"]
    xpre = dt("xpre", [V, TOK], F32).ap()
    q_dram = dt("q_dram", [V, TOK], BF16).ap()
    k_dram = dt("k_dram", [V, TOK], BF16).ap()
    v_dram = dt("v_dram", [TOK, V], BF16).ap()
    attn_dram = dt("attn_dram", [V, TOK], BF16).ap()
    k2_dram = dt("k2_dram", [V, TOK], BF16).ap()
    v2_dram = dt("v2_dram", [TOK, V], BF16).ap()
    ksh = dt("ksh", [2, V, TOK], BF16, addr_space="Shared").ap()
    vsh = dt("vsh", [2, TOK, V], BF16, addr_space="Shared").ap()
    kfl = dt("kfl", [2, 128, 512], I32, addr_space="Shared").ap()

    sub_sb = contextlib.ExitStack()
    sb = lambda name, shape, dtp=F32: sub_sb.enter_context(nc.sbuf_tensor(P_ + name, shape, dtp))

    xst = sb("xst", [128, TOK])
    scr = sb("scr", [128, TOK])
    scr2 = sb("scr2", [128, TOK])
    sqb = sb("sqb", [128, 512], BF16)
    cpb = sb("cpb", [128, 512], BF16)
    onescol = sb("onescol", [128, 1], BF16)
    onescolf = sb("onescolf", [128, 1])
    onesrow = sb("onesrow", [1, 128])
    gqrow = sb("gqrow", [1, 128])
    gkrow = sb("gkrow", [1, 128])
    idents = sb("idents", [128, 128])
    bqcol = sb("bqcol", [128, 32])
    bkcol = sb("bkcol", [128, 32])
    bocol = sb("bocol", [128, 32])
    bvb = sb("bvb", [128, 512])
    bvs = sb("bvs", [1, V])
    ep2s = sb("ep2s", [128, 512], I32)
    flsb = sb("flsb", [1, 8], I32)
    rowsA = sb("rowsA", [1, TOK])
    rowsB = sb("rowsB", [2, TOK])
    srow = sb("srow", [1, 512])
    srow2 = sb("srow2", [1, 512])
    rsrow = sb("rsrow", [1, TOK])
    negB = sb("negB", [128, 1])
    stat16 = sb("stat16", [128, 16])
    col16 = sb("col16", [16, 1])
    bmax = sb("bmax", [1, 16])
    kb = sb("kb", [128, TOK], BF16)
    wst = sb("wst", [128, 4096])
    wsb = sb("wsb", [128, 4096], BF16)

    if env is None:
        psz = lambda name, shape: ctx.enter_context(nc.psum_tensor(name, shape, F32))
        pA = psz("pA", [128, 512])
        pB = psz("pB", [128, 512])
        pC = psz("pC", [128, 512])
        pD = psz("pD", [128, 512])
        pE1 = psz("pE1", [1, 512])
        pE2 = psz("pE2", [1, 512])
        pF = psz("pF", [128, TOK])
    else:
        pA, pB, pC, pD = env["pb"]
        pE1, pE2, pF = env["pst1"], env["pst2"], env["pbc1"]

    s_ld = Sem(ctx, nc, P_ + "s_ld")
    s_pe = Sem(ctx, nc, P_ + "s_pe")
    s_v = Sem(ctx, nc, P_ + "s_v")
    s_s = Sem(ctx, nc, P_ + "s_s")
    s_st = Sem(ctx, nc, P_ + "s_st")
    s_g = Sem(ctx, nc, P_ + "s_g")
    s_bar = Sem(ctx, nc, P_ + "s_bar")
    s_rt = Sem(ctx, nc, P_ + "s_rt")

    PE, VE, SE, GE, SY = nc.tensor, nc.vector, nc.scalar, nc.gpsimd, nc.sync

    def v_done():
        VE.drain().then_inc(s_v.h, 1)
        return s_v.inc()

    def s_done():
        SE.drain().then_inc(s_s.h, 1)
        return s_s.inc()

    def pe_mark(inst):
        inst.then_inc(s_pe.h, 1)
        return s_pe.inc()

    def ld(dst, src):
        SY.dma_start(out=dst, in_=src).then_inc(s_ld.h, 16)
        return s_ld.inc(16)

    def gate():
        SY.wait_ge(s_v.h, s_v.n)
        SY.wait_ge(s_s.h, s_s.n)
        SY.wait_ge(s_pe.h, s_pe.n)

    def ln_rows_half(nfeat, eps, npe, th, src1, src2):
        sA = rowsA[0:1, th * 512:(th + 1) * 512]
        sB = rowsB[0:1, th * 512:(th + 1) * 512]
        sm = srow[0:1, :]
        sm2 = srow2[0:1, :]
        VE.wait_ge(s_pe.h, npe)
        VE.tensor_scalar(out=sm, in0=src1, scalar1=1.0 / nfeat, scalar2=None,
                         op0=ALU.mult)
        VE.tensor_scalar(out=sA, in0=src2, scalar1=1.0 / nfeat, scalar2=None,
                         op0=ALU.mult)
        VE.drain()
        VE.tensor_tensor(out=sm2, in0=sm, in1=sm, op=ALU.mult)
        VE.drain()
        VE.tensor_tensor(out=sA, in0=sA, in1=sm2, op=ALU.subtract)
        VE.drain()
        VE.tensor_scalar(out=sA, in0=sA, scalar1=float(eps), scalar2=None,
                         op0=ALU.add)
        nv = v_done()
        SE.wait_ge(s_v.h, nv)
        SE.activation(out=sA, in_=sA, func=AF.Sqrt, scale=1.0)
        ns = s_done()
        VE.wait_ge(s_s.h, ns)
        VE.reciprocal(out=sA, in_=sA)
        VE.drain()
        VE.tensor_tensor(out=sm2, in0=sm, in1=sA, op=ALU.mult)
        VE.drain()
        VE.tensor_scalar(out=sB, in0=sm2, scalar1=-1.0, scalar2=None, op0=ALU.mult)
        return v_done()

    HD = 128
    NH = 32
    SCALE = float(1.0 / np.sqrt(HD))
    EPS_R = 1e-6

    def emit(g):
        n0 = ld(onesrow[:], onesrow_in[0:1, 0:128])
        n0 = ld(rowsB[1:2, :], onesrow_in)
        n0 = ld(gqrow[:], gqrow_in)
        n0 = ld(gkrow[:], gkrow_in)
        n0 = ld(idents[:], ident)
        n0 = ld(bqcol[:], bq_c)
        n0 = ld(bkcol[:], bk_c)
        n0 = ld(bocol[:], bo_c)
        n0 = ld(ep2s[:], ep2)
        n0 = ld(bvs[:], bvrow)
        VE.wait_ge(s_ld.h, n0)
        VE.memset(onescolf[:], 1.0)
        VE.drain()
        VE.tensor_copy(out=onescol[:], in_=onescolf[:])
        v_done()

        sub = contextlib.ExitStack()
        x2b = sub.enter_context(nc.sbuf_tensor("x2b", [128, 32 * TOK], BF16))
        for i in range(32):
            gate()
            nx = ld(xst[:], xin.rearrange("(c p) t -> c p t", p=128)[i])
            VE.wait_ge(s_ld.h, nx)
            VE.tensor_copy(out=x2b[:, i * TOK:(i + 1) * TOK], in_=xst[:])
            v_done()

        def proj_qk(WT, bcol, grow, out_dram, eps):
            for h in range(NH):
                gate()
                nw = ld(wst[:, 0:4096].rearrange("p (c f) -> p c f", c=32),
                        WT.rearrange("(c p) f -> p c f", p=128)[:, :, h * 128:(h + 1) * 128])
                VE.wait_ge(s_ld.h, nw)
                VE.wait_ge(s_pe.h, s_pe.n)
                VE.wait_ge(s_s.h, s_s.n)
                VE.tensor_copy(out=wsb[:, 0:4096], in_=wst[:, 0:4096])
                nv = v_done()
                PE.wait_ge(s_v.h, nv)
                for c in range(32):
                    for th in range(2):
                        mm = PE.matmul(
                            out=(pA if th == 0 else pB)[:, 0:512],
                            lhsT=wsb[:, c * 128:(c + 1) * 128],
                            rhs=x2b[:, c * TOK + th * 512:c * TOK + th * 512 + 512],
                            start=(c == 0), stop=(c == 31))
                npe = pe_mark(mm)
                SE.wait_ge(s_pe.h, npe)
                for th in range(2):
                    SE.activation(out=sqb[:], in_=(pA if th == 0 else pB)[:, 0:512],
                                  func=AF.Square, scale=1.0, bias=bcol[:, h:h + 1])
                    ns = s_done()
                    PE.wait_ge(s_s.h, ns)
                    mm2 = PE.matmul(out=(pE1 if th == 0 else pE2)[0:1, :],
                                    lhsT=onescol[:], rhs=sqb[:], start=True, stop=True)
                    pe_mark(mm2)
                npe2 = s_pe.n
                VE.wait_ge(s_pe.h, npe2)
                for th in range(2):
                    sA = rowsA[0:1, th * 512:(th + 1) * 512]
                    VE.tensor_scalar(out=sA, in0=(pE1 if th == 0 else pE2)[0:1, :],
                                     scalar1=1.0 / HD, scalar2=float(eps),
                                     op0=ALU.mult, op1=ALU.add)
                nv = v_done()
                SE.wait_ge(s_v.h, nv)
                SE.activation(out=rowsA[:], in_=rowsA[:], func=AF.Sqrt, scale=1.0)
                ns = s_done()
                VE.wait_ge(s_s.h, ns)
                VE.reciprocal(out=rowsA[:], in_=rowsA[:])
                nv = v_done()
                PE.wait_ge(s_v.h, nv)
                for th in range(2):
                    mm3 = PE.matmul(out=pF[:, th * 512:(th + 1) * 512], lhsT=grow[:],
                                    rhs=rowsA[0:1, th * 512:(th + 1) * 512],
                                    start=True, stop=True)
                npe3 = pe_mark(mm3)
                SE.wait_ge(s_pe.h, npe3)
                for th in range(2):
                    SE.activation(out=(scr if th == 0 else scr2)[:, 0:512],
                                  in_=(pA if th == 0 else pB)[:, 0:512],
                                  func=AF.Identity, scale=1.0, bias=bcol[:, h:h + 1])
                ns = s_done()
                VE.wait_ge(s_s.h, ns)
                VE.tensor_tensor(out=kb[:, 0:512], in0=scr[:, 0:512], in1=pF[:, 0:512],
                                 op=ALU.mult)
                VE.tensor_tensor(out=kb[:, 512:1024], in0=scr2[:, 0:512],
                                 in1=pF[:, 512:1024], op=ALU.mult)
                nv = v_done()
                SY.wait_ge(s_v.h, nv)
                SY.dma_start(out=out_dram.rearrange("(c p) t -> c p t", p=128)[h],
                             in_=kb[:]).then_inc(s_st.h, 16)
                s_st.inc(32) if False else s_st.inc(16)
                SE.wait_ge(s_st.h, s_st.n)
                VE.wait_ge(s_st.h, s_st.n)
                PE.wait_ge(s_v.h, nv)

        proj_qk(WkT, bkcol, gkrow, k_dram, EPS_R)

        for hdg in range(8):
            mm = PE.matmul(out=pF[:, 0:512], lhsT=onesrow[:],
                           rhs=bvs[0:1, hdg * 512:(hdg + 1) * 512], start=True, stop=True)
            npe = pe_mark(mm)
            VE.wait_ge(s_pe.h, npe)
            VE.tensor_copy(out=bvb[:], in_=pF[:, 0:512])
            nv = v_done()
            for half4 in range(2):
                accs = [pA, pB, pC, pD]
                for i in range(32):
                    gate()
                    nw = ld(wst[:, 0:512], WvT.rearrange("(c p) f -> c p f", p=128)
                            [i, :, hdg * 512:(hdg + 1) * 512])
                    VE.wait_ge(s_ld.h, nw)
                    VE.wait_ge(s_pe.h, s_pe.n)
                    VE.tensor_copy(out=wsb[:, 0:512], in_=wst[:, 0:512])
                    nv = v_done()
                    PE.wait_ge(s_v.h, nv)
                    for kc4 in range(4):
                        kc = half4 * 4 + kc4
                        mm = PE.matmul(out=accs[kc4][:, 0:512],
                                       lhsT=x2b[:, i * TOK + kc * 128:i * TOK + (kc + 1) * 128],
                                       rhs=wsb[:, 0:512],
                                       start=(i == 0), stop=(i == 31))
                    npe = pe_mark(mm)
                VE.wait_ge(s_pe.h, npe)
                VE.wait_ge(s_st.h, s_st.n)
                for kc4 in range(4):
                    kc = half4 * 4 + kc4
                    VE.tensor_tensor(out=kb[:, 0:512].bitcast(BF16) if False else vbk[:, kc4 * 512:(kc4 + 1) * 512],
                                     in0=accs[kc4][:, 0:512], in1=bvb[:], op=ALU.add)
                nv = v_done()
                SY.wait_ge(s_v.h, nv)
                for kc4 in range(4):
                    kc = half4 * 4 + kc4
                    SY.dma_start(out=v_dram[kc * 128:(kc + 1) * 128,
                                            hdg * 512:(hdg + 1) * 512],
                                 in_=vbk[:, kc4 * 512:(kc4 + 1) * 512]).then_inc(s_st.h, 16)
                    s_st.inc(16)
                VE.wait_ge(s_st.h, s_st.n)
                PE.wait_ge(s_st.h, s_st.n)
        if stop <= 0:
            SY.wait_ge(s_st.h, s_st.n)
            g.wait_ge(s_st.h, s_st.n)
            return

        SY.wait_ge(s_st.h, s_st.n)
        g.wait_ge(s_st.h, s_st.n)
        pid = nc.partition_id()
        par = pid % 2
        sib = (pid + 1) % 2
        gb = s_g.n
        for part in range(4):
            g.dma_start(out=ksh[par, part * 1024:(part + 1) * 1024, :],
                        in_=k_dram[part * 1024:(part + 1) * 1024, :]).then_inc(s_g.h, 16)
        for part in range(4):
            g.dma_start(out=vsh[par, part * 256:(part + 1) * 256, :],
                        in_=v_dram[part * 256:(part + 1) * 256, :]).then_inc(s_g.h, 16)
        g.wait_ge(s_g.h, s_g.inc(128))
        g.dma_start(out=kfl[par], in_=ep2s[:]).then_inc(s_g.h, 16)
        g.wait_ge(s_g.h, s_g.inc(16))
        if stop <= 1:
            SY.wait_ge(s_st.h, s_st.n)
            return

        ns_now = s_s.n
        g.wait_ge(s_s.h, ns_now)
        bad = g.alloc_register("bad")
        fv = g.alloc_register("fv")
        f2 = g.alloc_register("f2")
        expv = g.alloc_register("expv")
        dlyr = g.alloc_register("dlyr")
        g.load(expv, ep2s[0:1, 0:1])
        g.reg_mov(dlyr, 250000)
        with g.While(dlyr):
            g.reg_sub(dlyr, dlyr, 1)
        g.dma_start(out=flsb[0:1, 0:1], in_=kfl[0, 0:1, 0:1]).then_inc(s_g.h, 16)
        g.dma_start(out=flsb[0:1, 1:2], in_=kfl[1, 0:1, 0:1]).then_inc(s_g.h, 16)
        g.wait_ge(s_g.h, s_g.inc(32))
        g.load(fv, flsb[0:1, 0:1])
        g.load(f2, flsb[0:1, 1:2])
        g.reg_alu(fv, fv, expv, op=ALU.bitwise_xor)
        g.reg_alu(f2, f2, expv, op=ALU.bitwise_xor)
        g.reg_alu(bad, fv, f2, op=ALU.bitwise_or)
        g.reg_alu(bad, bad, 0, op=ALU.not_equal)
        for retry in range(3):
            guard = g.If_eq(bad, 1)
            with guard:
                g.reg_mov(dlyr, 200000)
                with g.While(dlyr):
                    g.reg_sub(dlyr, dlyr, 1)
                g.dma_start(out=flsb[0:1, 0:1], in_=kfl[0, 0:1, 0:1]).then_inc(s_rt.h, 16)
                g.dma_start(out=flsb[0:1, 1:2], in_=kfl[1, 0:1, 0:1]).then_inc(s_rt.h, 16)
                g.drain()
                g.load(fv, flsb[0:1, 0:1])
                g.load(f2, flsb[0:1, 1:2])
                g.reg_alu(fv, fv, expv, op=ALU.bitwise_xor)
                g.reg_alu(f2, f2, expv, op=ALU.bitwise_xor)
                g.reg_alu(bad, fv, f2, op=ALU.bitwise_or)
                g.reg_alu(bad, bad, 0, op=ALU.not_equal)
        for part in range(4):
            g.dma_start(out=k2_dram[part * 1024:(part + 1) * 1024, :],
                        in_=ksh[sib, part * 1024:(part + 1) * 1024, :]).then_inc(s_g.h, 16)
        for part in range(4):
            g.dma_start(out=v2_dram[part * 256:(part + 1) * 256, :],
                        in_=vsh[sib, part * 256:(part + 1) * 256, :]).then_inc(s_g.h, 16)
        g.wait_ge(s_g.h, s_g.inc(128))
        guard = g.If_eq(bad, 1)
        with guard:
            g.memset(scr[:], 1.0e30)
        g.sem_inc(s_bar.h, 1)
        nbar = s_bar.inc()

        proj_qk(WqT, bqcol, gqrow, q_dram, EPS_R)
        sub.close()

        if stop <= 2:
            SY.wait_ge(s_bar.h, nbar)
            SY.wait_ge(s_st.h, s_st.n)
            g.wait_ge(s_bar.h, nbar)
            return
        SY.wait_ge(s_bar.h, nbar)
        sub2 = contextlib.ExitStack()
        scsp = sub2.enter_context(nc.sbuf_tensor("scsp", [128, 16 * TOK], F32))
        ptb = sub2.enter_context(nc.sbuf_tensor("ptb", [128, 16 * TOK], BF16))
        qh = sub2.enter_context(nc.sbuf_tensor("qh", [128, TOK], BF16))
        kh = sub2.enter_context(nc.sbuf_tensor("kh", [128, TOK], BF16))
        kh2 = sub2.enter_context(nc.sbuf_tensor("kh2", [128, TOK], BF16))
        vt = sub2.enter_context(nc.sbuf_tensor("vt", [128, 128], BF16))
        ab = sub2.enter_context(nc.sbuf_tensor("ab", [128, TOK], BF16))

        for h in range(NH):
            nq = ld(qh[:], q_dram.rearrange("(c p) t -> c p t", p=128)[h])
            nk = ld(kh[:], k_dram.rearrange("(c p) t -> c p t", p=128)[h])
            nk2 = ld(kh2[:], k2_dram.rearrange("(c p) t -> c p t", p=128)[h])
            PE.wait_ge(s_ld.h, max(nq, nk, nk2))
            if h > 0:
                PE.wait_ge(s_s.h, s_s.n)
            for kc in range(16):
                src = kh if kc < 8 else kh2
                kcl = kc % 8
                for th in range(2):
                    mm = PE.matmul(out=(pA if th == 0 else pB)[:, 0:512],
                                   lhsT=src[:, kcl * 128:(kcl + 1) * 128],
                                   rhs=qh[:, th * 512:(th + 1) * 512],
                                   start=True, stop=True)
                npe = pe_mark(mm)
                SE.wait_ge(s_pe.h, npe)
                for th in range(2):
                    SE.activation(out=scsp[:, kc * TOK + th * 512:kc * TOK + th * 512 + 512],
                                  in_=(pA if th == 0 else pB)[:, 0:512],
                                  func=AF.Copy, scale=SCALE)
                ns = s_done()
                VE.wait_ge(s_s.h, ns)
                VE.reduce_max(out=stat16[:, kc:kc + 1], in_=scsp[:, kc * TOK:(kc + 1) * TOK],
                              axis=AX.X)
                nv = v_done()
                PE.wait_ge(s_v.h, nv)
            nv = v_done()
            PE.wait_ge(s_v.h, nv)
            mm = PE.transpose(out=pC[0:16, 0:128], in_=stat16[:], identity=idents[:])
            npe = pe_mark(mm)
            VE.wait_ge(s_pe.h, npe)
            VE.reduce_max(out=col16[:], in_=pC[0:16, 0:128], axis=AX.X)
            nv = v_done()
            PE.wait_ge(s_v.h, nv)
            mm = PE.transpose(out=pD[0:1, 0:16], in_=col16[:], identity=idents[0:16, 0:16])
            npe = pe_mark(mm)
            VE.wait_ge(s_pe.h, npe)
            VE.reduce_max(out=bmax[0:1, 0:1], in_=pD[0:1, 0:16], axis=AX.X)
            nv = v_done()
            PE.wait_ge(s_v.h, nv)
            mm = PE.matmul(out=pD[:, 0:1], lhsT=onesrow[:], rhs=bmax[0:1, 0:1],
                           start=True, stop=True)
            npe = pe_mark(mm)
            VE.wait_ge(s_pe.h, npe)
            VE.tensor_scalar(out=negB[:], in0=pD[:, 0:1], scalar1=-1.0, scalar2=None,
                             op0=ALU.mult)
            nv = v_done()
            SE.wait_ge(s_v.h, nv)
            for kc in range(16):
                SE.activation(out=ptb[:, kc * TOK:(kc + 1) * TOK],
                              in_=scsp[:, kc * TOK:(kc + 1) * TOK], func=AF.Exp,
                              scale=1.0, bias=negB[:])
            ns = s_done()
            PE.wait_ge(s_s.h, ns)
            for kc in range(16):
                if kc < 8:
                    nv2 = ld(vt[:], v_dram[kc * 128:(kc + 1) * 128, h * 128:(h + 1) * 128])
                else:
                    nv2 = ld(vt[:], v2_dram[(kc - 8) * 128:(kc - 7) * 128,
                                            h * 128:(h + 1) * 128])
                PE.wait_ge(s_ld.h, nv2)
                for th in range(2):
                    PE.matmul(out=(pC if th == 0 else pD)[:, 0:512], lhsT=vt[:],
                              rhs=ptb[:, kc * TOK + th * 512:kc * TOK + th * 512 + 512],
                              start=(kc == 0), stop=(kc == 15))
                    mm = PE.matmul(out=(pE1 if th == 0 else pE2)[0:1, :], lhsT=onescol[:],
                                   rhs=ptb[:, kc * TOK + th * 512:kc * TOK + th * 512 + 512],
                                   start=(kc == 0), stop=(kc == 15))
                npe = pe_mark(mm)
                gate()
                SY.wait_ge(s_pe.h, s_pe.n)
            VE.wait_ge(s_pe.h, npe)
            VE.tensor_copy(out=rsrow[0:1, 0:512], in_=pE1[0:1, :])
            VE.tensor_copy(out=rsrow[0:1, 512:1024], in_=pE2[0:1, :])
            nv = v_done()
            SE.wait_ge(s_v.h, nv)
            SE.activation(out=rsrow[:], in_=rsrow[:], func=AF.Ln, scale=1.0)
            SE.drain()
            SE.activation(out=rsrow[:], in_=rsrow[:], func=AF.Exp, scale=-1.0)
            ns = s_done()
            PE.wait_ge(s_s.h, ns)
            for th in range(2):
                mm = PE.matmul(out=pF[:, th * 512:(th + 1) * 512], lhsT=onesrow[:],
                               rhs=rsrow[0:1, th * 512:(th + 1) * 512], start=True, stop=True)
            npe = pe_mark(mm)
            SE.wait_ge(s_pe.h, npe)
            SE.wait_ge(s_st.h, s_st.n)
            for th in range(2):
                SE.activation(out=(scr if th == 0 else scr2)[:, 0:512],
                              in_=(pC if th == 0 else pD)[:, 0:512], func=AF.Copy, scale=1.0)
            ns = s_done()
            VE.wait_ge(s_s.h, ns)
            VE.tensor_tensor(out=ab[:, 0:512], in0=scr[:, 0:512], in1=pF[:, 0:512],
                             op=ALU.mult)
            VE.tensor_tensor(out=ab[:, 512:1024], in0=scr2[:, 0:512], in1=pF[:, 512:1024],
                             op=ALU.mult)
            nv = v_done()
            SY.wait_ge(s_v.h, nv)
            SY.dma_start(out=attn_dram.rearrange("(c p) t -> c p t", p=128)[h],
                         in_=ab[:]).then_inc(s_st.h, 16)
            s_st.inc(16)
            VE.wait_ge(s_st.h, s_st.n)
            PE.wait_ge(s_v.h, nv)
        sub2.close()
        if stop <= 3:
            SY.wait_ge(s_st.h, s_st.n)
            g.wait_ge(s_st.h, s_st.n)
            return

        sub3 = contextlib.ExitStack()
        atr = sub3.enter_context(nc.sbuf_tensor("atr", [128, 32 * TOK], BF16))
        SY.wait_ge(s_st.h, s_st.n)
        for i in range(32):
            ld(atr[:, i * TOK:(i + 1) * TOK],
               attn_dram.rearrange("(c p) t -> c p t", p=128)[i])
        for o in range(32):
            gate()
            nw = ld(wst[:, 0:4096].rearrange("p (c f) -> p c f", c=32),
                    WoT.rearrange("(c p) f -> p c f", p=128)[:, :, o * 128:(o + 1) * 128])
            nx = ld(xst[:], xin.rearrange("(c p) t -> c p t", p=128)[o])
            VE.wait_ge(s_ld.h, max(nw, nx))
            if o > 0:
                VE.wait_ge(s_pe.h, s_pe.n)
            VE.tensor_copy(out=wsb[:, 0:4096], in_=wst[:, 0:4096])
            nv = v_done()
            PE.wait_ge(s_v.h, nv)
            for c in range(32):
                for th in range(2):
                    mm = PE.matmul(
                        out=(pA if th == 0 else pB)[:, 0:512],
                        lhsT=wsb[:, c * 128:(c + 1) * 128],
                        rhs=atr[:, c * TOK + th * 512:c * TOK + th * 512 + 512],
                        start=(c == 0), stop=(c == 31))
            npe = pe_mark(mm)
            SE.wait_ge(s_pe.h, npe)
            SE.wait_ge(s_st.h, s_st.n)
            for th in range(2):
                SE.activation(out=(scr if th == 0 else scr2)[:, 0:512],
                              in_=(pA if th == 0 else pB)[:, 0:512], func=AF.Identity,
                              bias=bocol[:, o:o + 1], scale=1.0)
            ns = s_done()
            VE.wait_ge(s_s.h, ns)
            VE.tensor_tensor(out=scr[:, 0:512], in0=scr[:, 0:512], in1=xst[:, 0:512],
                             op=ALU.add)
            VE.tensor_tensor(out=scr2[:, 0:512], in0=scr2[:, 0:512], in1=xst[:, 512:1024],
                             op=ALU.add)
            nvp = v_done()
            SY.wait_ge(s_v.h, nvp)
            SY.dma_start(out=xpre.rearrange("(c p) t -> c p t", p=128)[o, :, 0:512],
                         in_=scr[:, 0:512]).then_inc(s_st.h, 16)
            SY.dma_start(out=xpre.rearrange("(c p) t -> c p t", p=128)[o, :, 512:1024],
                         in_=scr2[:, 0:512]).then_inc(s_st.h, 16)
            s_st.inc(32)
            VE.tensor_copy(out=cpb[:], in_=scr[:, 0:512])
            nv = v_done()
            SE.wait_ge(s_v.h, nvp)
            SE.activation(out=sqb[:], in_=scr[:, 0:512], func=AF.Square, scale=1.0)
            ns = s_done()
            PE.wait_ge(s_v.h, nv)
            PE.wait_ge(s_s.h, ns)
            PE.matmul(out=pE1[0:1, :], lhsT=onescol[:], rhs=cpb[:],
                      start=(o == 0), stop=(o == 31))
            mm = PE.matmul(out=pE2[0:1, :], lhsT=onescol[:], rhs=sqb[:],
                           start=(o == 0), stop=(o == 31))
            pe_mark(mm)
            VE.wait_ge(s_pe.h, s_pe.n)
            VE.tensor_copy(out=cpb[:], in_=scr2[:, 0:512])
            nv = v_done()
            SE.wait_ge(s_pe.h, s_pe.n)
            SE.activation(out=sqb[:], in_=scr2[:, 0:512], func=AF.Square, scale=1.0)
            ns = s_done()
            PE.wait_ge(s_v.h, nv)
            PE.wait_ge(s_s.h, ns)
            PE.matmul(out=pC[0:1, :], lhsT=onescol[:], rhs=cpb[:],
                      start=(o == 0), stop=(o == 31))
            mm = PE.matmul(out=pD[0:1, :], lhsT=onescol[:], rhs=sqb[:],
                           start=(o == 0), stop=(o == 31))
            pe_mark(mm)
        npe = s_pe.n
        ln_rows_half(V, EPS_OUT, npe, 0, pE1[0:1, :], pE2[0:1, :])
        ln_rows_half(V, EPS_OUT, npe, 1, pC[0:1, :], pD[0:1, :])
        nv = v_done()
        PE.wait_ge(s_v.h, nv)
        SY.wait_ge(s_st.h, s_st.n)
        for o in range(32):
            mm = None
            for th in range(2):
                PE.matmul(out=pF[:, th * 512:(th + 1) * 512], lhsT=onesrow[:],
                          rhs=rowsA[0:1, th * 512:(th + 1) * 512], start=True, stop=True)
                mm = PE.matmul(out=(pC if th == 0 else pD)[:, 0:512],
                               lhsT=onesrow[:],
                               rhs=rowsB[0:1, th * 512:(th + 1) * 512],
                               start=True, stop=True)
            npe = pe_mark(mm)
            gate()
            nx = ld(xst[:], xpre.rearrange("(c p) t -> c p t", p=128)[o])
            VE.wait_ge(s_ld.h, nx)
            VE.wait_ge(s_pe.h, npe)
            VE.wait_ge(s_st.h, s_st.n)
            VE.tensor_tensor(out=scr[:], in0=xst[:], in1=pF[:], op=ALU.mult)
            VE.drain()
            VE.tensor_tensor(out=scr[:, 0:512], in0=scr[:, 0:512],
                             in1=pC[:, 0:512], op=ALU.add)
            VE.tensor_tensor(out=scr[:, 512:1024], in0=scr[:, 512:1024],
                             in1=pD[:, 0:512], op=ALU.add)
            nv = v_done()
            SY.wait_ge(s_v.h, nv)
            SY.dma_start(out=xout.rearrange("(c p) t -> c p t", p=128)[o],
                         in_=scr[:]).then_inc(s_st.h, 16)
            s_st.inc(16)
            PE.wait_ge(s_v.h, nv)
        SY.wait_ge(s_st.h, s_st.n)
        g.wait_ge(s_st.h, s_st.n)
        sub3.close()

    vbk = sub_sb.enter_context(nc.sbuf_tensor(P_ + "vbk", [128, 2048], BF16))
    kvstg = sub_sb.enter_context(nc.sbuf_tensor(P_ + "kvstg", [128, 8192], BF16))

    def emit_wrapped(g):
        if env is not None:
            for eng in (PE, VE, SE, SY, GE):
                for (sh, sv) in env.get("entry_deps", []):
                    eng.wait_ge(sh, sv)
        emit(g)
        sub_sb.close()
        if env is not None:
            env["entry_deps"] = [(s_v.h, s_v.n), (s_s.h, s_s.n), (s_pe.h, s_pe.n),
                                 (s_st.h, s_st.n), (s_ld.h, s_ld.n), (s_g.h, s_g.n)]

    if env is None:
        with nc.Block() as block:
            @block.gpsimd
            def _(g):
                emit_wrapped(g)
        return nc
    return emit_wrapped


def hostprep_C(inputs, core, x2, ep):
    b, half = core // 2, core % 2
    im = {
        "xin": (np.zeros((V, TOK), np.float32) if x2 is None else
                np.ascontiguousarray(_f32(x2[b, half * TOK:(half + 1) * TOK, :]).T)),
        "WqT": _f32(inputs["L1_attn_Wq"]),
        "WkT": _f32(inputs["L1_attn_Wk"]),
        "WvT": _f32(inputs["L1_attn_Wv"]),
        "WoT": _f32(inputs["L1_attn_Wo"]),
        "bq_c": colpack(inputs["L1_attn_bq"], 32),
        "bk_c": colpack(inputs["L1_attn_bk"], 32),
        "bvrow": _f32(inputs["L1_attn_bv"]).reshape(1, V),
        "bo_c": colpack(inputs["L1_attn_bo"], 32),
        "gqrow_in": _f32(inputs["L1_attn_gq"]).reshape(1, 128),
        "gkrow_in": _f32(inputs["L1_attn_gk"]).reshape(1, 128),
        "ident": np.eye(128, dtype=np.float32),
        "onesrow_in": np.ones((1, TOK), np.float32),
        "ep2": ep,
    }
    return im


def make_epoch():
    return np.full((128, 512), np.random.randint(1, 2**31 - 1, dtype=np.int64),
                   dtype=np.int32)


def run_stageC(inputs, x2):
    nc = build_stageC()
    ep = make_epoch()
    ims = [hostprep_C(inputs, c, x2, ep) for c in range(NCORES)]
    res = run_bass_kernel_spmd(nc, ims, core_ids=list(range(NCORES)))
    out = np.empty((B, L, V), np.float32)
    for c in range(NCORES):
        b, half = c // 2, c % 2
        out[b, half * TOK:(half + 1) * TOK, :] = res.results[c]["xout"].T
    return out





def build_fused():
    ctx = contextlib.ExitStack()
    nc = bass.Bass("TRN2", target_bir_lowering=False)
    env = {"nc": nc, "ctx": ctx}
    env["xin_full"] = nc.dram_tensor("xin_full", [V, L], F32, kind="ExternalInput").ap()
    env["x1d"] = nc.dram_tensor("x1d", [V, L], F32).ap()
    env["x2d"] = nc.dram_tensor("x2d", [V, TOK], F32).ap()
    env["x3d"] = nc.dram_tensor("x3d", [V, TOK], F32).ap()
    psz = lambda name, shape: ctx.enter_context(nc.psum_tensor(name, shape, F32))
    env["pb"] = [psz(f"pb{i}", [128, 512]) for i in range(4)]
    env["pbc1"] = psz("pbc1", [128, TOK])
    env["pst1"] = psz("pst1", [1, 512])
    env["pst2"] = psz("pst2", [1, 512])
    env["entry_deps"] = []

    with nc.Block() as block:
        @block.gpsimd
        def _(g):
            build_stageA(env=env)(g)
            build_stageB(env=env)(g)
            build_stageC(env=env)(g)
            build_stageD(env=env)(g)

    return nc


def hostprep_fused(inputs, core, ep):
    im = {}
    a = hostprep_A(inputs, core)
    xin = a.pop("xin")
    im["xin_full"] = xin
    for k, v in a.items():
        im["A_" + k] = v
    b = hostprep_B(inputs, core, None)
    b.pop("xin")
    for k, v in b.items():
        im["B_" + k] = v
    c = hostprep_C(inputs, core, None, ep)
    c.pop("xin")
    for k, v in c.items():
        im["C_" + k] = v
    d = hostprep_D(inputs, core, None)
    d.pop("xin")
    for k, v in d.items():
        im["D_" + k] = v
    return im


def run_fused(inputs):
    nc = build_fused()
    ep = make_epoch()
    ims = [hostprep_fused(inputs, c, ep) for c in range(NCORES)]
    res = run_bass_kernel_spmd(nc, ims, core_ids=list(range(NCORES)))
    out = np.empty((B, L, V), np.float32)
    for c in range(NCORES):
        b, half = c // 2, c % 2
        out[b, half * TOK:(half + 1) * TOK, :] = res.results[c]["D_xout"].T
    return out


def kernel(**inputs):
    return run_fused(inputs)




# version 06
# speedup vs baseline: 5.2944x; 5.2944x over previous
import contextlib
import numpy as np
import ml_dtypes
import concourse.bass as bass
import concourse.mybir as mybir
from concourse.bass_utils import run_bass_kernel_spmd

F32 = mybir.dt.float32
BF16 = mybir.dt.bfloat16
I32 = mybir.dt.int32
U8 = mybir.dt.uint8
AF = mybir.ActivationFunctionType
ALU = mybir.AluOpType
AX = mybir.AxisListType

B, L, V = 4, 2048, 4096
TOK = 1024
NCORES = 8
DEPTH_A, DIM_A, DFF_A = 4, 512, 2048
EPS_A = 1e-5
EPS_OUT = 1e-5


class Sem:
    def __init__(self, ctx, nc, name):
        self.h = ctx.enter_context(nc.semaphore(name))
        self.n = 0

    def inc(self, k=1):
        self.n += k
        return self.n


def _f32(x):
    return np.ascontiguousarray(np.asarray(x), dtype=np.float32)


def colpack(vec, nchunks):
    v = _f32(vec).reshape(nchunks, 128)
    return np.ascontiguousarray(v.T)


def arm_iter(nc, engines):
    for arm in nc.switch_core_id(engines=engines, n=128):
        try:
            lg = arm.logical
            if lg >= NCORES:
                continue
        except Exception:
            continue
        yield lg, lg % 2



def build_stageA(env=None):
    if env is None:
        ctx = contextlib.ExitStack()
        nc = bass.Bass("TRN2", target_bir_lowering=False)
        P_ = ""
    else:
        ctx = env["ctx"]
        nc = env["nc"]
        P_ = "A_"
    dt = lambda name, *a, **k: nc.dram_tensor(P_ + name, *a, **k)

    if env is None:
        xin = dt("xin", [V, L], F32, kind="ExternalInput").ap()
    else:
        xin = env["xin_full"]
    WinT = dt("WinT", [V, DIM_A], F32, kind="ExternalInput").ap()
    bin_c = dt("bin_c", [128, 4], F32, kind="ExternalInput").ap()
    ln1_rows = dt("ln1_rows", [DEPTH_A, 2, DIM_A], F32, kind="ExternalInput").ap()
    ln2_rows = dt("ln2_rows", [DEPTH_A, 2, DIM_A], F32, kind="ExternalInput").ap()
    lnl_rows = dt("lnl_rows", [2, DIM_A], F32, kind="ExternalInput").ap()
    phzre_c = dt("phzre_c", [DEPTH_A, 128, 4], F32, kind="ExternalInput").ap()
    phzim_c = dt("phzim_c", [DEPTH_A, 128, 4], F32, kind="ExternalInput").ap()
    lcre_c = dt("lcre_c", [DEPTH_A, 128, 4], F32, kind="ExternalInput").ap()
    lcim_c = dt("lcim_c", [DEPTH_A, 128, 4], F32, kind="ExternalInput").ap()
    scl_c = dt("scl_c", [DEPTH_A, 128, 4], F32, kind="ExternalInput").ap()
    W1T = dt("W1T", [DEPTH_A, DIM_A, DFF_A], F32, kind="ExternalInput").ap()
    b1_c = dt("b1_c", [DEPTH_A, 128, 16], F32, kind="ExternalInput").ap()
    W2T = dt("W2T", [DEPTH_A, DFF_A, DIM_A], F32, kind="ExternalInput").ap()
    b2_c = dt("b2_c", [DEPTH_A, 128, 4], F32, kind="ExternalInput").ap()
    WoutT = dt("WoutT", [DIM_A, V], F32, kind="ExternalInput").ap()
    bout_c = dt("bout_c", [128, 32], F32, kind="ExternalInput").ap()
    onesrow_in = dt("onesrow_in", [1, TOK], F32, kind="ExternalInput").ap()
    if env is None:
        xout = dt("xout", [V, L], F32, kind="ExternalOutput").ap()
    else:
        xout = env["x1d"]
    xpre = dt("xpre", [V, L], F32).ap()

    sub_sb = contextlib.ExitStack()
    sb = lambda name, shape, dtp=F32: sub_sb.enter_context(nc.sbuf_tensor(P_ + name, shape, dtp))
    if env is None:
        psz = lambda name, shape: ctx.enter_context(nc.psum_tensor(name, shape, F32))
    else:
        psz = None

    ht = sb("ht", [128, 8 * TOK])
    tre = sb("tre", [128, TOK])
    tim = sb("tim", [128, TOK])
    mre = sb("mre", [128, TOK])
    mim = sb("mim", [128, TOK])
    rhob = sb("rhob", [128, TOK])
    ure = sb("ure", [128, TOK])
    uim = sb("uim", [128, TOK])
    yt = sb("yt", [128, TOK])
    scr = sb("scr", [128, TOK])
    scr2 = sb("scr2", [128, TOK])
    y2f = sb("y2f", [128, 16 * TOK], BF16)
    ybf = sb("ybf", [128, 4 * TOK], BF16)
    sqb = sb("sqb", [128, 512], BF16)
    cpb = sb("cpb", [128, 512], BF16)
    xst = sb("xst", [128, 2 * 512])
    xbt = sb("xbt", [128, 2 * 512], BF16)
    wst = sb("wst", [128, 4096])
    wsb = sb("wsb", [128, 4096], BF16)
    iota1 = sb("iota1", [128, TOK])
    onescol = sb("onescol", [128, 1], BF16)
    onescolf = sb("onescolf", [128, 1])
    onesrow = sb("onesrow", [1, 128])
    pcol = {nm: sb("p_" + nm, [128, 4]) for nm in
            ["phzre", "phzim", "lcre", "lcim", "scl", "b2"]}
    b1col = sb("p_b1", [128, 16])
    bincol = sb("p_bin", [128, 4])
    boutcol = sb("p_bout", [128, 32])
    rr = sb("rr", [128, 4])
    rho = sb("rho", [128, 4])
    cwre = sb("cwre", [128, 4])
    cwim = sb("cwim", [128, 4])
    zsq = sb("zsq", [128, 4, 11])
    zsqi = sb("zsqi", [128, 4, 11])
    c0re = sb("c0re", [128, 4])
    c0im = sb("c0im", [128, 4])
    cendre = sb("cendre", [128, 4])
    cendim = sb("cendim", [128, 4])
    t4a = sb("t4a", [128, 4])
    t4b = sb("t4b", [128, 4])
    t4c = sb("t4c", [128, 4])
    lnw = sb("lnw", [2, DIM_A])
    rowsA = sb("rowsA", [1, TOK])
    rowsB = sb("rowsB", [2, TOK])
    srow = sb("srow", [1, 512])
    srow2 = sb("srow2", [1, 512])

    if env is None:
        pb = [psz(f"pb{i}", [128, 512]) for i in range(4)]
        pbc1 = psz("pbc1", [128, TOK])
        pst1 = psz("pst1", [1, 512])
        pst2 = psz("pst2", [1, 512])
    else:
        pb, pbc1, pst1, pst2 = env["pb"], env["pbc1"], env["pst1"], env["pst2"]

    s_ld = Sem(ctx, nc, P_ + "s_ld")
    s_pe = Sem(ctx, nc, P_ + "s_pe")
    s_v = Sem(ctx, nc, P_ + "s_v")
    s_s = Sem(ctx, nc, P_ + "s_s")
    s_st = Sem(ctx, nc, P_ + "s_st")
    s_bar = Sem(ctx, nc, P_ + "s_bar")

    PE, VE, SE, GE, SY = nc.tensor, nc.vector, nc.scalar, nc.gpsimd, nc.sync

    def v_done():
        VE.drain().then_inc(s_v.h, 1)
        return s_v.inc()

    def s_done():
        SE.drain().then_inc(s_s.h, 1)
        return s_s.inc()

    def pe_mark(inst):
        inst.then_inc(s_pe.h, 1)
        return s_pe.inc()

    def ld(dst, src):
        SY.dma_start(out=dst, in_=src).then_inc(s_ld.h, 16)
        return s_ld.inc(16)

    def gate():
        SY.wait_ge(s_v.h, s_v.n)
        SY.wait_ge(s_s.h, s_s.n)
        SY.wait_ge(s_pe.h, s_pe.n)

    def ln_stats_half(src_ap_fn, nchunks, th):
        SE.wait_ge(s_v.h, s_v.n)
        for d in range(nchunks):
            sl = src_ap_fn(d, th)
            if d > 0:
                SE.wait_ge(s_pe.h, s_pe.n)
                VE.wait_ge(s_pe.h, s_pe.n)
            SE.activation(out=sqb[:], in_=sl, func=AF.Square, scale=1.0)
            ns = s_done()
            VE.tensor_copy(out=cpb[:], in_=sl)
            nv = v_done()
            PE.wait_ge(s_s.h, ns)
            PE.wait_ge(s_v.h, nv)
            PE.matmul(out=pst1[0:1, :], lhsT=onescol[:], rhs=cpb[:],
                      start=(d == 0), stop=(d == nchunks - 1))
            mm = PE.matmul(out=pst2[0:1, :], lhsT=onescol[:], rhs=sqb[:],
                           start=(d == 0), stop=(d == nchunks - 1))
            pe_mark(mm)
        return s_pe.n

    def ln_rows_half(nfeat, eps, npe, th, src1=None, src2=None):
        if src1 is None:
            src1 = pst1[0:1, :]
        if src2 is None:
            src2 = pst2[0:1, :]
        sA = rowsA[0:1, th * 512:(th + 1) * 512]
        sB = rowsB[0:1, th * 512:(th + 1) * 512]
        sm = srow[0:1, :]
        sm2 = srow2[0:1, :]
        VE.wait_ge(s_pe.h, npe)
        VE.tensor_scalar(out=sm, in0=src1, scalar1=1.0 / nfeat, scalar2=None,
                         op0=ALU.mult)
        VE.tensor_scalar(out=sA, in0=src2, scalar1=1.0 / nfeat, scalar2=None,
                         op0=ALU.mult)
        VE.drain()
        VE.tensor_tensor(out=sm2, in0=sm, in1=sm, op=ALU.mult)
        VE.drain()
        VE.tensor_tensor(out=sA, in0=sA, in1=sm2, op=ALU.subtract)
        VE.drain()
        VE.tensor_scalar(out=sA, in0=sA, scalar1=float(eps), scalar2=None,
                         op0=ALU.add)
        nv = v_done()
        SE.wait_ge(s_v.h, nv)
        SE.activation(out=sA, in_=sA, func=AF.Sqrt, scale=1.0)
        ns = s_done()
        VE.wait_ge(s_s.h, ns)
        VE.reciprocal(out=sA, in_=sA)
        VE.drain()
        VE.tensor_tensor(out=sm2, in0=sm, in1=sA, op=ALU.mult)
        VE.drain()
        VE.tensor_scalar(out=sB, in0=sm2, scalar1=-1.0, scalar2=None, op0=ALU.mult)
        return v_done()

    def ln_full(src_ap_fn, nchunks, nfeat, eps):
        for th in range(2):
            npe = ln_stats_half(src_ap_fn, nchunks, th)
            ln_rows_half(nfeat, eps, npe, th)
        return v_done()

    def ln_bcast(d, affine):
        if affine:
            lh1 = lnw[0:1, d * 128:(d + 1) * 128]
            lh2 = lnw[0:2, d * 128:(d + 1) * 128]
        else:
            lh1 = onesrow[:]
            lh2 = None
        mm = None
        for th in range(2):
            PE.matmul(out=pbc1[:, th * 512:(th + 1) * 512], lhsT=lh1,
                      rhs=rowsA[0:1, th * 512:(th + 1) * 512], start=True, stop=True)
            tgt = pb[2] if th == 0 else pb[3]
            if affine:
                mm = PE.matmul(out=tgt[:, 0:512], lhsT=lh2,
                               rhs=rowsB[0:2, th * 512:(th + 1) * 512],
                               start=True, stop=True)
            else:
                mm = PE.matmul(out=tgt[:, 0:512], lhsT=lh1,
                               rhs=rowsB[0:1, th * 512:(th + 1) * 512],
                               start=True, stop=True)
        return pe_mark(mm)

    def ln_apply(src_ap, npe):
        VE.wait_ge(s_pe.h, npe)
        VE.wait_ge(s_st.h, s_st.n)
        VE.tensor_tensor(out=scr[:], in0=src_ap, in1=pbc1[:], op=ALU.mult)
        VE.drain()
        VE.tensor_tensor(out=scr[:, 0:512], in0=scr[:, 0:512], in1=pb[2][:, 0:512],
                         op=ALU.add)
        VE.tensor_tensor(out=scr[:, 512:1024], in0=scr[:, 512:1024],
                         in1=pb[3][:, 0:512], op=ALU.add)
        return v_done()

    def emit(g):
        n0 = 0
        for dsti, srci in [(bincol[:], bin_c), (boutcol[:], bout_c),
                           (onesrow[:], onesrow_in[0:1, 0:128]),
                           (rowsB[1:2, :], onesrow_in)]:
            n0 = ld(dsti, srci)
        GE.iota(iota1[:], pattern=[[1, TOK]], base=1, channel_multiplier=0,
                allow_small_or_imprecise_dtypes=True)
        GE.drain().then_inc(s_bar.h, 1)
        ng0 = s_bar.inc()
        VE.wait_ge(s_bar.h, ng0)
        VE.memset(onescolf[:], 1.0)
        VE.drain()
        VE.tensor_copy(out=onescol[:], in_=onescolf[:])
        VE.wait_ge(s_ld.h, n0)
        v_done()

        def hslice(hh, d, th=None):
            base = (hh * 4 + d) * TOK
            if th is None:
                return ht[:, base:base + TOK]
            return ht[:, base + th * 512:base + th * 512 + 512]

        for hh in range(2):
            for th in range(2):
                for i in range(32):
                    buf = i % 2
                    gate()
                    nx = ld(xst[:, buf * 512:(buf + 1) * 512],
                            xin.rearrange("(c p) t -> c p t", p=128)
                            [i, :, hh * TOK + th * 512:hh * TOK + th * 512 + 512])
                    nw = ld(wst[:, buf * 2048:buf * 2048 + 512],
                            WinT.rearrange("(c p) d -> c p d", p=128)[i])
                    VE.wait_ge(s_ld.h, max(nx, nw))
                    if i >= 2:
                        VE.wait_ge(s_pe.h, s_pe.n)
                    VE.tensor_copy(out=xbt[:, buf * 512:(buf + 1) * 512],
                                   in_=xst[:, buf * 512:(buf + 1) * 512])
                    VE.tensor_copy(out=wsb[:, buf * 2048:buf * 2048 + 512],
                                   in_=wst[:, buf * 2048:buf * 2048 + 512])
                    nv = v_done()
                    PE.wait_ge(s_v.h, nv)
                    for d in range(4):
                        mm = PE.matmul(
                            out=pb[d][:, 0:512],
                            lhsT=wsb[:, buf * 2048 + d * 128:buf * 2048 + (d + 1) * 128],
                            rhs=xbt[:, buf * 512:(buf + 1) * 512],
                            start=(i == 0), stop=(i == 31))
                    npe = pe_mark(mm)
                SE.wait_ge(s_pe.h, npe)
                for d in range(4):
                    SE.activation(out=hslice(hh, d, th), in_=pb[d][:, 0:512],
                                  func=AF.Identity, bias=bincol[:, d:d + 1], scale=1.0)
                ns = s_done()
                PE.wait_ge(s_s.h, ns)

        for t in range(DEPTH_A):
            gate()
            for nm, srci in [("phzre", phzre_c[t]), ("phzim", phzim_c[t]),
                             ("lcre", lcre_c[t]), ("lcim", lcim_c[t]),
                             ("scl", scl_c[t]), ("b2", b2_c[t])]:
                ld(pcol[nm][:], srci)
            ld(b1col[:], b1_c[t])
            nln1 = ld(lnw[:], ln1_rows[t])
            VE.wait_ge(s_ld.h, s_ld.n)
            VE.tensor_tensor(out=rr[:], in0=pcol["phzre"][:], in1=pcol["phzre"][:], op=ALU.mult)
            VE.tensor_tensor(out=t4a[:], in0=pcol["phzim"][:], in1=pcol["phzim"][:], op=ALU.mult)
            VE.drain()
            VE.tensor_tensor(out=rr[:], in0=rr[:], in1=t4a[:], op=ALU.add)
            nv = v_done()
            SE.wait_ge(s_v.h, nv)
            SE.activation(out=rr[:], in_=rr[:], func=AF.Sqrt, scale=1.0)
            SE.drain()
            SE.activation(out=rho[:], in_=rr[:], func=AF.Tanh, scale=1.0)
            ns = s_done()
            VE.wait_ge(s_s.h, ns)
            VE.reciprocal(out=t4a[:], in_=rr[:])
            VE.drain()
            VE.tensor_tensor(out=cwre[:], in0=pcol["phzre"][:], in1=t4a[:], op=ALU.mult)
            VE.tensor_tensor(out=cwim[:], in0=pcol["phzim"][:], in1=t4a[:], op=ALU.mult)
            VE.drain()
            VE.tensor_copy(out=zsq[:, :, 0], in_=cwre[:])
            VE.tensor_scalar(out=zsqi[:, :, 0], in0=cwim[:], scalar1=-1.0, scalar2=None,
                             op0=ALU.mult)
            VE.drain()
            for k in range(1, 11):
                a, bq = zsq[:, :, k - 1], zsqi[:, :, k - 1]
                VE.tensor_tensor(out=t4a[:], in0=a, in1=a, op=ALU.mult)
                VE.tensor_tensor(out=t4b[:], in0=bq, in1=bq, op=ALU.mult)
                VE.tensor_tensor(out=t4c[:], in0=a, in1=bq, op=ALU.mult)
                VE.drain()
                VE.tensor_tensor(out=zsq[:, :, k], in0=t4a[:], in1=t4b[:], op=ALU.subtract)
                VE.tensor_scalar(out=zsqi[:, :, k], in0=t4c[:], scalar1=2.0, scalar2=None,
                                 op0=ALU.mult)
                VE.drain()
            VE.tensor_tensor(out=t4a[:], in0=cwre[:], in1=pcol["lcre"][:], op=ALU.mult)
            VE.tensor_tensor(out=t4b[:], in0=cwim[:], in1=pcol["lcim"][:], op=ALU.mult)
            VE.tensor_tensor(out=t4c[:], in0=cwre[:], in1=pcol["lcim"][:], op=ALU.mult)
            VE.tensor_tensor(out=c0im[:], in0=cwim[:], in1=pcol["lcre"][:], op=ALU.mult)
            VE.drain()
            VE.tensor_tensor(out=c0re[:], in0=t4a[:], in1=t4b[:], op=ALU.subtract)
            VE.tensor_tensor(out=c0im[:], in0=c0im[:], in1=t4c[:], op=ALU.add)
            v_done()

            for hh in range(2):
                ln_full(lambda d, th: hslice(hh, d, th), 4, DIM_A, EPS_A)
                VE.wait_ge(s_ld.h, nln1)
                nv = v_done()
                PE.wait_ge(s_v.h, nv)
                for d in range(4):
                    npe = ln_bcast(d, affine=True)
                    nv = ln_apply(hslice(hh, d), npe)
                    SE.wait_ge(s_v.h, nv)
                    SE.activation(out=yt[:], in_=scr[:], func=AF.Silu, scale=1.0)
                    ns = s_done()
                    PE.wait_ge(s_s.h, ns)
                    VE.wait_ge(s_s.h, ns)
                    if hh == 0:
                        VE.memset(mre[:, 0:1], 1.0)
                        VE.memset(mim[:, 0:1], 0.0)
                    else:
                        VE.tensor_copy(out=mre[:, 0:1], in_=zsq[:, d:d + 1, 10])
                        VE.tensor_copy(out=mim[:, 0:1], in_=zsqi[:, d:d + 1, 10])
                    VE.drain()
                    for k in range(10):
                        w = 1 << k
                        re0 = mre[:, 0:w]
                        im0 = mim[:, 0:w]
                        VE.tensor_scalar(out=scr[:, 0:w], in0=re0, scalar1=zsq[:, d:d + 1, k],
                                         scalar2=None, op0=ALU.mult)
                        VE.tensor_scalar(out=scr2[:, 0:w], in0=im0, scalar1=zsqi[:, d:d + 1, k],
                                         scalar2=None, op0=ALU.mult)
                        VE.tensor_scalar(out=ure[:, 0:w], in0=re0, scalar1=zsqi[:, d:d + 1, k],
                                         scalar2=None, op0=ALU.mult)
                        VE.tensor_scalar(out=uim[:, 0:w], in0=im0, scalar1=zsq[:, d:d + 1, k],
                                         scalar2=None, op0=ALU.mult)
                        VE.drain()
                        VE.tensor_tensor(out=mre[:, w:2 * w], in0=scr[:, 0:w],
                                         in1=scr2[:, 0:w], op=ALU.subtract)
                        VE.tensor_tensor(out=mim[:, w:2 * w], in0=ure[:, 0:w],
                                         in1=uim[:, 0:w], op=ALU.add)
                        VE.drain()
                    VE.tensor_scalar(out=rhob[:], in0=iota1[:], scalar1=0.0,
                                     scalar2=rho[:, d:d + 1], op0=ALU.mult, op1=ALU.add)
                    VE.drain()
                    VE.tensor_tensor(out=ure[:], in0=yt[:], in1=mre[:],
                                     op=ALU.mult)
                    VE.tensor_tensor(out=uim[:], in0=yt[:], in1=mim[:],
                                     op=ALU.mult)
                    VE.drain()
                    init_re = c0re[:, d:d + 1] if hh == 0 else cendre[:, d:d + 1]
                    init_im = c0im[:, d:d + 1] if hh == 0 else cendim[:, d:d + 1]
                    VE.tensor_tensor_scan(out=tre[:], data0=rhob[:],
                                          data1=ure[:], initial=init_re,
                                          op0=ALU.mult, op1=ALU.add)
                    VE.drain()
                    VE.tensor_tensor_scan(out=tim[:], data0=rhob[:],
                                          data1=uim[:], initial=init_im,
                                          op0=ALU.mult, op1=ALU.add)
                    VE.drain()
                    if hh == 0:
                        VE.tensor_copy(out=cendre[:, d:d + 1],
                                       in_=tre[:, TOK - 1:TOK])
                        VE.tensor_copy(out=cendim[:, d:d + 1],
                                       in_=tim[:, TOK - 1:TOK])
                        VE.drain()
                    VE.tensor_tensor(out=scr[:], in0=mre[:],
                                     in1=tre[:], op=ALU.mult)
                    VE.tensor_tensor(out=scr2[:], in0=mim[:],
                                     in1=tim[:], op=ALU.mult)
                    VE.drain()
                    VE.tensor_tensor(out=scr[:], in0=scr[:], in1=scr2[:], op=ALU.add)
                    VE.drain()
                    VE.scalar_tensor_tensor(out=hslice(hh, d), in0=scr[:],
                                            scalar=pcol["scl"][:, d:d + 1],
                                            in1=hslice(hh, d), op0=ALU.mult, op1=ALU.add)
                    VE.drain()
                v_done()

            gate()
            nln2 = ld(lnw[:], ln2_rows[t])
            for hh in range(2):
                ln_full(lambda d, th: hslice(hh, d, th), 4, DIM_A, EPS_A)
                VE.wait_ge(s_ld.h, nln2)
                nv = v_done()
                PE.wait_ge(s_v.h, nv)
                for d in range(4):
                    npe = ln_bcast(d, affine=True)
                    nv = ln_apply(hslice(hh, d), npe)
                    SE.wait_ge(s_v.h, nv)
                    SE.activation(out=ybf[:, d * TOK:(d + 1) * TOK], in_=scr[:],
                                  func=AF.Copy, scale=1.0)
                    ns = s_done()
                    PE.wait_ge(s_s.h, ns)
                ny = s_s.n
                for o in range(16):
                    gate()
                    nw = ld(wst[:, 0:512].rearrange("p (c f) -> p c f", c=4),
                            W1T.rearrange("t (c p) f -> t p c f", p=128)
                            [t, :, :, o * 128:(o + 1) * 128])
                    VE.wait_ge(s_ld.h, nw)
                    VE.wait_ge(s_pe.h, s_pe.n)
                    VE.tensor_copy(out=wsb[:, 0:512], in_=wst[:, 0:512])
                    nv = v_done()
                    PE.wait_ge(s_v.h, nv)
                    PE.wait_ge(s_s.h, ny if o == 0 else s_s.n)
                    for c in range(4):
                        for th in range(2):
                            mm = PE.matmul(
                                out=pb[th][:, 0:512],
                                lhsT=wsb[:, c * 128:(c + 1) * 128],
                                rhs=ybf[:, c * TOK + th * 512:c * TOK + th * 512 + 512],
                                start=(c == 0), stop=(c == 3))
                    npe = pe_mark(mm)
                    SE.wait_ge(s_pe.h, npe)
                    for th in range(2):
                        SE.activation(out=y2f[:, o * TOK + th * 512:o * TOK + th * 512 + 512],
                                      in_=pb[th][:, 0:512], func=AF.Silu,
                                      bias=b1col[:, o:o + 1], scale=1.0)
                    s_done()
                for od in range(4):
                    gate()
                    nw = ld(wst[:, 0:2048].rearrange("p (i f) -> p i f", i=16),
                            W2T.rearrange("t (i p) f -> t p i f", p=128)
                            [t, :, :, od * 128:(od + 1) * 128])
                    VE.wait_ge(s_ld.h, nw)
                    VE.wait_ge(s_pe.h, s_pe.n)
                    VE.tensor_copy(out=wsb[:, 0:2048], in_=wst[:, 0:2048])
                    nv = v_done()
                    PE.wait_ge(s_v.h, nv)
                    PE.wait_ge(s_s.h, s_s.n)
                    for i in range(16):
                        for th in range(2):
                            mm = PE.matmul(
                                out=pb[th][:, 0:512],
                                lhsT=wsb[:, i * 128:(i + 1) * 128],
                                rhs=y2f[:, i * TOK + th * 512:i * TOK + th * 512 + 512],
                                start=(i == 0), stop=(i == 15))
                    npe = pe_mark(mm)
                    SE.wait_ge(s_pe.h, npe)
                    for th in range(2):
                        SE.activation(out=(scr if th == 0 else scr2)[:, 0:512],
                                      in_=pb[th][:, 0:512], func=AF.Identity,
                                      bias=pcol["b2"][:, od:od + 1], scale=1.0)
                    ns = s_done()
                    VE.wait_ge(s_s.h, ns)
                    VE.tensor_tensor(out=hslice(hh, od, 0), in0=hslice(hh, od, 0),
                                     in1=scr[:, 0:512], op=ALU.add)
                    VE.tensor_tensor(out=hslice(hh, od, 1), in0=hslice(hh, od, 1),
                                     in1=scr2[:, 0:512], op=ALU.add)
                    nv = v_done()
                    PE.wait_ge(s_v.h, nv)

        gate()
        nlnl = ld(lnw[:], lnl_rows)
        for hh in range(2):
            ln_full(lambda d, th: hslice(hh, d, th), 4, DIM_A, EPS_A)
            VE.wait_ge(s_ld.h, nlnl)
            nv = v_done()
            PE.wait_ge(s_v.h, nv)
            for d in range(4):
                npe = ln_bcast(d, affine=True)
                nv = ln_apply(hslice(hh, d), npe)
                SE.wait_ge(s_v.h, nv)
                SE.activation(out=ybf[:, d * TOK:(d + 1) * TOK], in_=scr[:],
                              func=AF.Copy, scale=1.0)
                ns = s_done()
                PE.wait_ge(s_s.h, ns)
            for o in range(32):
                gate()
                nw = ld(wst[:, 0:512].rearrange("p (c v) -> p c v", c=4),
                        WoutT.rearrange("(c p) v -> p c v", p=128)[:, :, o * 128:(o + 1) * 128])
                nx = ld(xst[:, 0:512],
                        xin.rearrange("(c p) t -> c p t", p=128)
                        [o, :, hh * TOK:hh * TOK + 512])
                nx2 = ld(xst[:, 512:1024],
                         xin.rearrange("(c p) t -> c p t", p=128)
                         [o, :, hh * TOK + 512:hh * TOK + 1024])
                VE.wait_ge(s_ld.h, max(nw, nx, nx2))
                if o > 0:
                    VE.wait_ge(s_pe.h, s_pe.n)
                VE.tensor_copy(out=wsb[:, 0:512], in_=wst[:, 0:512])
                nv = v_done()
                PE.wait_ge(s_v.h, nv)
                for c in range(4):
                    for th in range(2):
                        mm = PE.matmul(
                            out=pb[th][:, 0:512],
                            lhsT=wsb[:, c * 128:(c + 1) * 128],
                            rhs=ybf[:, c * TOK + th * 512:c * TOK + th * 512 + 512],
                            start=(c == 0), stop=(c == 3))
                npe = pe_mark(mm)
                SE.wait_ge(s_pe.h, npe)
                SE.wait_ge(s_st.h, s_st.n)
                for th in range(2):
                    SE.activation(out=(scr if th == 0 else scr2)[:, 0:512],
                                  in_=pb[th][:, 0:512], func=AF.Identity,
                                  bias=boutcol[:, o:o + 1], scale=1.0)
                ns = s_done()
                VE.wait_ge(s_s.h, ns)
                VE.tensor_tensor(out=scr[:, 0:512], in0=scr[:, 0:512], in1=xst[:, 0:512],
                                 op=ALU.add)
                VE.tensor_tensor(out=scr2[:, 0:512], in0=scr2[:, 0:512],
                                 in1=xst[:, 512:1024], op=ALU.add)
                nvp = v_done()
                SY.wait_ge(s_v.h, nvp)
                SY.dma_start(out=xpre.rearrange("(c p) t -> c p t", p=128)
                             [o, :, hh * TOK:hh * TOK + 512],
                             in_=scr[:, 0:512]).then_inc(s_st.h, 16)
                SY.dma_start(out=xpre.rearrange("(c p) t -> c p t", p=128)
                             [o, :, hh * TOK + 512:hh * TOK + 1024],
                             in_=scr2[:, 0:512]).then_inc(s_st.h, 16)
                s_st.inc(32)
                VE.tensor_copy(out=cpb[:], in_=scr[:, 0:512])
                nv = v_done()
                SE.wait_ge(s_v.h, nvp)
                SE.activation(out=sqb[:], in_=scr[:, 0:512], func=AF.Square, scale=1.0)
                ns = s_done()
                PE.wait_ge(s_v.h, nv)
                PE.wait_ge(s_s.h, ns)
                PE.matmul(out=pst1[0:1, :], lhsT=onescol[:], rhs=cpb[:],
                          start=(o == 0), stop=(o == 31))
                mm = PE.matmul(out=pst2[0:1, :], lhsT=onescol[:], rhs=sqb[:],
                               start=(o == 0), stop=(o == 31))
                pe_mark(mm)
                VE.wait_ge(s_pe.h, s_pe.n)
                VE.tensor_copy(out=cpb[:], in_=scr2[:, 0:512])
                nv = v_done()
                SE.wait_ge(s_pe.h, s_pe.n)
                SE.activation(out=sqb[:], in_=scr2[:, 0:512], func=AF.Square, scale=1.0)
                ns = s_done()
                PE.wait_ge(s_v.h, nv)
                PE.wait_ge(s_s.h, ns)
                PE.matmul(out=pb[2][0:1, :], lhsT=onescol[:], rhs=cpb[:],
                          start=(o == 0), stop=(o == 31))
                mm = PE.matmul(out=pb[3][0:1, :], lhsT=onescol[:], rhs=sqb[:],
                               start=(o == 0), stop=(o == 31))
                pe_mark(mm)
            npe = s_pe.n
            ln_rows_half(V, EPS_OUT, npe, 0)
            ln_rows_half(V, EPS_OUT, npe, 1, src1=pb[2][0:1, :], src2=pb[3][0:1, :])
            nv = v_done()
            PE.wait_ge(s_v.h, nv)
            SY.wait_ge(s_st.h, s_st.n)
            for o in range(32):
                npe = ln_bcast(0, affine=False)
                gate()
                nx = ld(xst[:, 0:512],
                        xpre.rearrange("(c p) t -> c p t", p=128)
                        [o, :, hh * TOK:hh * TOK + 512])
                nx2 = ld(xst[:, 512:1024],
                         xpre.rearrange("(c p) t -> c p t", p=128)
                         [o, :, hh * TOK + 512:hh * TOK + 1024])
                VE.wait_ge(s_ld.h, max(nx, nx2))
                VE.wait_ge(s_pe.h, npe)
                if o > 0:
                    VE.wait_ge(s_st.h, s_st.n)
                VE.tensor_tensor(out=scr[:], in0=xst[:, 0:1024], in1=pbc1[:], op=ALU.mult)
                VE.drain()
                VE.tensor_tensor(out=scr[:, 0:512], in0=scr[:, 0:512],
                                 in1=pb[2][:, 0:512], op=ALU.add)
                VE.tensor_tensor(out=scr[:, 512:1024], in0=scr[:, 512:1024],
                                 in1=pb[3][:, 0:512], op=ALU.add)
                nv = v_done()
                SY.wait_ge(s_v.h, nv)
                SY.dma_start(out=xout.rearrange("(c p) t -> c p t", p=128)
                             [o, :, hh * TOK:(hh + 1) * TOK],
                             in_=scr[:]).then_inc(s_st.h, 16)
                s_st.inc(16)
                PE.wait_ge(s_v.h, nv)
        SY.wait_ge(s_st.h, s_st.n)
        g.wait_ge(s_st.h, s_st.n)

    def emit_wrapped(g):
        if env is not None:
            for eng in (PE, VE, SE, SY, GE):
                for (sh, sv) in env.get("entry_deps", []):
                    eng.wait_ge(sh, sv)
        emit(g)
        sub_sb.close()
        if env is not None:
            env["entry_deps"] = [(s_v.h, s_v.n), (s_s.h, s_s.n), (s_pe.h, s_pe.n),
                                 (s_st.h, s_st.n), (s_ld.h, s_ld.n)]

    if env is None:
        with nc.Block() as block:
            @block.gpsimd
            def _(g):
                emit_wrapped(g)
        return nc
    return emit_wrapped


def hostprep_A(inputs, core):
    b = core // 2
    x = np.asarray(inputs["L0_attn_x"])
    im = {
        "xin": np.ascontiguousarray(_f32(x[b]).T),
        "WinT": np.ascontiguousarray(_f32(inputs["L0_attn_Win"]).T),
        "bin_c": colpack(inputs["L0_attn_bin"], 4),
        "ln1_rows": np.ascontiguousarray(
            np.stack([_f32(inputs["L0_attn_ln1_w"]), _f32(inputs["L0_attn_ln1_b"])], axis=1)),
        "ln2_rows": np.ascontiguousarray(
            np.stack([_f32(inputs["L0_attn_ln2_w"]), _f32(inputs["L0_attn_ln2_b"])], axis=1)),
        "lnl_rows": np.ascontiguousarray(
            np.stack([_f32(inputs["L0_attn_lnl_w"]), _f32(inputs["L0_attn_lnl_b"])], axis=0)),
        "phzre_c": np.stack([colpack(_f32(inputs["L0_attn_phz_re"])[t], 4) for t in range(4)]),
        "phzim_c": np.stack([colpack(_f32(inputs["L0_attn_phz_im"])[t], 4) for t in range(4)]),
        "lcre_c": np.stack([colpack(_f32(inputs["L0_attn_lc_re"])[t], 4) for t in range(4)]),
        "lcim_c": np.stack([colpack(_f32(inputs["L0_attn_lc_im"])[t], 4) for t in range(4)]),
        "scl_c": np.stack([colpack(_f32(inputs["L0_attn_sc_lin"])[t], 4) for t in range(4)]),
        "W1T": np.ascontiguousarray(_f32(inputs["L0_attn_W1"]).transpose(0, 2, 1)),
        "b1_c": np.stack([colpack(_f32(inputs["L0_attn_b1"])[t], 16) for t in range(4)]),
        "W2T": np.ascontiguousarray(_f32(inputs["L0_attn_W2"]).transpose(0, 2, 1)),
        "b2_c": np.stack([colpack(_f32(inputs["L0_attn_b2"])[t], 4) for t in range(4)]),
        "WoutT": np.ascontiguousarray(_f32(inputs["L0_attn_Wout"]).T),
        "bout_c": colpack(inputs["L0_attn_bout"], 32),
        "onesrow_in": np.ones((1, TOK), np.float32),
    }
    return im


def run_stageA(inputs):
    nc = build_stageA()
    ims = [hostprep_A(inputs, c) for c in range(NCORES)]
    res = run_bass_kernel_spmd(nc, ims, core_ids=list(range(NCORES)))
    out = np.empty((B, L, V), np.float32)
    for c in range(0, NCORES, 2):
        out[c // 2] = res.results[c]["xout"].T
    return out





def build_stageB(env=None):
    if env is None:
        ctx = contextlib.ExitStack()
        nc = bass.Bass("TRN2", target_bir_lowering=False)
        P_ = ""
    else:
        ctx = env["ctx"]
        nc = env["nc"]
        P_ = "B_"
    dt = lambda name, *a, **k: nc.dram_tensor(P_ + name, *a, **k)

    if env is None:
        xin = dt("xin", [V, TOK], F32, kind="ExternalInput").ap()
        xsrc = lambda i: xin.rearrange("(c p) t -> c p t", p=128)[i]
    else:
        xin = None
        x1f = env["x1d"].rearrange("(c p) t -> c p t", p=128)
        xsrc = lambda i: x1f[i]
    hsel = dt("hsel", [128, 2], F32, kind="ExternalInput").ap()
    WbT = dt("WbT", [V, V], F32, kind="ExternalInput").ap()
    bb_c = dt("bb_c", [128, 32], F32, kind="ExternalInput").ap()
    encT = dt("encT", [128, 32 * 40], F32, kind="ExternalInput").ap()
    FT = dt("FT", [V, V], BF16, kind="ExternalInput").ap()
    coefs0 = dt("coefs0", [3, V, 32], F32, kind="ExternalInput").ap()
    coefs1 = dt("coefs1", [3, V, 8], F32, kind="ExternalInput").ap()
    mask0 = dt("mask0", [3, V, 32], U8, kind="ExternalInput").ap()
    mask1 = dt("mask1", [3, V, 8], U8, kind="ExternalInput").ap()
    onesrow_in = dt("onesrow_in", [1, TOK], F32, kind="ExternalInput").ap()
    if env is None:
        xout = dt("xout", [V, TOK], F32, kind="ExternalOutput").ap()
    else:
        xout = env["x2d"]
    xpre = dt("xpre", [V, TOK], F32).ap()

    sub_sb = contextlib.ExitStack()
    sb = lambda name, shape, dtp=F32: sub_sb.enter_context(nc.sbuf_tensor(P_ + name, shape, dtp))

    x1b = sb("x1b", [128, 32 * TOK], BF16)
    xw = sb("xw", [128, 2 * L]) if env is not None else None
    hsels = sb("hsels", [128, 2])
    w01 = sb("w01", [128, 2, 3, 40])
    w01m = sb("w01m", [128, 3, 40])
    w01s = sb("w01s", [128, 40])
    w01all = sb("w01all", [128, 32, 40], BF16)
    fwT = sb("fwT", [40, V], BF16)
    fwacc = sb("fwacc", [40, V])
    ebf = sb("ebf", [40, TOK], BF16)
    encw = sb("encw", [128, 32 * 40])
    encb = sb("encb", [128, 32 * 40], BF16)
    wst = sb("wst", [128, 2 * 4096])
    wsb = sb("wsb", [128, 2 * 4096], BF16)
    xst = sb("xst", [128, 2 * TOK])
    bsc = sb("bsc", [128, TOK])
    scr = sb("scr", [128, TOK])
    scr2 = sb("scr2", [128, TOK])
    sqb = sb("sqb", [128, 512], BF16)
    cpb = sb("cpb", [128, 512], BF16)
    onescol = sb("onescol", [128, 1], BF16)
    onescolf = sb("onescolf", [128, 1])
    onesrow = sb("onesrow", [1, 128])
    bbcol = sb("bbcol", [128, 32])
    rowsA = sb("rowsA", [1, TOK])
    rowsB = sb("rowsB", [2, TOK])
    srow = sb("srow", [1, 512])
    srow2 = sb("srow2", [1, 512])

    if env is None:
        psz = lambda name, shape: ctx.enter_context(nc.psum_tensor(name, shape, F32))
        pb = [psz(f"pb{i}", [128, 512]) for i in range(4)]
        pbc1 = psz("pbc1", [128, TOK])
        pst1 = psz("pst1", [1, 512])
        pst2 = psz("pst2", [1, 512])
    else:
        pb, pbc1, pst1, pst2 = env["pb"], env["pbc1"], env["pst1"], env["pst2"]

    s_ld = Sem(ctx, nc, P_ + "s_ld")
    s_pe = Sem(ctx, nc, P_ + "s_pe")
    s_v = Sem(ctx, nc, P_ + "s_v")
    s_s = Sem(ctx, nc, P_ + "s_s")
    s_st = Sem(ctx, nc, P_ + "s_st")
    s_lg = Sem(ctx, nc, P_ + "s_lg")
    s_lp = [Sem(ctx, nc, P_ + "s_l0"), Sem(ctx, nc, P_ + "s_l1")]
    s_xp = [Sem(ctx, nc, P_ + "s_x0"), Sem(ctx, nc, P_ + "s_x1")]
    s_stp = [Sem(ctx, nc, P_ + "s_sp0"), Sem(ctx, nc, P_ + "s_sp1")]

    PE, VE, SE, GE, SY = nc.tensor, nc.vector, nc.scalar, nc.gpsimd, nc.sync

    def v_done():
        VE.drain().then_inc(s_v.h, 1)
        return s_v.inc()

    def s_done():
        SE.drain().then_inc(s_s.h, 1)
        return s_s.inc()

    def pe_mark(inst):
        inst.then_inc(s_pe.h, 1)
        return s_pe.inc()

    def ld(dst, src):
        SY.dma_start(out=dst, in_=src).then_inc(s_ld.h, 16)
        return s_ld.inc(16)

    def ldp(sem, dst, src):
        SY.dma_start(out=dst, in_=src).then_inc(sem.h, 16)
        return sem.inc(16)

    def ln_rows_half(nfeat, eps, npe, th, src1, src2):
        sA = rowsA[0:1, th * 512:(th + 1) * 512]
        sB = rowsB[0:1, th * 512:(th + 1) * 512]
        sm = srow[0:1, :]
        sm2 = srow2[0:1, :]
        VE.wait_ge(s_pe.h, npe)
        VE.tensor_scalar(out=sm, in0=src1, scalar1=1.0 / nfeat, scalar2=None,
                         op0=ALU.mult)
        VE.tensor_scalar(out=sA, in0=src2, scalar1=1.0 / nfeat, scalar2=None,
                         op0=ALU.mult)
        VE.drain()
        VE.tensor_tensor(out=sm2, in0=sm, in1=sm, op=ALU.mult)
        VE.drain()
        VE.tensor_tensor(out=sA, in0=sA, in1=sm2, op=ALU.subtract)
        VE.drain()
        VE.tensor_scalar(out=sA, in0=sA, scalar1=float(eps), scalar2=None,
                         op0=ALU.add)
        nv = v_done()
        SE.wait_ge(s_v.h, nv)
        SE.activation(out=sA, in_=sA, func=AF.Sqrt, scale=1.0)
        ns = s_done()
        VE.wait_ge(s_s.h, ns)
        VE.reciprocal(out=sA, in_=sA)
        VE.drain()
        VE.tensor_tensor(out=sm2, in0=sm, in1=sA, op=ALU.mult)
        VE.drain()
        VE.tensor_scalar(out=sB, in0=sm2, scalar1=-1.0, scalar2=None, op0=ALU.mult)
        return v_done()

    def x_dma(i, b):
        if env is None:
            return ldp(s_xp[b], xst[:, b * TOK:(b + 1) * TOK], xsrc(i))
        return ldp(s_xp[b], xw[:, b * L:(b + 1) * L], xsrc(i))

    def x_blend(b, nx):
        VE.wait_ge(s_xp[b].h, nx)
        if env is None:
            return v_done()
        VE.tensor_scalar(out=bsc[:], in0=xw[:, b * L:b * L + TOK],
                         scalar1=hsels[:, 0:1], scalar2=None, op0=ALU.mult)
        VE.drain()
        VE.scalar_tensor_tensor(out=xst[:, b * TOK:(b + 1) * TOK],
                                in0=xw[:, b * L + TOK:(b + 1) * L],
                                scalar=hsels[:, 1:2], in1=bsc[:],
                                op0=ALU.mult, op1=ALU.add)
        return v_done()

    def emit(g):
        n0 = ld(hsels[:], hsel)
        n0 = ld(bbcol[:], bb_c)
        n0 = ld(onesrow[:], onesrow_in[0:1, 0:128])
        n0 = ld(rowsB[1:2, :], onesrow_in)
        n0 = ld(encw[:], encT)
        VE.wait_ge(s_ld.h, n0)
        VE.memset(onescolf[:], 1.0)
        VE.memset(fwacc[:], 0.0)
        VE.drain()
        VE.tensor_copy(out=onescol[:], in_=onescolf[:])
        VE.tensor_copy(out=encb[:], in_=encw[:])
        v_done()

        xm = {}
        for i in range(32):
            b = i % 2
            if i >= 2:
                SY.wait_ge(s_v.h, xm[i - 2])
            nx = x_dma(i, b)
            x_blend(b, nx)
            VE.tensor_copy(out=x1b[:, i * TOK:(i + 1) * TOK],
                           in_=xst[:, b * TOK:(b + 1) * TOK])
            xm[i] = v_done()

        wm = {}
        for c in range(32):
            b = c % 2
            if c >= 2:
                SY.wait_ge(s_v.h, wm[c - 2])
            n1 = ldp(s_lp[b], w01[:, b, :, 0:32],
                     coefs0.rearrange("e (c p) r -> c p e r", p=128)[c])
            n1 = ldp(s_lp[b], w01[:, b, :, 32:40],
                     coefs1.rearrange("e (c p) r -> c p e r", p=128)[c])
            n1 = ldp(s_lp[b], mu8[:, b, :, 0:32],
                     mask0.rearrange("e (c p) r -> c p e r", p=128)[c])
            n1 = ldp(s_lp[b], mu8[:, b, :, 32:40],
                     mask1.rearrange("e (c p) r -> c p e r", p=128)[c])
            VE.wait_ge(s_lp[b].h, n1)
            VE.tensor_copy(out=w01m[:], in_=mu8[:, b])
            VE.drain()
            VE.tensor_tensor(out=w01m[:], in0=w01m[:], in1=w01[:, b], op=ALU.mult)
            VE.drain()
            VE.tensor_tensor(out=w01s[:], in0=w01m[:, 0, :], in1=w01m[:, 1, :],
                             op=ALU.add)
            VE.drain()
            VE.tensor_tensor(out=w01all[:, c, :], in0=w01s[:], in1=w01m[:, 2, :],
                             op=ALU.add)
            wm[c] = v_done()

        pacc = lambda n: (pb[n][0:40, 0:512] if n < 4 else
                          (pbc1[0:40, (n - 4) * 512:(n - 3) * 512] if n < 6 else
                           pb[n - 6][0:40, 0:512]))
        pemj = {}
        m1 = {}
        m2 = {}
        PE.wait_ge(s_v.h, wm[31])
        for j in range(32):
            b = j % 2
            wb = wsb[:, b * 4096:(b + 1) * 4096]
            if j >= 2:
                SY.wait_ge(s_pe.h, pemj[(j - 2, 7)])
            else:
                SY.wait_ge(s_v.h, wm[30 + j])
            nl = ldp(s_lp[b], wb[:, 0:2048],
                     FT.rearrange("(j p) n -> j p n", p=128)[j, :, 0:2048])
            nl = ldp(s_lp[b], wb[:, 2048:4096],
                     FT.rearrange("(j p) n -> j p n", p=128)[j, :, 2048:4096])
            PE.wait_ge(s_lp[b].h, nl)
            if j >= 1:
                PE.wait_ge(s_v.h, m2[j - 1])
            for n in range(8):
                if n == 6 and j >= 0:
                    PE.wait_ge(s_v.h, m1[j])
                mm = PE.matmul(out=pacc(n), lhsT=w01all[:, j, :],
                               rhs=wb[:, n * 512:(n + 1) * 512],
                               start=True, stop=True)
                pemj[(j, n)] = pe_mark(mm)
                VE.wait_ge(s_pe.h, pemj[(j, n)])
                VE.tensor_tensor(out=fwacc[:, n * 512:(n + 1) * 512],
                                 in0=fwacc[:, n * 512:(n + 1) * 512],
                                 in1=pacc(n), op=ALU.add)
                if n == 3:
                    m1[j] = v_done()
            m2[j] = v_done()
        VE.tensor_copy(out=fwT[:], in_=fwacc[:])
        nv_fw = v_done()

        PE.wait_ge(s_v.h, nv_fw)
        for i in range(32):
            for th in range(2):
                mm = PE.matmul(out=(pb[2] if th == 0 else pb[3])[0:40, :],
                               lhsT=encb[:, i * 40:(i + 1) * 40],
                               rhs=x1b[:, i * TOK + th * 512:i * TOK + th * 512 + 512],
                               start=(i == 0), stop=(i == 31))
        npe = pe_mark(mm)
        SE.wait_ge(s_pe.h, npe)
        for th in range(2):
            SE.activation(out=ebf[:, th * 512:(th + 1) * 512],
                          in_=(pb[2] if th == 0 else pb[3])[0:40, :],
                          func=AF.Copy, scale=1.0)
        ns = s_done()
        PE.wait_ge(s_s.h, ns)

        cm = {}
        bm = {}
        xbm = {}
        stm = {}
        pend = None
        for o in range(32):
            b = o % 2
            wb = wst[:, b * 4096:(b + 1) * 4096].rearrange("p (c f) -> p c f", c=32)
            wbb = wsb[:, b * 4096:(b + 1) * 4096]
            if o >= 2:
                SY.wait_ge(s_v.h, cm[o - 2])
            else:
                SY.wait_ge(s_pe.h, pemj[(30 + o, 7)])
            nw = ldp(s_lp[b], wb[:, 0:16, :],
                     WbT.rearrange("(c p) f -> p c f", p=128)[:, 0:16, o * 128:(o + 1) * 128])
            nw = ldp(s_lp[b], wb[:, 16:32, :],
                     WbT.rearrange("(c p) f -> p c f", p=128)[:, 16:32, o * 128:(o + 1) * 128])
            if o >= 2:
                SY.wait_ge(s_v.h, xbm[o - 2])
            elif o == 0:
                SY.wait_ge(s_v.h, xm[31])
            nx = x_dma(o, b)
            if pend is not None:
                po, nvp_p = pend
                pb_ = po % 2
                SY.wait_ge(s_v.h, nvp_p)
                SY.dma_start(out=xpre.rearrange("(c p) t -> c p t", p=128)[po, :, 0:512],
                             in_=scr[:, pb_ * 512:pb_ * 512 + 512]).then_inc(s_stp[pb_].h, 16)
                SY.dma_start(out=xpre.rearrange("(c p) t -> c p t", p=128)[po, :, 512:1024],
                             in_=scr2[:, pb_ * 512:pb_ * 512 + 512]).then_inc(s_stp[pb_].h, 16)
                stm[po] = s_stp[pb_].inc(32)
                pend = None
            VE.wait_ge(s_lp[b].h, nw)
            if o >= 2:
                VE.wait_ge(s_pe.h, bm[o - 2])
            VE.tensor_copy(out=wbb[:], in_=wst[:, b * 4096:(b + 1) * 4096])
            cm[o] = v_done()
            xbm[o] = x_blend(b, nx)
            PE.wait_ge(s_v.h, cm[o])
            for c in range(32):
                for th in range(2):
                    mm = PE.matmul(
                        out=pb[th][:, 0:512],
                        lhsT=wbb[:, c * 128:(c + 1) * 128],
                        rhs=x1b[:, c * TOK + th * 512:c * TOK + th * 512 + 512],
                        start=(c == 0), stop=(c == 31))
            for th in range(2):
                mm = PE.matmul(out=pbc1[:, th * 512:th * 512 + 512],
                               lhsT=fwT[:, o * 128:(o + 1) * 128],
                               rhs=ebf[:, th * 512:(th + 1) * 512],
                               start=True, stop=True)
            npe = pe_mark(mm)
            bm[o] = npe
            SE.wait_ge(s_pe.h, npe)
            if o >= 2:
                SE.wait_ge(s_stp[b].h, stm[o - 2])
            for th in range(2):
                SE.activation(out=(scr if th == 0 else scr2)[:, b * 512:b * 512 + 512],
                              in_=pb[th][:, 0:512], func=AF.Identity,
                              bias=bbcol[:, o:o + 1], scale=1.0)
            ns = s_done()
            VE.wait_ge(s_s.h, ns)
            VE.wait_ge(s_pe.h, npe)
            VE.scalar_tensor_tensor(out=scr[:, b * 512:b * 512 + 512],
                                    in0=pbc1[:, 0:512], scalar=0.5,
                                    in1=scr[:, b * 512:b * 512 + 512],
                                    op0=ALU.mult, op1=ALU.add)
            VE.scalar_tensor_tensor(out=scr2[:, b * 512:b * 512 + 512],
                                    in0=pbc1[:, 512:1024], scalar=0.5,
                                    in1=scr2[:, b * 512:b * 512 + 512],
                                    op0=ALU.mult, op1=ALU.add)
            VE.drain()
            VE.wait_ge(s_v.h, xbm[o])
            VE.tensor_tensor(out=scr[:, b * 512:b * 512 + 512],
                             in0=scr[:, b * 512:b * 512 + 512],
                             in1=xst[:, b * TOK:b * TOK + 512], op=ALU.add)
            VE.tensor_tensor(out=scr2[:, b * 512:b * 512 + 512],
                             in0=scr2[:, b * 512:b * 512 + 512],
                             in1=xst[:, b * TOK + 512:b * TOK + 1024], op=ALU.add)
            nvp = v_done()
            pend = (o, nvp)
            VE.tensor_copy(out=cpb[:], in_=scr[:, b * 512:b * 512 + 512])
            nv = v_done()
            SE.wait_ge(s_v.h, nvp)
            SE.activation(out=sqb[:], in_=scr[:, b * 512:b * 512 + 512],
                          func=AF.Square, scale=1.0)
            ns = s_done()
            PE.wait_ge(s_v.h, nv)
            PE.wait_ge(s_s.h, ns)
            PE.matmul(out=pst1[0:1, :], lhsT=onescol[:], rhs=cpb[:],
                      start=(o == 0), stop=(o == 31))
            mm = PE.matmul(out=pst2[0:1, :], lhsT=onescol[:], rhs=sqb[:],
                           start=(o == 0), stop=(o == 31))
            pe_mark(mm)
            VE.wait_ge(s_pe.h, s_pe.n)
            VE.tensor_copy(out=cpb[:], in_=scr2[:, b * 512:b * 512 + 512])
            nv = v_done()
            SE.wait_ge(s_pe.h, s_pe.n)
            SE.activation(out=sqb[:], in_=scr2[:, b * 512:b * 512 + 512],
                          func=AF.Square, scale=1.0)
            ns = s_done()
            PE.wait_ge(s_v.h, nv)
            PE.wait_ge(s_s.h, ns)
            PE.matmul(out=pb[2][0:1, :], lhsT=onescol[:], rhs=cpb[:],
                      start=(o == 0), stop=(o == 31))
            mm = PE.matmul(out=pb[3][0:1, :], lhsT=onescol[:], rhs=sqb[:],
                           start=(o == 0), stop=(o == 31))
            pe_mark(mm)
        po, nvp_p = pend
        pb_ = po % 2
        SY.wait_ge(s_v.h, nvp_p)
        SY.dma_start(out=xpre.rearrange("(c p) t -> c p t", p=128)[po, :, 0:512],
                     in_=scr[:, pb_ * 512:pb_ * 512 + 512]).then_inc(s_stp[pb_].h, 16)
        SY.dma_start(out=xpre.rearrange("(c p) t -> c p t", p=128)[po, :, 512:1024],
                     in_=scr2[:, pb_ * 512:pb_ * 512 + 512]).then_inc(s_stp[pb_].h, 16)
        stm[po] = s_stp[pb_].inc(32)
        npe = s_pe.n
        ln_rows_half(V, EPS_OUT, npe, 0, pst1[0:1, :], pst2[0:1, :])
        ln_rows_half(V, EPS_OUT, npe, 1, pb[2][0:1, :], pb[3][0:1, :])
        nv = v_done()
        PE.wait_ge(s_v.h, nv)
        for th in range(2):
            PE.matmul(out=pbc1[:, th * 512:(th + 1) * 512], lhsT=onesrow[:],
                      rhs=rowsA[0:1, th * 512:(th + 1) * 512], start=True, stop=True)
            mm = PE.matmul(out=(pb[2] if th == 0 else pb[3])[:, 0:512],
                           lhsT=onesrow[:],
                           rhs=rowsB[0:1, th * 512:(th + 1) * 512],
                           start=True, stop=True)
        npe_bc = pe_mark(mm)
        for sp in s_stp:
            SY.wait_ge(sp.h, sp.n)
            VE.wait_ge(sp.h, sp.n)
        vm = {}
        fsm = {}
        pend = None
        for o in range(32):
            b = o % 2
            ob = scr if b == 0 else scr2
            if o >= 2:
                SY.wait_ge(s_v.h, vm[o - 2])
            nx = ldp(s_xp[b], xst[:, b * TOK:(b + 1) * TOK],
                     xpre.rearrange("(c p) t -> c p t", p=128)[o])
            if pend is not None:
                po = pend
                pb_ = po % 2
                SY.wait_ge(s_v.h, vm[po])
                SY.dma_start(out=xout.rearrange("(c p) t -> c p t", p=128)[po],
                             in_=(scr if pb_ == 0 else scr2)[:]).then_inc(s_stp[pb_].h, 16)
                fsm[po] = s_stp[pb_].inc(16)
                pend = None
            VE.wait_ge(s_xp[b].h, nx)
            VE.wait_ge(s_pe.h, npe_bc)
            if o >= 2:
                VE.wait_ge(s_stp[b].h, fsm[o - 2])
            VE.tensor_tensor(out=ob[:], in0=xst[:, b * TOK:(b + 1) * TOK],
                             in1=pbc1[:], op=ALU.mult)
            VE.drain()
            VE.tensor_tensor(out=ob[:, 0:512], in0=ob[:, 0:512],
                             in1=pb[2][:, 0:512], op=ALU.add)
            VE.tensor_tensor(out=ob[:, 512:1024], in0=ob[:, 512:1024],
                             in1=pb[3][:, 0:512], op=ALU.add)
            vm[o] = v_done()
            pend = o
        po = pend
        pb_ = po % 2
        SY.wait_ge(s_v.h, vm[po])
        SY.dma_start(out=xout.rearrange("(c p) t -> c p t", p=128)[po],
                     in_=(scr if pb_ == 0 else scr2)[:]).then_inc(s_stp[pb_].h, 16)
        fsm[po] = s_stp[pb_].inc(16)
        for sp in s_stp:
            SY.wait_ge(sp.h, sp.n)
            g.wait_ge(sp.h, sp.n)

    mu8 = sub_sb.enter_context(nc.sbuf_tensor(P_ + "mu8", [128, 2, 3, 40], U8))

    def emit_wrapped(g):
        if env is not None:
            for eng in (PE, VE, SE, SY, GE):
                for (sh, sv) in env.get("entry_deps", []):
                    eng.wait_ge(sh, sv)
        emit(g)
        sub_sb.close()
        if env is not None:
            env["entry_deps"] = ([(s_v.h, s_v.n), (s_s.h, s_s.n), (s_pe.h, s_pe.n),
                                  (s_st.h, s_st.n), (s_ld.h, s_ld.n)] +
                                 [(sp.h, sp.n) for sp in s_lp + s_xp + s_stp])

    if env is None:
        with nc.Block() as block:
            @block.gpsimd
            def _(g):
                emit_wrapped(g)
        return nc
    return emit_wrapped


def make_FT():
    n = V
    t = np.arange(n, dtype=np.float64)
    FT = np.empty((n, n), dtype=np.float32)
    FT[0, :] = 1.0 / np.sqrt(n)
    maxf = (n - 1) // 2
    ps = 1.0 / np.sqrt(n / 2.0)
    f = np.arange(1, maxf + 1, dtype=np.float64)
    ang = 2.0 * np.pi * np.outer(f, t) / n
    FT[1:1 + 2 * maxf:2, :] = (np.cos(ang) * ps).astype(np.float32)
    FT[2:2 + 2 * maxf:2, :] = (np.sin(ang) * ps).astype(np.float32)
    FT[n - 1, :] = ((-1.0) ** t / np.sqrt(n)).astype(np.float32)
    return FT


_FT_CACHE = [None]


def hostprep_B(inputs, core, x1):
    b, half = core // 2, core % 2
    if _FT_CACHE[0] is None:
        _FT_CACHE[0] = make_FT().astype(ml_dtypes.bfloat16)
    im = {
        "xin": (np.zeros((V, TOK), np.float32) if x1 is None else
                np.ascontiguousarray(_f32(x1[b, half * TOK:(half + 1) * TOK, :]).T)),
        "WbT": np.ascontiguousarray(_f32(inputs["L0_ffn_W_base"]).T),
        "bb_c": colpack(inputs["L0_ffn_b_base"], 32),
        "encT": np.ascontiguousarray(np.concatenate(
            [_f32(inputs["L0_ffn_enc0"]).T, _f32(inputs["L0_ffn_enc1"]).T],
            axis=1).reshape(32, 128, 40).transpose(1, 0, 2).reshape(128, 1280)),
        "FT": _FT_CACHE[0],
        "coefs0": _f32(inputs["L0_ffn_coefs0"]),
        "coefs1": _f32(inputs["L0_ffn_coefs1"]),
        "mask0": np.ascontiguousarray(np.asarray(inputs["L0_ffn_mask0"]).view(np.uint8)),
        "mask1": np.ascontiguousarray(np.asarray(inputs["L0_ffn_mask1"]).view(np.uint8)),
        "onesrow_in": np.ones((1, TOK), np.float32),
        "hsel": np.broadcast_to(
            np.array([1.0 - half, float(half)], np.float32), (128, 2)).copy(),
    }
    return im


def run_stageB(inputs, x1):
    nc = build_stageB()
    ims = [hostprep_B(inputs, c, x1) for c in range(NCORES)]
    res = run_bass_kernel_spmd(nc, ims, core_ids=list(range(NCORES)))
    out = np.empty((B, L, V), np.float32)
    for c in range(NCORES):
        b, half = c // 2, c % 2
        out[b, half * TOK:(half + 1) * TOK, :] = res.results[c]["xout"].T
    return out





def build_stageD(env=None):
    if env is None:
        ctx = contextlib.ExitStack()
        nc = bass.Bass("TRN2", target_bir_lowering=False)
        P_ = ""
    else:
        ctx = env["ctx"]
        nc = env["nc"]
        P_ = "D_"
    dt = lambda name, *a, **k: nc.dram_tensor(P_ + name, *a, **k)

    if env is None:
        xin = dt("xin", [V, TOK], F32, kind="ExternalInput").ap()
    else:
        xin = env["x3d"]
    AwT = dt("AwT", [V, 512], F32, kind="ExternalInput").ap()
    BwT = dt("BwT", [512, V], F32, kind="ExternalInput").ap()
    WqT = dt("WqT", [V, 16], F32, kind="ExternalInput").ap()
    keysT = dt("keysT", [16, 64], F32, kind="ExternalInput").ap()
    ident = dt("ident", [128, 128], F32, kind="ExternalInput").ap()
    expT = dt("expT", [64, 512], F32, kind="ExternalInput").ap()
    onesrow_in = dt("onesrow_in", [1, TOK], F32, kind="ExternalInput").ap()
    xout = dt("xout", [V, TOK], F32, kind="ExternalOutput").ap()
    xpre = dt("xpre", [V, TOK], F32).ap()

    sub_sb = contextlib.ExitStack()
    sb = lambda name, shape, dtp=F32: sub_sb.enter_context(nc.sbuf_tensor(P_ + name, shape, dtp))

    x2b = sb("x2b", [128, 32 * TOK], BF16)
    x2f = sb("x2f", [128, 2 * TOK])
    wqs = sb("wqs", [128, 2 * 16])
    keyss = sb("keyss", [16, 64])
    idents = sb("idents", [128, 128])
    expTs = sb("expTs", [64, 512])
    qrs = sb("qrs", [16, TOK])
    scs = sb("scs", [64, TOK])
    stm = sb("stm", [128, 8, 64])
    work = sb("work", [128, 64])
    work2 = sb("work2", [128, 64])
    mx1 = sb("mx1", [128, 8])
    mx2 = sb("mx2", [128, 8])
    mneg = sb("mneg", [128, 1])
    expo = sb("expo", [128, 64])
    sel = sb("sel", [128, 64])
    den = sb("den", [128, 1])
    gtm = sb("gtm", [128, 8, 64])
    gfm = sb("gfm", [64, TOK])
    zgb = sb("zgb", [128, 4 * TOK], BF16)
    wst = sb("wst", [128, 4096])
    wsb = sb("wsb", [128, 4096], BF16)
    xst = sb("xst", [128, TOK])
    scr = sb("scr", [128, TOK])
    scr2 = sb("scr2", [128, TOK])
    sqb = sb("sqb", [128, 512], BF16)
    cpb = sb("cpb", [128, 512], BF16)
    onescol = sb("onescol", [128, 1], BF16)
    onescolf = sb("onescolf", [128, 1])
    onesrow = sb("onesrow", [1, 128])
    rowsA = sb("rowsA", [1, TOK])
    rowsB = sb("rowsB", [2, TOK])
    srow = sb("srow", [1, 512])
    srow2 = sb("srow2", [1, 512])

    if env is None:
        psz = lambda name, shape: ctx.enter_context(nc.psum_tensor(name, shape, F32))
        pb = [psz(f"pb{i}", [128, 512]) for i in range(4)]
        pbc1 = psz("pbc1", [128, TOK])
        pst1 = psz("pst1", [1, 512])
        pst2 = psz("pst2", [1, 512])
    else:
        pb, pbc1, pst1, pst2 = env["pb"], env["pbc1"], env["pst1"], env["pst2"]

    s_ld = Sem(ctx, nc, P_ + "s_ld")
    s_pe = Sem(ctx, nc, P_ + "s_pe")
    s_v = Sem(ctx, nc, P_ + "s_v")
    s_s = Sem(ctx, nc, P_ + "s_s")
    s_st = Sem(ctx, nc, P_ + "s_st")

    PE, VE, SE, GE, SY = nc.tensor, nc.vector, nc.scalar, nc.gpsimd, nc.sync

    def v_done():
        VE.drain().then_inc(s_v.h, 1)
        return s_v.inc()

    def s_done():
        SE.drain().then_inc(s_s.h, 1)
        return s_s.inc()

    def pe_mark(inst):
        inst.then_inc(s_pe.h, 1)
        return s_pe.inc()

    def ld(dst, src):
        SY.dma_start(out=dst, in_=src).then_inc(s_ld.h, 16)
        return s_ld.inc(16)

    def gate():
        SY.wait_ge(s_v.h, s_v.n)
        SY.wait_ge(s_s.h, s_s.n)
        SY.wait_ge(s_pe.h, s_pe.n)

    def ln_rows_half(nfeat, eps, npe, th, src1, src2):
        sA = rowsA[0:1, th * 512:(th + 1) * 512]
        sB = rowsB[0:1, th * 512:(th + 1) * 512]
        sm = srow[0:1, :]
        sm2 = srow2[0:1, :]
        VE.wait_ge(s_pe.h, npe)
        VE.tensor_scalar(out=sm, in0=src1, scalar1=1.0 / nfeat, scalar2=None,
                         op0=ALU.mult)
        VE.tensor_scalar(out=sA, in0=src2, scalar1=1.0 / nfeat, scalar2=None,
                         op0=ALU.mult)
        VE.drain()
        VE.tensor_tensor(out=sm2, in0=sm, in1=sm, op=ALU.mult)
        VE.drain()
        VE.tensor_tensor(out=sA, in0=sA, in1=sm2, op=ALU.subtract)
        VE.drain()
        VE.tensor_scalar(out=sA, in0=sA, scalar1=float(eps), scalar2=None,
                         op0=ALU.add)
        nv = v_done()
        SE.wait_ge(s_v.h, nv)
        SE.activation(out=sA, in_=sA, func=AF.Sqrt, scale=1.0)
        ns = s_done()
        VE.wait_ge(s_s.h, ns)
        VE.reciprocal(out=sA, in_=sA)
        VE.drain()
        VE.tensor_tensor(out=sm2, in0=sm, in1=sA, op=ALU.mult)
        VE.drain()
        VE.tensor_scalar(out=sB, in0=sm2, scalar1=-1.0, scalar2=None, op0=ALU.mult)
        return v_done()

    def emit(g):
        n0 = ld(onesrow[:], onesrow_in[0:1, 0:128])
        n0 = ld(rowsB[1:2, :], onesrow_in)
        n0 = ld(keyss[:], keysT)
        n0 = ld(idents[:], ident)
        n0 = ld(expTs[:], expT)
        VE.wait_ge(s_ld.h, n0)
        VE.memset(onescolf[:], 1.0)
        VE.drain()
        VE.tensor_copy(out=onescol[:], in_=onescolf[:])
        v_done()

        for i in range(32):
            gate()
            nx = ld(x2f[:, 0:TOK], xin.rearrange("(c p) t -> c p t", p=128)[i])
            nw = ld(wqs[:, 0:16], WqT.rearrange("(c p) r -> c p r", p=128)[i])
            VE.wait_ge(s_ld.h, max(nx, nw))
            if i >= 1:
                VE.wait_ge(s_pe.h, s_pe.n)
            VE.tensor_copy(out=x2b[:, i * TOK:(i + 1) * TOK], in_=x2f[:, 0:TOK])
            nv = v_done()
            PE.wait_ge(s_v.h, nv)
            for th in range(2):
                mm = PE.matmul(out=(pb[0] if th == 0 else pb[1])[0:16, :],
                               lhsT=wqs[:, 0:16],
                               rhs=x2f[:, th * 512:(th + 1) * 512],
                               start=(i == 0), stop=(i == 31))
            npe = pe_mark(mm)
        SE.wait_ge(s_pe.h, npe)
        for th in range(2):
            SE.activation(out=qrs[:, th * 512:(th + 1) * 512],
                          in_=(pb[0] if th == 0 else pb[1])[0:16, :],
                          func=AF.Copy, scale=1.0)
        ns = s_done()
        PE.wait_ge(s_s.h, ns)
        for th in range(2):
            mm = PE.matmul(out=(pb[0] if th == 0 else pb[1])[0:64, :], lhsT=keyss[:],
                           rhs=qrs[:, th * 512:(th + 1) * 512], start=True, stop=True)
        npe = pe_mark(mm)
        SE.wait_ge(s_pe.h, npe)
        for th in range(2):
            SE.activation(out=scs[:, th * 512:(th + 1) * 512],
                          in_=(pb[0] if th == 0 else pb[1])[0:64, :],
                          func=AF.Copy, scale=1.0)
        ns = s_done()
        PE.wait_ge(s_s.h, ns)
        for tt8 in range(8):
            mm = PE.transpose(out=pb[2][:, 0:64], in_=scs[0:64, tt8 * 128:(tt8 + 1) * 128],
                              identity=idents[0:64, 0:64])
            npe = pe_mark(mm)
            VE.wait_ge(s_pe.h, npe)
            VE.tensor_copy(out=stm[:, tt8, :], in_=pb[2][:, 0:64])
            nv = v_done()
            PE.wait_ge(s_v.h, nv)
        for tt8 in range(8):
            VE.tensor_copy(out=work[:], in_=stm[:, tt8, :])
            VE.drain()
            VE.max(out=mx1[:], in_=work[:])
            VE.drain()
            VE.match_replace(out=work2[:], in_to_replace=mx1[:], in_values=work[:],
                             imm_value=-1e30)
            VE.drain()
            VE.max(out=mx2[:], in_=work2[:])
            VE.drain()
            VE.tensor_scalar(out=mneg[:], in0=mx1[:, 0:1], scalar1=-1.0, scalar2=None,
                             op0=ALU.mult)
            nv = v_done()
            SE.wait_ge(s_v.h, nv)
            SE.activation(out=expo[:], in_=stm[:, tt8, :], func=AF.Exp, scale=1.0,
                          bias=mneg[:])
            ns = s_done()
            VE.wait_ge(s_s.h, ns)
            VE.tensor_scalar(out=sel[:], in0=stm[:, tt8, :], scalar1=mx2[:, 7:8],
                             scalar2=None, op0=ALU.is_ge)
            VE.drain()
            VE.tensor_tensor(out=expo[:], in0=expo[:], in1=sel[:], op=ALU.mult)
            VE.drain()
            VE.tensor_reduce(out=den[:], in_=expo[:], axis=AX.X, op=ALU.add)
            VE.drain()
            VE.reciprocal(out=den[:], in_=den[:])
            VE.drain()
            VE.tensor_scalar(out=gtm[:, tt8, :], in0=expo[:], scalar1=den[:],
                             scalar2=None, op0=ALU.mult)
            v_done()
        for tt8 in range(8):
            nv = v_done()
            PE.wait_ge(s_v.h, nv)
            mm = PE.transpose(out=pb[2][0:64, 0:128], in_=gtm[:, tt8, :], identity=idents[:])
            npe = pe_mark(mm)
            VE.wait_ge(s_pe.h, npe)
            VE.tensor_copy(out=gfm[:, tt8 * 128:(tt8 + 1) * 128], in_=pb[2][0:64, 0:128])
            v_done()
        for c in range(4):
            nv = v_done()
            PE.wait_ge(s_v.h, nv)
            for th in range(2):
                mm = PE.matmul(out=pbc1[:, th * 512:th * 512 + 512],
                               lhsT=expTs[:, c * 128:(c + 1) * 128],
                               rhs=gfm[:, th * 512:(th + 1) * 512], start=True, stop=True)
            pe_mark(mm)
            for i in range(32):
                gate()
                nw = ld(wst[:, 0:128], AwT.rearrange("(c p) f -> c p f", p=128)
                        [i, :, c * 128:(c + 1) * 128])
                VE.wait_ge(s_ld.h, nw)
                VE.wait_ge(s_pe.h, s_pe.n)
                VE.tensor_copy(out=wsb[:, 0:128], in_=wst[:, 0:128])
                nv = v_done()
                PE.wait_ge(s_v.h, nv)
                for th in range(2):
                    mm = PE.matmul(
                        out=pb[th][:, 0:512], lhsT=wsb[:, 0:128],
                        rhs=x2b[:, i * TOK + th * 512:i * TOK + th * 512 + 512],
                        start=(i == 0), stop=(i == 31))
                npe = pe_mark(mm)
            SE.wait_ge(s_pe.h, npe)
            for th in range(2):
                SE.activation(out=(scr if th == 0 else scr2)[:, 0:512],
                              in_=pb[th][:, 0:512], func=AF.Copy, scale=1.0)
            ns = s_done()
            VE.wait_ge(s_s.h, ns)
            VE.tensor_tensor(out=zgb[:, c * TOK:c * TOK + 512], in0=scr[:, 0:512],
                             in1=pbc1[:, 0:512], op=ALU.mult)
            VE.tensor_tensor(out=zgb[:, c * TOK + 512:(c + 1) * TOK], in0=scr2[:, 0:512],
                             in1=pbc1[:, 512:1024], op=ALU.mult)
            nv = v_done()
            PE.wait_ge(s_v.h, nv)

        for o in range(32):
            gate()
            nw = ld(wst[:, 0:512].rearrange("p (c f) -> p c f", c=4),
                    BwT.rearrange("(c p) f -> p c f", p=128)[:, :, o * 128:(o + 1) * 128])
            nx = ld(xst[:], xin.rearrange("(c p) t -> c p t", p=128)[o])
            VE.wait_ge(s_ld.h, max(nw, nx))
            if o > 0:
                VE.wait_ge(s_pe.h, s_pe.n)
            VE.tensor_copy(out=wsb[:, 0:512], in_=wst[:, 0:512])
            nv = v_done()
            PE.wait_ge(s_v.h, nv)
            for c in range(4):
                for th in range(2):
                    mm = PE.matmul(
                        out=pb[th][:, 0:512],
                        lhsT=wsb[:, c * 128:(c + 1) * 128],
                        rhs=zgb[:, c * TOK + th * 512:c * TOK + th * 512 + 512],
                        start=(c == 0), stop=(c == 3))
            npe = pe_mark(mm)
            SE.wait_ge(s_pe.h, npe)
            SE.wait_ge(s_st.h, s_st.n)
            for th in range(2):
                SE.activation(out=(scr if th == 0 else scr2)[:, 0:512],
                              in_=pb[th][:, 0:512], func=AF.Copy, scale=2.0)
            ns = s_done()
            VE.wait_ge(s_s.h, ns)
            VE.tensor_tensor(out=scr[:, 0:512], in0=scr[:, 0:512], in1=xst[:, 0:512],
                             op=ALU.add)
            VE.tensor_tensor(out=scr2[:, 0:512], in0=scr2[:, 0:512], in1=xst[:, 512:1024],
                             op=ALU.add)
            nvp = v_done()
            SY.wait_ge(s_v.h, nvp)
            SY.dma_start(out=xpre.rearrange("(c p) t -> c p t", p=128)[o, :, 0:512],
                         in_=scr[:, 0:512]).then_inc(s_st.h, 16)
            SY.dma_start(out=xpre.rearrange("(c p) t -> c p t", p=128)[o, :, 512:1024],
                         in_=scr2[:, 0:512]).then_inc(s_st.h, 16)
            s_st.inc(32)
            VE.tensor_copy(out=cpb[:], in_=scr[:, 0:512])
            nv = v_done()
            SE.wait_ge(s_v.h, nvp)
            SE.activation(out=sqb[:], in_=scr[:, 0:512], func=AF.Square, scale=1.0)
            ns = s_done()
            PE.wait_ge(s_v.h, nv)
            PE.wait_ge(s_s.h, ns)
            PE.matmul(out=pst1[0:1, :], lhsT=onescol[:], rhs=cpb[:],
                      start=(o == 0), stop=(o == 31))
            mm = PE.matmul(out=pst2[0:1, :], lhsT=onescol[:], rhs=sqb[:],
                           start=(o == 0), stop=(o == 31))
            pe_mark(mm)
            VE.wait_ge(s_pe.h, s_pe.n)
            VE.tensor_copy(out=cpb[:], in_=scr2[:, 0:512])
            nv = v_done()
            SE.wait_ge(s_pe.h, s_pe.n)
            SE.activation(out=sqb[:], in_=scr2[:, 0:512], func=AF.Square, scale=1.0)
            ns = s_done()
            PE.wait_ge(s_v.h, nv)
            PE.wait_ge(s_s.h, ns)
            PE.matmul(out=pb[2][0:1, :], lhsT=onescol[:], rhs=cpb[:],
                      start=(o == 0), stop=(o == 31))
            mm = PE.matmul(out=pb[3][0:1, :], lhsT=onescol[:], rhs=sqb[:],
                           start=(o == 0), stop=(o == 31))
            pe_mark(mm)
        npe = s_pe.n
        ln_rows_half(V, EPS_OUT, npe, 0, pst1[0:1, :], pst2[0:1, :])
        ln_rows_half(V, EPS_OUT, npe, 1, pb[2][0:1, :], pb[3][0:1, :])
        nv = v_done()
        PE.wait_ge(s_v.h, nv)
        SY.wait_ge(s_st.h, s_st.n)
        for o in range(32):
            mm = None
            for th in range(2):
                PE.matmul(out=pbc1[:, th * 512:(th + 1) * 512], lhsT=onesrow[:],
                          rhs=rowsA[0:1, th * 512:(th + 1) * 512], start=True, stop=True)
                mm = PE.matmul(out=(pb[2] if th == 0 else pb[3])[:, 0:512],
                               lhsT=onesrow[:],
                               rhs=rowsB[0:1, th * 512:(th + 1) * 512],
                               start=True, stop=True)
            npe = pe_mark(mm)
            gate()
            nx = ld(xst[:], xpre.rearrange("(c p) t -> c p t", p=128)[o])
            VE.wait_ge(s_ld.h, nx)
            VE.wait_ge(s_pe.h, npe)
            VE.wait_ge(s_st.h, s_st.n)
            VE.tensor_tensor(out=scr[:], in0=xst[:], in1=pbc1[:], op=ALU.mult)
            VE.drain()
            VE.tensor_tensor(out=scr[:, 0:512], in0=scr[:, 0:512],
                             in1=pb[2][:, 0:512], op=ALU.add)
            VE.tensor_tensor(out=scr[:, 512:1024], in0=scr[:, 512:1024],
                             in1=pb[3][:, 0:512], op=ALU.add)
            nv = v_done()
            SY.wait_ge(s_v.h, nv)
            SY.dma_start(out=xout.rearrange("(c p) t -> c p t", p=128)[o],
                         in_=scr[:]).then_inc(s_st.h, 16)
            s_st.inc(16)
            PE.wait_ge(s_v.h, nv)
        SY.wait_ge(s_st.h, s_st.n)
        g.wait_ge(s_st.h, s_st.n)

    def emit_wrapped(g):
        if env is not None:
            for eng in (PE, VE, SE, SY, GE):
                for (sh, sv) in env.get("entry_deps", []):
                    eng.wait_ge(sh, sv)
        emit(g)
        sub_sb.close()

    if env is None:
        with nc.Block() as block:
            @block.gpsimd
            def _(g):
                emit_wrapped(g)
        return nc
    return emit_wrapped


def hostprep_D(inputs, core, x2):
    b, half = core // 2, core % 2
    expm = np.zeros((64, 512), np.float32)
    for e in range(64):
        expm[e, e * 8:(e + 1) * 8] = 1.0
    im = {
        "xin": (np.zeros((V, TOK), np.float32) if x2 is None else
                np.ascontiguousarray(_f32(x2[b, half * TOK:(half + 1) * TOK, :]).T)),
        "AwT": np.ascontiguousarray(_f32(inputs["L1_ffn_A_w"]).T),
        "BwT": np.ascontiguousarray(_f32(inputs["L1_ffn_B_w"]).T),
        "WqT": np.ascontiguousarray(_f32(inputs["L1_ffn_Wq_w"]).T),
        "keysT": np.ascontiguousarray(_f32(inputs["L1_ffn_keys"]).T),
        "ident": np.eye(128, dtype=np.float32),
        "expT": expm,
        "onesrow_in": np.ones((1, TOK), np.float32),
    }
    return im


def run_stageD(inputs, x2):
    nc = build_stageD()
    ims = [hostprep_D(inputs, c, x2) for c in range(NCORES)]
    res = run_bass_kernel_spmd(nc, ims, core_ids=list(range(NCORES)))
    out = np.empty((B, L, V), np.float32)
    for c in range(NCORES):
        b, half = c // 2, c % 2
        out[b, half * TOK:(half + 1) * TOK, :] = res.results[c]["xout"].T
    return out





def build_stageC(stop=99, env=None):
    if env is None:
        ctx = contextlib.ExitStack()
        nc = bass.Bass("TRN2", target_bir_lowering=False)
        P_ = ""
    else:
        ctx = env["ctx"]
        nc = env["nc"]
        P_ = "C_"
    dt = lambda name, *a, **k: nc.dram_tensor(P_ + name, *a, **k)

    if env is None:
        xin = dt("xin", [V, TOK], F32, kind="ExternalInput").ap()
    else:
        xin = env["x2d"]
    WqT = dt("WqT", [V, V], F32, kind="ExternalInput").ap()
    WkT = dt("WkT", [V, V], F32, kind="ExternalInput").ap()
    WvT = dt("WvT", [V, V], F32, kind="ExternalInput").ap()
    WoT = dt("WoT", [V, V], F32, kind="ExternalInput").ap()
    bq_c = dt("bq_c", [128, 32], F32, kind="ExternalInput").ap()
    bk_c = dt("bk_c", [128, 32], F32, kind="ExternalInput").ap()
    bvrow = dt("bvrow", [1, V], F32, kind="ExternalInput").ap()
    bo_c = dt("bo_c", [128, 32], F32, kind="ExternalInput").ap()
    gqrow_in = dt("gqrow_in", [1, 128], F32, kind="ExternalInput").ap()
    gkrow_in = dt("gkrow_in", [1, 128], F32, kind="ExternalInput").ap()
    ident = dt("ident", [128, 128], F32, kind="ExternalInput").ap()
    onesrow_in = dt("onesrow_in", [1, TOK], F32, kind="ExternalInput").ap()
    ep2 = dt("ep2", [128, 512], I32, kind="ExternalInput").ap()
    if env is None:
        xout = dt("xout", [V, TOK], F32, kind="ExternalOutput").ap()
    else:
        xout = env["x3d"]
    xpre = dt("xpre", [V, TOK], F32).ap()
    q_dram = dt("q_dram", [V, TOK], BF16).ap()
    k_dram = dt("k_dram", [V, TOK], BF16).ap()
    v_dram = dt("v_dram", [TOK, V], BF16).ap()
    attn_dram = dt("attn_dram", [V, TOK], BF16).ap()
    k2_dram = dt("k2_dram", [V, TOK], BF16).ap()
    v2_dram = dt("v2_dram", [TOK, V], BF16).ap()
    ksh = dt("ksh", [2, V, TOK], BF16, addr_space="Shared").ap()
    vsh = dt("vsh", [2, TOK, V], BF16, addr_space="Shared").ap()
    kfl = dt("kfl", [2, 128, 512], I32, addr_space="Shared").ap()

    sub_sb = contextlib.ExitStack()
    sb = lambda name, shape, dtp=F32: sub_sb.enter_context(nc.sbuf_tensor(P_ + name, shape, dtp))

    xst = sb("xst", [128, TOK])
    scr = sb("scr", [128, TOK])
    scr2 = sb("scr2", [128, TOK])
    sqb = sb("sqb", [128, 512], BF16)
    cpb = sb("cpb", [128, 512], BF16)
    onescol = sb("onescol", [128, 1], BF16)
    onescolf = sb("onescolf", [128, 1])
    onesrow = sb("onesrow", [1, 128])
    gqrow = sb("gqrow", [1, 128])
    gkrow = sb("gkrow", [1, 128])
    idents = sb("idents", [128, 128])
    bqcol = sb("bqcol", [128, 32])
    bkcol = sb("bkcol", [128, 32])
    bocol = sb("bocol", [128, 32])
    bvb = sb("bvb", [128, 512])
    bvs = sb("bvs", [1, V])
    ep2s = sb("ep2s", [128, 512], I32)
    flsb = sb("flsb", [1, 8], I32)
    rowsA = sb("rowsA", [1, TOK])
    rowsB = sb("rowsB", [2, TOK])
    srow = sb("srow", [1, 512])
    srow2 = sb("srow2", [1, 512])
    rsrow = sb("rsrow", [1, TOK])
    negB = sb("negB", [128, 1])
    stat16 = sb("stat16", [128, 16])
    col16 = sb("col16", [16, 1])
    bmax = sb("bmax", [1, 16])
    kb = sb("kb", [128, TOK], BF16)
    wst = sb("wst", [128, 4096])
    wsb = sb("wsb", [128, 4096], BF16)

    if env is None:
        psz = lambda name, shape: ctx.enter_context(nc.psum_tensor(name, shape, F32))
        pA = psz("pA", [128, 512])
        pB = psz("pB", [128, 512])
        pC = psz("pC", [128, 512])
        pD = psz("pD", [128, 512])
        pE1 = psz("pE1", [1, 512])
        pE2 = psz("pE2", [1, 512])
        pF = psz("pF", [128, TOK])
    else:
        pA, pB, pC, pD = env["pb"]
        pE1, pE2, pF = env["pst1"], env["pst2"], env["pbc1"]

    s_ld = Sem(ctx, nc, P_ + "s_ld")
    s_pe = Sem(ctx, nc, P_ + "s_pe")
    s_v = Sem(ctx, nc, P_ + "s_v")
    s_s = Sem(ctx, nc, P_ + "s_s")
    s_st = Sem(ctx, nc, P_ + "s_st")
    s_g = Sem(ctx, nc, P_ + "s_g")
    s_bar = Sem(ctx, nc, P_ + "s_bar")
    s_rt = Sem(ctx, nc, P_ + "s_rt")

    PE, VE, SE, GE, SY = nc.tensor, nc.vector, nc.scalar, nc.gpsimd, nc.sync

    def v_done():
        VE.drain().then_inc(s_v.h, 1)
        return s_v.inc()

    def s_done():
        SE.drain().then_inc(s_s.h, 1)
        return s_s.inc()

    def pe_mark(inst):
        inst.then_inc(s_pe.h, 1)
        return s_pe.inc()

    def ld(dst, src):
        SY.dma_start(out=dst, in_=src).then_inc(s_ld.h, 16)
        return s_ld.inc(16)

    def gate():
        SY.wait_ge(s_v.h, s_v.n)
        SY.wait_ge(s_s.h, s_s.n)
        SY.wait_ge(s_pe.h, s_pe.n)

    def ln_rows_half(nfeat, eps, npe, th, src1, src2):
        sA = rowsA[0:1, th * 512:(th + 1) * 512]
        sB = rowsB[0:1, th * 512:(th + 1) * 512]
        sm = srow[0:1, :]
        sm2 = srow2[0:1, :]
        VE.wait_ge(s_pe.h, npe)
        VE.tensor_scalar(out=sm, in0=src1, scalar1=1.0 / nfeat, scalar2=None,
                         op0=ALU.mult)
        VE.tensor_scalar(out=sA, in0=src2, scalar1=1.0 / nfeat, scalar2=None,
                         op0=ALU.mult)
        VE.drain()
        VE.tensor_tensor(out=sm2, in0=sm, in1=sm, op=ALU.mult)
        VE.drain()
        VE.tensor_tensor(out=sA, in0=sA, in1=sm2, op=ALU.subtract)
        VE.drain()
        VE.tensor_scalar(out=sA, in0=sA, scalar1=float(eps), scalar2=None,
                         op0=ALU.add)
        nv = v_done()
        SE.wait_ge(s_v.h, nv)
        SE.activation(out=sA, in_=sA, func=AF.Sqrt, scale=1.0)
        ns = s_done()
        VE.wait_ge(s_s.h, ns)
        VE.reciprocal(out=sA, in_=sA)
        VE.drain()
        VE.tensor_tensor(out=sm2, in0=sm, in1=sA, op=ALU.mult)
        VE.drain()
        VE.tensor_scalar(out=sB, in0=sm2, scalar1=-1.0, scalar2=None, op0=ALU.mult)
        return v_done()

    HD = 128
    NH = 32
    SCALE = float(1.0 / np.sqrt(HD))
    EPS_R = 1e-6

    def emit(g):
        n0 = ld(onesrow[:], onesrow_in[0:1, 0:128])
        n0 = ld(rowsB[1:2, :], onesrow_in)
        n0 = ld(gqrow[:], gqrow_in)
        n0 = ld(gkrow[:], gkrow_in)
        n0 = ld(idents[:], ident)
        n0 = ld(bqcol[:], bq_c)
        n0 = ld(bkcol[:], bk_c)
        n0 = ld(bocol[:], bo_c)
        n0 = ld(ep2s[:], ep2)
        n0 = ld(bvs[:], bvrow)
        VE.wait_ge(s_ld.h, n0)
        VE.memset(onescolf[:], 1.0)
        VE.drain()
        VE.tensor_copy(out=onescol[:], in_=onescolf[:])
        v_done()

        sub = contextlib.ExitStack()
        x2b = sub.enter_context(nc.sbuf_tensor("x2b", [128, 32 * TOK], BF16))
        for i in range(32):
            gate()
            nx = ld(xst[:], xin.rearrange("(c p) t -> c p t", p=128)[i])
            VE.wait_ge(s_ld.h, nx)
            VE.tensor_copy(out=x2b[:, i * TOK:(i + 1) * TOK], in_=xst[:])
            v_done()

        def proj_qk(WT, bcol, grow, out_dram, eps):
            for h in range(NH):
                gate()
                nw = ld(wst[:, 0:4096].rearrange("p (c f) -> p c f", c=32),
                        WT.rearrange("(c p) f -> p c f", p=128)[:, :, h * 128:(h + 1) * 128])
                VE.wait_ge(s_ld.h, nw)
                VE.wait_ge(s_pe.h, s_pe.n)
                VE.wait_ge(s_s.h, s_s.n)
                VE.tensor_copy(out=wsb[:, 0:4096], in_=wst[:, 0:4096])
                nv = v_done()
                PE.wait_ge(s_v.h, nv)
                for c in range(32):
                    for th in range(2):
                        mm = PE.matmul(
                            out=(pA if th == 0 else pB)[:, 0:512],
                            lhsT=wsb[:, c * 128:(c + 1) * 128],
                            rhs=x2b[:, c * TOK + th * 512:c * TOK + th * 512 + 512],
                            start=(c == 0), stop=(c == 31))
                npe = pe_mark(mm)
                SE.wait_ge(s_pe.h, npe)
                for th in range(2):
                    SE.activation(out=sqb[:], in_=(pA if th == 0 else pB)[:, 0:512],
                                  func=AF.Square, scale=1.0, bias=bcol[:, h:h + 1])
                    ns = s_done()
                    PE.wait_ge(s_s.h, ns)
                    mm2 = PE.matmul(out=(pE1 if th == 0 else pE2)[0:1, :],
                                    lhsT=onescol[:], rhs=sqb[:], start=True, stop=True)
                    pe_mark(mm2)
                npe2 = s_pe.n
                VE.wait_ge(s_pe.h, npe2)
                for th in range(2):
                    sA = rowsA[0:1, th * 512:(th + 1) * 512]
                    VE.tensor_scalar(out=sA, in0=(pE1 if th == 0 else pE2)[0:1, :],
                                     scalar1=1.0 / HD, scalar2=float(eps),
                                     op0=ALU.mult, op1=ALU.add)
                nv = v_done()
                SE.wait_ge(s_v.h, nv)
                SE.activation(out=rowsA[:], in_=rowsA[:], func=AF.Sqrt, scale=1.0)
                ns = s_done()
                VE.wait_ge(s_s.h, ns)
                VE.reciprocal(out=rowsA[:], in_=rowsA[:])
                nv = v_done()
                PE.wait_ge(s_v.h, nv)
                for th in range(2):
                    mm3 = PE.matmul(out=pF[:, th * 512:(th + 1) * 512], lhsT=grow[:],
                                    rhs=rowsA[0:1, th * 512:(th + 1) * 512],
                                    start=True, stop=True)
                npe3 = pe_mark(mm3)
                SE.wait_ge(s_pe.h, npe3)
                for th in range(2):
                    SE.activation(out=(scr if th == 0 else scr2)[:, 0:512],
                                  in_=(pA if th == 0 else pB)[:, 0:512],
                                  func=AF.Identity, scale=1.0, bias=bcol[:, h:h + 1])
                ns = s_done()
                VE.wait_ge(s_s.h, ns)
                VE.tensor_tensor(out=kb[:, 0:512], in0=scr[:, 0:512], in1=pF[:, 0:512],
                                 op=ALU.mult)
                VE.tensor_tensor(out=kb[:, 512:1024], in0=scr2[:, 0:512],
                                 in1=pF[:, 512:1024], op=ALU.mult)
                nv = v_done()
                SY.wait_ge(s_v.h, nv)
                SY.dma_start(out=out_dram.rearrange("(c p) t -> c p t", p=128)[h],
                             in_=kb[:]).then_inc(s_st.h, 16)
                s_st.inc(32) if False else s_st.inc(16)
                SE.wait_ge(s_st.h, s_st.n)
                VE.wait_ge(s_st.h, s_st.n)
                PE.wait_ge(s_v.h, nv)

        proj_qk(WkT, bkcol, gkrow, k_dram, EPS_R)

        for hdg in range(8):
            mm = PE.matmul(out=pF[:, 0:512], lhsT=onesrow[:],
                           rhs=bvs[0:1, hdg * 512:(hdg + 1) * 512], start=True, stop=True)
            npe = pe_mark(mm)
            VE.wait_ge(s_pe.h, npe)
            VE.tensor_copy(out=bvb[:], in_=pF[:, 0:512])
            nv = v_done()
            for half4 in range(2):
                accs = [pA, pB, pC, pD]
                for i in range(32):
                    gate()
                    nw = ld(wst[:, 0:512], WvT.rearrange("(c p) f -> c p f", p=128)
                            [i, :, hdg * 512:(hdg + 1) * 512])
                    VE.wait_ge(s_ld.h, nw)
                    VE.wait_ge(s_pe.h, s_pe.n)
                    VE.tensor_copy(out=wsb[:, 0:512], in_=wst[:, 0:512])
                    nv = v_done()
                    PE.wait_ge(s_v.h, nv)
                    for kc4 in range(4):
                        kc = half4 * 4 + kc4
                        mm = PE.matmul(out=accs[kc4][:, 0:512],
                                       lhsT=x2b[:, i * TOK + kc * 128:i * TOK + (kc + 1) * 128],
                                       rhs=wsb[:, 0:512],
                                       start=(i == 0), stop=(i == 31))
                    npe = pe_mark(mm)
                VE.wait_ge(s_pe.h, npe)
                VE.wait_ge(s_st.h, s_st.n)
                for kc4 in range(4):
                    kc = half4 * 4 + kc4
                    VE.tensor_tensor(out=kb[:, 0:512].bitcast(BF16) if False else vbk[:, kc4 * 512:(kc4 + 1) * 512],
                                     in0=accs[kc4][:, 0:512], in1=bvb[:], op=ALU.add)
                nv = v_done()
                SY.wait_ge(s_v.h, nv)
                for kc4 in range(4):
                    kc = half4 * 4 + kc4
                    SY.dma_start(out=v_dram[kc * 128:(kc + 1) * 128,
                                            hdg * 512:(hdg + 1) * 512],
                                 in_=vbk[:, kc4 * 512:(kc4 + 1) * 512]).then_inc(s_st.h, 16)
                    s_st.inc(16)
                VE.wait_ge(s_st.h, s_st.n)
                PE.wait_ge(s_st.h, s_st.n)
        if stop <= 0:
            SY.wait_ge(s_st.h, s_st.n)
            g.wait_ge(s_st.h, s_st.n)
            return

        SY.wait_ge(s_st.h, s_st.n)
        g.wait_ge(s_st.h, s_st.n)
        pid = nc.partition_id()
        par = pid % 2
        sib = (pid + 1) % 2
        gb = s_g.n
        for part in range(4):
            g.dma_start(out=ksh[par, part * 1024:(part + 1) * 1024, :],
                        in_=k_dram[part * 1024:(part + 1) * 1024, :]).then_inc(s_g.h, 16)
        for part in range(4):
            g.dma_start(out=vsh[par, part * 256:(part + 1) * 256, :],
                        in_=v_dram[part * 256:(part + 1) * 256, :]).then_inc(s_g.h, 16)
        g.wait_ge(s_g.h, s_g.inc(128))
        g.dma_start(out=kfl[par], in_=ep2s[:]).then_inc(s_g.h, 16)
        g.wait_ge(s_g.h, s_g.inc(16))
        if stop <= 1:
            SY.wait_ge(s_st.h, s_st.n)
            return

        ns_now = s_s.n
        g.wait_ge(s_s.h, ns_now)
        bad = g.alloc_register("bad")
        fv = g.alloc_register("fv")
        f2 = g.alloc_register("f2")
        expv = g.alloc_register("expv")
        dlyr = g.alloc_register("dlyr")
        g.load(expv, ep2s[0:1, 0:1])
        g.reg_mov(dlyr, 400)
        with g.While(dlyr):
            g.reg_sub(dlyr, dlyr, 1)
        g.dma_start(out=flsb[0:1, 0:1], in_=kfl[0, 0:1, 0:1]).then_inc(s_g.h, 16)
        g.dma_start(out=flsb[0:1, 1:2], in_=kfl[1, 0:1, 0:1]).then_inc(s_g.h, 16)
        g.wait_ge(s_g.h, s_g.inc(32))
        g.load(fv, flsb[0:1, 0:1])
        g.load(f2, flsb[0:1, 1:2])
        g.reg_alu(fv, fv, expv, op=ALU.bitwise_xor)
        g.reg_alu(f2, f2, expv, op=ALU.bitwise_xor)
        g.reg_alu(bad, fv, f2, op=ALU.bitwise_or)
        g.reg_alu(bad, bad, 0, op=ALU.not_equal)
        for retry in range(3):
            guard = g.If_eq(bad, 1)
            with guard:
                g.reg_mov(dlyr, 1500)
                with g.While(dlyr):
                    g.reg_sub(dlyr, dlyr, 1)
                g.dma_start(out=flsb[0:1, 0:1], in_=kfl[0, 0:1, 0:1]).then_inc(s_rt.h, 16)
                g.dma_start(out=flsb[0:1, 1:2], in_=kfl[1, 0:1, 0:1]).then_inc(s_rt.h, 16)
                g.drain()
                g.load(fv, flsb[0:1, 0:1])
                g.load(f2, flsb[0:1, 1:2])
                g.reg_alu(fv, fv, expv, op=ALU.bitwise_xor)
                g.reg_alu(f2, f2, expv, op=ALU.bitwise_xor)
                g.reg_alu(bad, fv, f2, op=ALU.bitwise_or)
                g.reg_alu(bad, bad, 0, op=ALU.not_equal)
        for part in range(4):
            g.dma_start(out=k2_dram[part * 1024:(part + 1) * 1024, :],
                        in_=ksh[sib, part * 1024:(part + 1) * 1024, :]).then_inc(s_g.h, 16)
        for part in range(4):
            g.dma_start(out=v2_dram[part * 256:(part + 1) * 256, :],
                        in_=vsh[sib, part * 256:(part + 1) * 256, :]).then_inc(s_g.h, 16)
        g.wait_ge(s_g.h, s_g.inc(128))
        guard = g.If_eq(bad, 1)
        with guard:
            g.memset(scr[:], 1.0e30)
        g.sem_inc(s_bar.h, 1)
        nbar = s_bar.inc()

        proj_qk(WqT, bqcol, gqrow, q_dram, EPS_R)
        sub.close()

        if stop <= 2:
            SY.wait_ge(s_bar.h, nbar)
            SY.wait_ge(s_st.h, s_st.n)
            g.wait_ge(s_bar.h, nbar)
            return
        SY.wait_ge(s_bar.h, nbar)
        sub2 = contextlib.ExitStack()
        scsp = sub2.enter_context(nc.sbuf_tensor("scsp", [128, 16 * TOK], F32))
        ptb = sub2.enter_context(nc.sbuf_tensor("ptb", [128, 16 * TOK], BF16))
        qh = sub2.enter_context(nc.sbuf_tensor("qh", [128, TOK], BF16))
        kh = sub2.enter_context(nc.sbuf_tensor("kh", [128, TOK], BF16))
        kh2 = sub2.enter_context(nc.sbuf_tensor("kh2", [128, TOK], BF16))
        vt = sub2.enter_context(nc.sbuf_tensor("vt", [128, 128], BF16))
        ab = sub2.enter_context(nc.sbuf_tensor("ab", [128, TOK], BF16))

        for h in range(NH):
            nq = ld(qh[:], q_dram.rearrange("(c p) t -> c p t", p=128)[h])
            nk = ld(kh[:], k_dram.rearrange("(c p) t -> c p t", p=128)[h])
            nk2 = ld(kh2[:], k2_dram.rearrange("(c p) t -> c p t", p=128)[h])
            PE.wait_ge(s_ld.h, max(nq, nk, nk2))
            if h > 0:
                PE.wait_ge(s_s.h, s_s.n)
            for kc in range(16):
                src = kh if kc < 8 else kh2
                kcl = kc % 8
                for th in range(2):
                    mm = PE.matmul(out=(pA if th == 0 else pB)[:, 0:512],
                                   lhsT=src[:, kcl * 128:(kcl + 1) * 128],
                                   rhs=qh[:, th * 512:(th + 1) * 512],
                                   start=True, stop=True)
                npe = pe_mark(mm)
                SE.wait_ge(s_pe.h, npe)
                for th in range(2):
                    SE.activation(out=scsp[:, kc * TOK + th * 512:kc * TOK + th * 512 + 512],
                                  in_=(pA if th == 0 else pB)[:, 0:512],
                                  func=AF.Copy, scale=SCALE)
                ns = s_done()
                VE.wait_ge(s_s.h, ns)
                VE.reduce_max(out=stat16[:, kc:kc + 1], in_=scsp[:, kc * TOK:(kc + 1) * TOK],
                              axis=AX.X)
                nv = v_done()
                PE.wait_ge(s_v.h, nv)
            nv = v_done()
            PE.wait_ge(s_v.h, nv)
            mm = PE.transpose(out=pC[0:16, 0:128], in_=stat16[:], identity=idents[:])
            npe = pe_mark(mm)
            VE.wait_ge(s_pe.h, npe)
            VE.reduce_max(out=col16[:], in_=pC[0:16, 0:128], axis=AX.X)
            nv = v_done()
            PE.wait_ge(s_v.h, nv)
            mm = PE.transpose(out=pD[0:1, 0:16], in_=col16[:], identity=idents[0:16, 0:16])
            npe = pe_mark(mm)
            VE.wait_ge(s_pe.h, npe)
            VE.reduce_max(out=bmax[0:1, 0:1], in_=pD[0:1, 0:16], axis=AX.X)
            nv = v_done()
            PE.wait_ge(s_v.h, nv)
            mm = PE.matmul(out=pD[:, 0:1], lhsT=onesrow[:], rhs=bmax[0:1, 0:1],
                           start=True, stop=True)
            npe = pe_mark(mm)
            VE.wait_ge(s_pe.h, npe)
            VE.tensor_scalar(out=negB[:], in0=pD[:, 0:1], scalar1=-1.0, scalar2=None,
                             op0=ALU.mult)
            nv = v_done()
            SE.wait_ge(s_v.h, nv)
            for kc in range(16):
                SE.activation(out=ptb[:, kc * TOK:(kc + 1) * TOK],
                              in_=scsp[:, kc * TOK:(kc + 1) * TOK], func=AF.Exp,
                              scale=1.0, bias=negB[:])
            ns = s_done()
            PE.wait_ge(s_s.h, ns)
            for kc in range(16):
                if kc < 8:
                    nv2 = ld(vt[:], v_dram[kc * 128:(kc + 1) * 128, h * 128:(h + 1) * 128])
                else:
                    nv2 = ld(vt[:], v2_dram[(kc - 8) * 128:(kc - 7) * 128,
                                            h * 128:(h + 1) * 128])
                PE.wait_ge(s_ld.h, nv2)
                for th in range(2):
                    PE.matmul(out=(pC if th == 0 else pD)[:, 0:512], lhsT=vt[:],
                              rhs=ptb[:, kc * TOK + th * 512:kc * TOK + th * 512 + 512],
                              start=(kc == 0), stop=(kc == 15))
                    mm = PE.matmul(out=(pE1 if th == 0 else pE2)[0:1, :], lhsT=onescol[:],
                                   rhs=ptb[:, kc * TOK + th * 512:kc * TOK + th * 512 + 512],
                                   start=(kc == 0), stop=(kc == 15))
                npe = pe_mark(mm)
                gate()
                SY.wait_ge(s_pe.h, s_pe.n)
            VE.wait_ge(s_pe.h, npe)
            VE.tensor_copy(out=rsrow[0:1, 0:512], in_=pE1[0:1, :])
            VE.tensor_copy(out=rsrow[0:1, 512:1024], in_=pE2[0:1, :])
            nv = v_done()
            SE.wait_ge(s_v.h, nv)
            SE.activation(out=rsrow[:], in_=rsrow[:], func=AF.Ln, scale=1.0)
            SE.drain()
            SE.activation(out=rsrow[:], in_=rsrow[:], func=AF.Exp, scale=-1.0)
            ns = s_done()
            PE.wait_ge(s_s.h, ns)
            for th in range(2):
                mm = PE.matmul(out=pF[:, th * 512:(th + 1) * 512], lhsT=onesrow[:],
                               rhs=rsrow[0:1, th * 512:(th + 1) * 512], start=True, stop=True)
            npe = pe_mark(mm)
            SE.wait_ge(s_pe.h, npe)
            SE.wait_ge(s_st.h, s_st.n)
            for th in range(2):
                SE.activation(out=(scr if th == 0 else scr2)[:, 0:512],
                              in_=(pC if th == 0 else pD)[:, 0:512], func=AF.Copy, scale=1.0)
            ns = s_done()
            VE.wait_ge(s_s.h, ns)
            VE.tensor_tensor(out=ab[:, 0:512], in0=scr[:, 0:512], in1=pF[:, 0:512],
                             op=ALU.mult)
            VE.tensor_tensor(out=ab[:, 512:1024], in0=scr2[:, 0:512], in1=pF[:, 512:1024],
                             op=ALU.mult)
            nv = v_done()
            SY.wait_ge(s_v.h, nv)
            SY.dma_start(out=attn_dram.rearrange("(c p) t -> c p t", p=128)[h],
                         in_=ab[:]).then_inc(s_st.h, 16)
            s_st.inc(16)
            VE.wait_ge(s_st.h, s_st.n)
            PE.wait_ge(s_v.h, nv)
        sub2.close()
        if stop <= 3:
            SY.wait_ge(s_st.h, s_st.n)
            g.wait_ge(s_st.h, s_st.n)
            return

        sub3 = contextlib.ExitStack()
        atr = sub3.enter_context(nc.sbuf_tensor("atr", [128, 32 * TOK], BF16))
        SY.wait_ge(s_st.h, s_st.n)
        for i in range(32):
            ld(atr[:, i * TOK:(i + 1) * TOK],
               attn_dram.rearrange("(c p) t -> c p t", p=128)[i])
        for o in range(32):
            gate()
            nw = ld(wst[:, 0:4096].rearrange("p (c f) -> p c f", c=32),
                    WoT.rearrange("(c p) f -> p c f", p=128)[:, :, o * 128:(o + 1) * 128])
            nx = ld(xst[:], xin.rearrange("(c p) t -> c p t", p=128)[o])
            VE.wait_ge(s_ld.h, max(nw, nx))
            if o > 0:
                VE.wait_ge(s_pe.h, s_pe.n)
            VE.tensor_copy(out=wsb[:, 0:4096], in_=wst[:, 0:4096])
            nv = v_done()
            PE.wait_ge(s_v.h, nv)
            for c in range(32):
                for th in range(2):
                    mm = PE.matmul(
                        out=(pA if th == 0 else pB)[:, 0:512],
                        lhsT=wsb[:, c * 128:(c + 1) * 128],
                        rhs=atr[:, c * TOK + th * 512:c * TOK + th * 512 + 512],
                        start=(c == 0), stop=(c == 31))
            npe = pe_mark(mm)
            SE.wait_ge(s_pe.h, npe)
            SE.wait_ge(s_st.h, s_st.n)
            for th in range(2):
                SE.activation(out=(scr if th == 0 else scr2)[:, 0:512],
                              in_=(pA if th == 0 else pB)[:, 0:512], func=AF.Identity,
                              bias=bocol[:, o:o + 1], scale=1.0)
            ns = s_done()
            VE.wait_ge(s_s.h, ns)
            VE.tensor_tensor(out=scr[:, 0:512], in0=scr[:, 0:512], in1=xst[:, 0:512],
                             op=ALU.add)
            VE.tensor_tensor(out=scr2[:, 0:512], in0=scr2[:, 0:512], in1=xst[:, 512:1024],
                             op=ALU.add)
            nvp = v_done()
            SY.wait_ge(s_v.h, nvp)
            SY.dma_start(out=xpre.rearrange("(c p) t -> c p t", p=128)[o, :, 0:512],
                         in_=scr[:, 0:512]).then_inc(s_st.h, 16)
            SY.dma_start(out=xpre.rearrange("(c p) t -> c p t", p=128)[o, :, 512:1024],
                         in_=scr2[:, 0:512]).then_inc(s_st.h, 16)
            s_st.inc(32)
            VE.tensor_copy(out=cpb[:], in_=scr[:, 0:512])
            nv = v_done()
            SE.wait_ge(s_v.h, nvp)
            SE.activation(out=sqb[:], in_=scr[:, 0:512], func=AF.Square, scale=1.0)
            ns = s_done()
            PE.wait_ge(s_v.h, nv)
            PE.wait_ge(s_s.h, ns)
            PE.matmul(out=pE1[0:1, :], lhsT=onescol[:], rhs=cpb[:],
                      start=(o == 0), stop=(o == 31))
            mm = PE.matmul(out=pE2[0:1, :], lhsT=onescol[:], rhs=sqb[:],
                           start=(o == 0), stop=(o == 31))
            pe_mark(mm)
            VE.wait_ge(s_pe.h, s_pe.n)
            VE.tensor_copy(out=cpb[:], in_=scr2[:, 0:512])
            nv = v_done()
            SE.wait_ge(s_pe.h, s_pe.n)
            SE.activation(out=sqb[:], in_=scr2[:, 0:512], func=AF.Square, scale=1.0)
            ns = s_done()
            PE.wait_ge(s_v.h, nv)
            PE.wait_ge(s_s.h, ns)
            PE.matmul(out=pC[0:1, :], lhsT=onescol[:], rhs=cpb[:],
                      start=(o == 0), stop=(o == 31))
            mm = PE.matmul(out=pD[0:1, :], lhsT=onescol[:], rhs=sqb[:],
                           start=(o == 0), stop=(o == 31))
            pe_mark(mm)
        npe = s_pe.n
        ln_rows_half(V, EPS_OUT, npe, 0, pE1[0:1, :], pE2[0:1, :])
        ln_rows_half(V, EPS_OUT, npe, 1, pC[0:1, :], pD[0:1, :])
        nv = v_done()
        PE.wait_ge(s_v.h, nv)
        SY.wait_ge(s_st.h, s_st.n)
        for o in range(32):
            mm = None
            for th in range(2):
                PE.matmul(out=pF[:, th * 512:(th + 1) * 512], lhsT=onesrow[:],
                          rhs=rowsA[0:1, th * 512:(th + 1) * 512], start=True, stop=True)
                mm = PE.matmul(out=(pC if th == 0 else pD)[:, 0:512],
                               lhsT=onesrow[:],
                               rhs=rowsB[0:1, th * 512:(th + 1) * 512],
                               start=True, stop=True)
            npe = pe_mark(mm)
            gate()
            nx = ld(xst[:], xpre.rearrange("(c p) t -> c p t", p=128)[o])
            VE.wait_ge(s_ld.h, nx)
            VE.wait_ge(s_pe.h, npe)
            VE.wait_ge(s_st.h, s_st.n)
            VE.tensor_tensor(out=scr[:], in0=xst[:], in1=pF[:], op=ALU.mult)
            VE.drain()
            VE.tensor_tensor(out=scr[:, 0:512], in0=scr[:, 0:512],
                             in1=pC[:, 0:512], op=ALU.add)
            VE.tensor_tensor(out=scr[:, 512:1024], in0=scr[:, 512:1024],
                             in1=pD[:, 0:512], op=ALU.add)
            nv = v_done()
            SY.wait_ge(s_v.h, nv)
            SY.dma_start(out=xout.rearrange("(c p) t -> c p t", p=128)[o],
                         in_=scr[:]).then_inc(s_st.h, 16)
            s_st.inc(16)
            PE.wait_ge(s_v.h, nv)
        SY.wait_ge(s_st.h, s_st.n)
        g.wait_ge(s_st.h, s_st.n)
        sub3.close()

    vbk = sub_sb.enter_context(nc.sbuf_tensor(P_ + "vbk", [128, 2048], BF16))
    kvstg = sub_sb.enter_context(nc.sbuf_tensor(P_ + "kvstg", [128, 8192], BF16))

    def emit_wrapped(g):
        if env is not None:
            for eng in (PE, VE, SE, SY, GE):
                for (sh, sv) in env.get("entry_deps", []):
                    eng.wait_ge(sh, sv)
        emit(g)
        sub_sb.close()
        if env is not None:
            env["entry_deps"] = [(s_v.h, s_v.n), (s_s.h, s_s.n), (s_pe.h, s_pe.n),
                                 (s_st.h, s_st.n), (s_ld.h, s_ld.n), (s_g.h, s_g.n)]

    if env is None:
        with nc.Block() as block:
            @block.gpsimd
            def _(g):
                emit_wrapped(g)
        return nc
    return emit_wrapped


def hostprep_C(inputs, core, x2, ep):
    b, half = core // 2, core % 2
    im = {
        "xin": (np.zeros((V, TOK), np.float32) if x2 is None else
                np.ascontiguousarray(_f32(x2[b, half * TOK:(half + 1) * TOK, :]).T)),
        "WqT": _f32(inputs["L1_attn_Wq"]),
        "WkT": _f32(inputs["L1_attn_Wk"]),
        "WvT": _f32(inputs["L1_attn_Wv"]),
        "WoT": _f32(inputs["L1_attn_Wo"]),
        "bq_c": colpack(inputs["L1_attn_bq"], 32),
        "bk_c": colpack(inputs["L1_attn_bk"], 32),
        "bvrow": _f32(inputs["L1_attn_bv"]).reshape(1, V),
        "bo_c": colpack(inputs["L1_attn_bo"], 32),
        "gqrow_in": _f32(inputs["L1_attn_gq"]).reshape(1, 128),
        "gkrow_in": _f32(inputs["L1_attn_gk"]).reshape(1, 128),
        "ident": np.eye(128, dtype=np.float32),
        "onesrow_in": np.ones((1, TOK), np.float32),
        "ep2": ep,
    }
    return im


def make_epoch():
    return np.full((128, 512), np.random.randint(1, 2**31 - 1, dtype=np.int64),
                   dtype=np.int32)


def run_stageC(inputs, x2):
    nc = build_stageC()
    ep = make_epoch()
    ims = [hostprep_C(inputs, c, x2, ep) for c in range(NCORES)]
    res = run_bass_kernel_spmd(nc, ims, core_ids=list(range(NCORES)))
    out = np.empty((B, L, V), np.float32)
    for c in range(NCORES):
        b, half = c // 2, c % 2
        out[b, half * TOK:(half + 1) * TOK, :] = res.results[c]["xout"].T
    return out





def build_fused():
    ctx = contextlib.ExitStack()
    nc = bass.Bass("TRN2", target_bir_lowering=False)
    env = {"nc": nc, "ctx": ctx}
    env["xin_full"] = nc.dram_tensor("xin_full", [V, L], F32, kind="ExternalInput").ap()
    env["x1d"] = nc.dram_tensor("x1d", [V, L], F32).ap()
    env["x2d"] = nc.dram_tensor("x2d", [V, TOK], F32).ap()
    env["x3d"] = nc.dram_tensor("x3d", [V, TOK], F32).ap()
    psz = lambda name, shape: ctx.enter_context(nc.psum_tensor(name, shape, F32))
    env["pb"] = [psz(f"pb{i}", [128, 512]) for i in range(4)]
    env["pbc1"] = psz("pbc1", [128, TOK])
    env["pst1"] = psz("pst1", [1, 512])
    env["pst2"] = psz("pst2", [1, 512])
    env["entry_deps"] = []

    with nc.Block() as block:
        @block.gpsimd
        def _(g):
            build_stageA(env=env)(g)
            build_stageB(env=env)(g)
            build_stageC(env=env)(g)
            build_stageD(env=env)(g)

    return nc


def hostprep_fused(inputs, core, ep):
    im = {}
    a = hostprep_A(inputs, core)
    xin = a.pop("xin")
    im["xin_full"] = xin
    for k, v in a.items():
        im["A_" + k] = v
    b = hostprep_B(inputs, core, None)
    b.pop("xin")
    for k, v in b.items():
        im["B_" + k] = v
    c = hostprep_C(inputs, core, None, ep)
    c.pop("xin")
    for k, v in c.items():
        im["C_" + k] = v
    d = hostprep_D(inputs, core, None)
    d.pop("xin")
    for k, v in d.items():
        im["D_" + k] = v
    return im


def run_fused(inputs):
    nc = build_fused()
    ep = make_epoch()
    ims = [hostprep_fused(inputs, c, ep) for c in range(NCORES)]
    res = run_bass_kernel_spmd(nc, ims, core_ids=list(range(NCORES)))
    out = np.empty((B, L, V), np.float32)
    for c in range(NCORES):
        b, half = c // 2, c % 2
        out[b, half * TOK:(half + 1) * TOK, :] = res.results[c]["D_xout"].T
    return out


def kernel(**inputs):
    return run_fused(inputs)


